# version 03
# baseline (speedup 1.0000x reference)
import contextlib

import numpy as np

import concourse.bass as bass
import concourse.mybir as mybir
from concourse.bass_utils import run_bass_kernel_spmd

F32 = mybir.dt.float32
BF16 = mybir.dt.bfloat16
AF = mybir.ActivationFunctionType
ALU = mybir.AluOpType
AX = mybir.AxisListType

NCORES = 8
BPC = 4
T = 2048
D = 1024
NTOK = BPC * T
EPS_LN = 1e-5
EPS_RMS = 1.1920929e-07



def _cc(a):
    return np.ascontiguousarray(a)


def prep_shared(inputs):
    w = {}
    wm = np.asarray(inputs["L0_attn_W_m"], np.float32)
    wm_perm = wm.reshape(256, 4, 1024).transpose(1, 0, 2).reshape(1024, 1024)
    w["wm_l"] = _cc(wm_perm.T)
    v = np.asarray(inputs["L0_attn_v_vec"], np.float32)[:, :, 0]
    w["vcols"] = _cc(v.reshape(2, 128, 2).transpose(0, 2, 1).reshape(4, 128))
    wp = np.asarray(inputs["L0_attn_W_proj"], np.float32)
    w["wproj_l"] = _cc(wp.T.reshape(256, 2, 1024).transpose(1, 0, 2).reshape(512, 1024))
    w["oscale"] = _cc(np.asarray(inputs["L0_attn_out_scale"], np.float32).reshape(8, 128))
    w["shup"] = _cc(np.asarray(inputs["L0_ffn_shared_up"], np.float32)[0])
    w["shdn"] = _cc(np.asarray(inputs["L0_ffn_shared_down"], np.float32)[0])
    w["rup"] = _cc(np.asarray(inputs["L0_ffn_routed_up"], np.float32))
    w["rdn"] = _cc(np.asarray(inputs["L0_ffn_routed_down"], np.float32))
    w["router"] = _cc(np.asarray(inputs["L0_ffn_router_w"], np.float32))
    w["wqkv_l"] = _cc(np.asarray(inputs["L1_attn_w_qkv"], np.float32).T)
    w["wout_l"] = _cc(np.asarray(inputs["L1_attn_w_out"], np.float32).T)
    w["bout"] = _cc(np.asarray(inputs["L1_attn_b_out"], np.float32).reshape(8, 128))
    w["esel_l"] = _cc(np.asarray(inputs["L1_ffn_expert_sel"], np.float32).T)
    w["keysw"] = _cc(np.asarray(inputs["L1_ffn_keys"], np.float32))
    w["valsw"] = _cc(np.asarray(inputs["L1_ffn_values"], np.float32))
    hd = 64
    inv = (1.0 / (np.float32(10000.0) ** (np.arange(0, hd, 2, dtype=np.float32)
                                          / np.float32(hd)))).astype(np.float32)
    tt = np.arange(T, dtype=np.float32)
    fr = tt[:, None] * inv[None, :]
    w["ropec"] = _cc(np.cos(fr).astype(np.float32).reshape(16, 128, 32)
                     .transpose(1, 0, 2).reshape(128, 512))
    w["ropes"] = _cc(np.sin(fr).astype(np.float32).reshape(16, 128, 32)
                     .transpose(1, 0, 2).reshape(128, 512))
    return w


def prep_core_x(x, core):
    xs = np.asarray(x, np.float32)[core * BPC:(core + 1) * BPC]
    return _cc(xs.transpose(0, 2, 1).reshape(BPC * D, T))


DRAM_SPEC = {
    "xin":   ([BPC * D, T], F32),
    "wm_l":  ([1024, 1024], F32),
    "vcols": ([4, 128], F32),
    "wproj_l": ([512, 1024], F32),
    "oscale": ([8, 128], F32),
    "shup":  ([1024, 4096], F32),
    "shdn":  ([4096, 1024], F32),
    "rup":   ([8, 1024, 4096], F32),
    "rdn":   ([8, 4096, 1024], F32),
    "router": ([1024, 8], F32),
    "wqkv_l": ([1024, 3072], F32),
    "wout_l": ([1024, 1024], F32),
    "bout":  ([8, 128], F32),
    "esel_l": ([1024, 64], F32),
    "keysw": ([64, 1024, 128], F32),
    "valsw": ([64, 128, 1024], F32),
    "ropec": ([128, 512], F32),
    "ropes": ([128, 512], F32),
    "XN32":  ([BPC * D, T], F32),
    "XNBF":  ([BPC * D, T], BF16),
    "ATT":   ([BPC * D, T], F32),
    "ACC":   ([NTOK, D], F32),
    "X1":    ([BPC * D, T], F32),
    "GATES0": ([NTOK, 8], F32),
    "GATES1": ([NTOK, 64], F32),
    "UPbf":  ([9, 1024, 4096], BF16),
    "DOWNbf": ([9, 4096, 1024], BF16),
    "WQKVbf": ([1024, 3072], BF16),
    "WOUTbf": ([1024, 1024], BF16),
    "KEYSbf": ([64, 1024, 128], BF16),
    "VALbf": ([64, 128, 1024], BF16),
    "XN132": ([BPC * D, T], F32),
    "XN1BF": ([BPC * D, T], BF16),
    "ATT1":  ([BPC * D, T], F32),
    "out":   ([BPC * D, T], F32),
}

PRECAST_FFN0 = ([("shup", None, "UPbf", 0)] +
                [("rup", e, "UPbf", e + 1) for e in range(8)] +
                [("shdn", None, "DOWNbf", 0)] +
                [("rdn", e, "DOWNbf", e + 1) for e in range(8)])
PRECAST_ATTN1 = [("wqkv_l", None, "WQKVbf", None), ("wout_l", None, "WOUTbf", None)]
PRECAST_FFN1 = [("keysw", None, "KEYSbf", None), ("valsw", None, "VALbf", None)]


class Prog:
    def __init__(self, ins, outs, scratch):
        self.nc = bass.Bass("TRN2", target_bir_lowering=False, debug=False,
                            detect_race_conditions=False)
        self.ctx = contextlib.ExitStack()
        self.t = {}
        for name in ins:
            shape, dt = DRAM_SPEC[name]
            self.t[name] = self.nc.dram_tensor(name, shape, dt, kind="ExternalInput").ap()
        for name in outs:
            shape, dt = DRAM_SPEC[name]
            self.t[name] = self.nc.dram_tensor(name, shape, dt, kind="ExternalOutput").ap()
        for name in scratch:
            shape, dt = DRAM_SPEC[name]
            self.t[name] = self.nc.dram_tensor(name, shape, dt, kind="Internal").ap()
        c, nc = self.ctx, self.nc
        self.A1 = c.enter_context(nc.sbuf_tensor("A1", [128, 32768], BF16))
        self.A2 = c.enter_context(nc.sbuf_tensor("A2", [128, 32768], BF16))
        self.A3 = c.enter_context(nc.sbuf_tensor("A3", [128, 20480], BF16))
        self.GS = c.enter_context(nc.sbuf_tensor("GS", [128, 4096], BF16))
        self.ident_b = c.enter_context(nc.sbuf_tensor("ident_b", [128, 128], BF16))
        self.ident_f = c.enter_context(nc.sbuf_tensor("ident_f", [128, 128], F32))
        self.onesc = c.enter_context(nc.sbuf_tensor("onesc", [128, 1], F32))
        self.onesr = c.enter_context(nc.sbuf_tensor("onesr", [1, 128], F32))
        self.rows = c.enter_context(nc.sbuf_tensor("rows", [1, 3072], F32))
        self.cols = c.enter_context(nc.sbuf_tensor("cols", [128, 16], F32))
        self.oscale_c = c.enter_context(nc.sbuf_tensor("oscale_c", [128, 8], F32))
        self.bout_c = c.enter_context(nc.sbuf_tensor("bout_c", [128, 8], F32))
        self.vcols_c = c.enter_context(nc.sbuf_tensor("vcols_c", [128, 4], F32))
        self.wcar = c.enter_context(nc.sbuf_tensor("wcar", [128, 96], F32))
        self.P4 = c.enter_context(nc.psum_tensor("P4", [128, 2048], F32))
        self.PB4 = c.enter_context(nc.psum_tensor("PB4", [128, 512], F32))
        self.PB5 = c.enter_context(nc.psum_tensor("PB5", [128, 512], F32))
        self.PB6 = c.enter_context(nc.psum_tensor("PB6", [128, 512], F32))
        self.PB7 = c.enter_context(nc.psum_tensor("PB7", [128, 512], F32))
        self.sems = {}
        self.bar = self.sem("bar")
        self.phases = []

    def sem(self, name):
        h = self.ctx.enter_context(self.nc.semaphore(name))
        self.sems[name] = h
        return h

    def add(self, emitfn):
        self.phases.append(emitfn)

    def finalize(self):
        nc, phases, bar = self.nc, self.phases, self.bar
        with nc.Block() as block:
            def mk(en):
                def body(eng):
                    for k, ph in enumerate(phases):
                        if k > 0:
                            eng.wait_ge(bar, 5 * k)
                        ph(en, eng)
                        eng.drain()
                        eng.sem_inc(bar, 1)
                return body
            block.sync(mk("sync"))
            block.tensor(mk("tensor"))
            block.vector(mk("vector"))
            block.scalar(mk("scalar"))
            block.gpsimd(mk("gpsimd"))
        self.ctx.close()
        return self.nc

    def a_f32(self, arena, start_f32, n_f32):
        return arena[:].bitcast(F32)[:, start_f32:start_f32 + n_f32]

    def a_bf(self, arena, start_bf, n_bf):
        return arena[:, start_bf:start_bf + n_bf]

    def row(self, i):
        return self.rows[0:1, i * 512:(i + 1) * 512]


class Ring:

    def __init__(self, P, name, n):
        self.n = n
        self.ld = [P.sem(f"{name}_ld{i}") for i in range(n)]
        self.fr = [P.sem(f"{name}_fr{i}") for i in range(n)]

    def produce_guard(self, eng, i):
        if i >= self.n:
            eng.wait_ge(self.fr[i % self.n], (i - self.n) // self.n + 1)

    def ld_sem(self, i):
        return self.ld[i % self.n]

    def consume_wait(self, eng, i):
        eng.wait_ge(self.ld[i % self.n], 16 * (i // self.n + 1))

    def free(self, eng, i):
        eng.sem_inc(self.fr[i % self.n], 1)

    def free_on(self, instr, i):
        instr.then_inc(self.fr[i % self.n], 1)


class StRing:

    def __init__(self, P, name, n):
        self.n = n
        self.st = [P.sem(f"{name}_st{i}") for i in range(n)]
        self.rd = P.sem(f"{name}_rd")

    def produce_guard(self, eng, i):
        if i >= self.n:
            eng.wait_ge(self.st[i % self.n], 16 * ((i - self.n) // self.n + 1))

    def filled(self, eng, i):
        eng.sem_inc(self.rd, 1)

    def filled_on(self, instr, i):
        instr.then_inc(self.rd, 1)

    def store_wait(self, eng, i):
        eng.wait_ge(self.rd, i + 1)

    def st_sem(self, i):
        return self.st[i % self.n]

    def wait_all(self, eng, total):
        for j in range(self.n):
            cnt = (total - j + self.n - 1) // self.n if total > j else 0
            if cnt:
                eng.wait_ge(self.st[j], 16 * cnt)


def fmaj(dram_ap, b, c0, w):
    return dram_ap[b * D:(b + 1) * D, c0:c0 + w].rearrange("(dc p) u -> p dc u", p=128)


def v3(ap2d, u):
    return ap2d.rearrange("p (dc u) -> p dc u", u=u)



def ph_init(P):
    t = P.t
    sld = P.sem("ini_ld")
    sgi = P.sem("ini_g")

    def emit(en, eng):
        if en == "gpsimd":
            eng.memset(P.ident_b[:], 0.0)
            eng.memset(P.ident_f[:], 0.0).then_inc(sgi, 1)
            eng.wait_ge(sgi, 1)
            for idn in (P.ident_b, P.ident_f):
                eng.affine_select(out=idn[:], in_=idn[:], compare_op=ALU.not_equal,
                                  fill=1.0, base=0, pattern=[[-1, 128]], channel_multiplier=1)
        if en == "vector":
            eng.memset(P.onesc[:], 1.0)
            eng.memset(P.onesr[:], 1.0)
        if en == "sync":
            n = 0
            for name, dst, k in (("oscale", P.oscale_c, 8), ("bout", P.bout_c, 8),
                                 ("vcols", P.vcols_c, 4)):
                if name in t:
                    for c in range(k):
                        eng.dma_start(out=dst[:, c:c + 1],
                                      in_=t[name][c, :].unsqueeze(1)).then_inc(sld, 16)
                    n += k
            eng.wait_ge(sld, 16 * n)
    return emit



def ph_precast(P, jobs, tag):
    t = P.t
    CH = 1048576
    plan = []
    for (src, si, d, di) in jobs:
        shape = DRAM_SPEC[src][0]
        n = int(np.prod(shape[1:]) if si is not None else np.prod(shape))
        assert n % CH == 0, (src, n)
        for c in range(n // CH):
            plan.append((src, si, d, di, c))
    rin = Ring(P, f"pc{tag}i", 2)
    rout = StRing(P, f"pc{tag}o", 2)
    scv = P.sem(f"pc_cv_{tag}")

    def flat_view(name, idx, c):
        ap = t[name]
        if idx is not None:
            ap = ap[idx]
        if len(ap.shape) == 2:
            ap = ap.rearrange("a b -> (a b)")
        elif len(ap.shape) == 3:
            ap = ap.rearrange("a b c -> (a b c)")
        return ap.rearrange("(c p m) -> c p m", p=128, m=8192)[c]

    def emit(en, eng):
        nch = len(plan)
        for i, (src, si, d, di, c) in enumerate(plan):
            slot = i % 2
            stg = P.a_f32(P.A1, slot * 8192, 8192)
            obf = P.a_bf(P.A2, slot * 8192, 8192)
            if en == "sync":
                if i >= 2:
                    eng.wait_ge(rout.rd, i - 1)
                eng.dma_start(out=stg, in_=flat_view(src, si, c)).then_inc(rin.ld_sem(i), 16)
            if en == "vector":
                rin.consume_wait(eng, i)
                rout.produce_guard(eng, i)
                rout.filled_on(eng.tensor_copy(obf, stg), i)
            if en == "gpsimd":
                rout.store_wait(eng, i)
                eng.dma_start(out=flat_view(d, di, c), in_=obf).then_inc(rout.st_sem(i), 16)
        if en == "gpsimd":
            rout.wait_all(eng, nch)
        _ = scv
    return emit



def emit_ln_rows(P, en, eng, s_rv, s_rs, rvb, rsb, rms2):
    r_mean, r_msq, r_t2, r_var, r_A, r_B = (P.row(i) for i in range(6))
    r_A2, r_B2 = r_mean, r_var
    r_tmp, r_ve = r_t2, r_msq
    if en == "vector":
        eng.tensor_single_scalar(r_mean, P.PB4[0:1, :], 1.0 / 1024.0, ALU.mult
                                 ).then_inc(s_rv, 1)
        eng.tensor_single_scalar(r_msq, P.PB5[0:1, :], 1.0 / 1024.0, ALU.mult)
        eng.wait_ge(s_rs, rsb + 1)
        eng.tensor_tensor(r_var, r_msq, r_tmp, ALU.subtract)
        eng.tensor_single_scalar(r_ve, r_var, EPS_LN, ALU.add).then_inc(s_rv, 1)
        eng.wait_ge(s_rs, rsb + 2)
        eng.wait_ge(s_rv, rvb + 1)
        eng.tensor_tensor(r_B, r_mean, r_A, ALU.mult)
        if rms2:
            eng.tensor_tensor(r_t2, r_A, r_A, ALU.mult)
            eng.tensor_tensor(r_t2, r_var, r_t2, ALU.mult)
            eng.tensor_single_scalar(r_t2, r_t2, EPS_RMS, ALU.add).then_inc(s_rv, 1)
            eng.wait_ge(s_rs, rsb + 3)
            eng.tensor_tensor(r_A2, r_A, r_t2, ALU.mult)
            eng.tensor_tensor(r_B2, r_B, r_t2, ALU.mult).then_inc(s_rv, 1)
        else:
            eng.tensor_copy(P.row(0), r_A).then_inc(s_rv, 1)
    if en == "scalar":
        eng.wait_ge(s_rv, rvb + 1)
        eng.square(r_tmp, r_mean).then_inc(s_rs, 1)
        eng.wait_ge(s_rv, rvb + 2)
        eng.activation(r_A, r_ve, AF.Ln)
        eng.activation(r_A, r_A, AF.Exp, scale=-0.5).then_inc(s_rs, 1)
        if rms2:
            eng.wait_ge(s_rv, rvb + 3)
            eng.activation(r_t2, r_t2, AF.Ln)
            eng.activation(r_t2, r_t2, AF.Exp, scale=-0.5).then_inc(s_rs, 1)
    return (4, 3) if rms2 else (3, 2)



def ph_l0a(P, xn32="XN32", xnbf="XNBF", att="ATT"):
    t = P.t
    s_wld = P.sem("la_wld")
    s_wcv = P.sem("la_wcv")
    s_xld = P.sem("la_xld")
    s_sq = P.sem("la_sq")
    s_stt = P.sem("la_stt")
    s_rv = P.sem("la_rv")
    s_rs = P.sem("la_rs")
    s_bc = P.sem("la_bc")
    s_xn = P.sem("la_xn")
    s_mm = P.sem("la_mm")
    s_me = P.sem("la_me")
    s_xst = P.sem("la_xst")
    s_nv = P.sem("la_nv")
    s_ns = P.sem("la_ns")
    s_hb = P.sem("la_hb")
    s_wp = P.sem("la_wp")
    s_wc2 = P.sem("la_wc2")
    s_op = P.sem("la_op")
    s_oe = P.sem("la_oe")
    r_att = None
    NRV, NRS = 4, 3
    r_att = StRing(P, "la_att", 4)

    def emit(en, eng):
        SA = [P.a_f32(P.A1, i * 2048, 2048) for i in range(8)]
        SB = [P.a_f32(P.A2, i * 2048, 2048) for i in range(8)]
        SA3 = [x.rearrange("p (c u) -> p c u", u=128) for x in SA]
        SB3 = [x.rearrange("p (c u) -> p c u", u=128) for x in SB]
        WM = P.a_bf(P.A3, 0, 8192)
        HBF = [P.a_bf(P.A3, 8192 + i * 2048, 2048) for i in range(4)]
        HBF3 = [x.rearrange("p (c u) -> p c u", u=128) for x in HBF]
        WPJ = P.a_bf(P.A3, 16384, 4096)
        TMP = P.a_f32(P.GS, 0, 2048)
        TMP3 = TMP.rearrange("p (c u) -> p c u", u=128)
        XCH = P.a_f32(P.A2, 0, 4096)
        XSQ = P.a_f32(P.A2, 4096, 4096)
        XNF = P.a_f32(P.A2, 8192, 4096)
        XNB = P.a_bf(P.A2, 24576, 4096)
        XN2B = P.a_bf(P.A2, 28672, 4096)
        WSTG = [P.a_f32(P.A2, 0, 4096), P.a_f32(P.A2, 4096, 4096)]
        ATS = [P.a_f32(P.A2, i * 512, 512) for i in range(4)]
        XCH3 = XCH.rearrange("p (dc u) -> p dc u", u=512)
        XSQ3 = XSQ.rearrange("p (dc u) -> p dc u", u=512)
        XNF3 = XNF.rearrange("p (dc u) -> p dc u", u=512)
        XNB3 = XNB.rearrange("p (dc u) -> p dc u", u=512)
        XN2B3 = XN2B.rearrange("p (dc u) -> p dc u", u=512)
        WPJ3 = WPJ.rearrange("p (ic f) -> p ic f", f=1024)

        if en == "sync":
            for half in range(2):
                eng.dma_start(out=v3(WSTG[half], 1024),
                              in_=t["wm_l"][half * 512:(half + 1) * 512, :]
                              .rearrange("(dc p) f -> p dc f", p=128)).then_inc(s_wld, 16)
        if en == "vector":
            eng.wait_ge(s_wld, 32)
            for half in range(2):
                eng.tensor_copy(P.a_bf(P.A3, half * 4096, 4096), WSTG[half]).then_inc(s_wcv, 1)

        for b in range(BPC):
            for tc in range(4):
                ci = b * 4 + tc
                c0 = tc * 512
                rvb, rsb = NRV * ci, NRS * ci
                if en == "sync":
                    if ci == 0:
                        eng.wait_ge(s_wcv, 2)
                    else:
                        eng.wait_ge(s_xn, ci)
                    if tc == 0 and b > 0:
                        r_att.wait_all(eng, 32 * b)
                    eng.dma_start(out=XCH3, in_=fmaj(t["xin"], b, c0, 512)
                                  ).then_inc(s_xld, 16)
                if en == "scalar":
                    eng.wait_ge(s_xld, 16 * (ci + 1))
                    eng.activation(XSQ, XCH, AF.Square).then_inc(s_sq, 1)
                    emit_ln_rows(P, en, eng, s_rv, s_rs, rvb, rsb, rms2=True)
                if en == "tensor":
                    eng.wait_ge(s_xld, 16 * (ci + 1))
                    if ci > 0:
                        eng.wait_ge(s_xn, ci)
                    if b > 0 and tc == 0:
                        eng.wait_ge(r_att.rd, 32 * b)
                    for dc in range(8):
                        eng.matmul(P.PB4[0:1, :], P.onesc[:], XCH3[:, dc, :],
                                   start=(dc == 0), stop=(dc == 7))
                    eng.wait_ge(s_sq, ci + 1)
                    for dc in range(8):
                        ti = eng.matmul(P.PB5[0:1, :], P.onesc[:], XSQ3[:, dc, :],
                                        start=(dc == 0), stop=(dc == 7))
                    ti.then_inc(s_stt, 1)
                    eng.wait_ge(s_rv, rvb + NRV)
                    eng.wait_ge(s_rs, rsb + NRS)
                    eng.matmul(P.PB4[:], P.onesr[:], P.row(4), start=True, stop=True)
                    eng.matmul(P.PB5[:], P.onesr[:], P.row(5), start=True, stop=True)
                    eng.matmul(P.PB6[:], P.onesr[:], P.row(0), start=True, stop=True)
                    ti = eng.matmul(P.PB7[:], P.onesr[:], P.row(3), start=True, stop=True)
                    ti.then_inc(s_bc, 1)
                    eng.wait_ge(s_xn, ci + 1)
                    for fc in range(8):
                        g = ci * 8 + fc
                        ps = P.P4[:, (fc % 4) * 512:(fc % 4) * 512 + 512]
                        if g >= 4:
                            eng.wait_ge(s_me, g - 3)
                        for dc in range(8):
                            ti = eng.matmul(
                                ps, WM[:, dc * 1024 + fc * 128:dc * 1024 + fc * 128 + 128],
                                XN2B3[:, dc, :], start=(dc == 0), stop=(dc == 7))
                        ti.then_inc(s_mm, 1)
                if en == "vector":
                    eng.wait_ge(s_stt, ci + 1)
                    emit_ln_rows(P, en, eng, s_rv, s_rs, rvb, rsb, rms2=True)
                    eng.wait_ge(s_bc, ci + 1)
                    if ci > 0:
                        eng.wait_ge(s_xst, 32 * ci)
                    for dc in range(8):
                        eng.tensor_tensor(XNF3[:, dc, :], XCH3[:, dc, :], P.PB4[:], ALU.mult)
                        eng.tensor_tensor(XNF3[:, dc, :], XNF3[:, dc, :], P.PB5[:],
                                          ALU.subtract)
                        eng.tensor_copy(XNB3[:, dc, :], XNF3[:, dc, :])
                        eng.tensor_tensor(XSQ3[:, dc, :], XCH3[:, dc, :], P.PB6[:], ALU.mult)
                        ti = eng.tensor_tensor(XN2B3[:, dc, :], XSQ3[:, dc, :], P.PB7[:],
                                               ALU.subtract)
                    ti.then_inc(s_xn, 1)
                    for fc in range(8):
                        g = ci * 8 + fc
                        eng.wait_ge(s_mm, g + 1)
                        ps = P.P4[:, (fc % 4) * 512:(fc % 4) * 512 + 512]
                        eng.tensor_copy(SA[fc][:, c0:c0 + 512], ps).then_inc(s_me, 1)
                if en == "gpsimd":
                    eng.wait_ge(s_xn, ci + 1)
                    eng.dma_start(out=fmaj(t[xn32], b, c0, 512), in_=XNF3).then_inc(s_xst, 16)
                    eng.dma_start(out=fmaj(t[xnbf], b, c0, 512), in_=XNB3).then_inc(s_xst, 16)

            if en == "vector":
                eng.wait_ge(s_me, (b * 4 + 4) * 8)
                for kh in range(2):
                    U = SB[7]
                    eng.tensor_tensor(TMP, SA[kh], SA[kh], ALU.mult)
                    for c in range(1, 4):
                        eng.tensor_tensor(U, SA[2 * c + kh], SA[2 * c + kh], ALU.mult)
                        eng.tensor_tensor(TMP, TMP, U, ALU.add)
                    eng.tensor_single_scalar(TMP, TMP, 1e-12, ALU.max).then_inc(s_nv, 1)
                    eng.wait_ge(s_ns, b * 2 + kh + 1)
                    for c in range(4):
                        eng.tensor_tensor(SA[2 * c + kh], SA[2 * c + kh], TMP, ALU.mult)
            if en == "scalar":
                for kh in range(2):
                    eng.wait_ge(s_nv, b * 2 + kh + 1)
                    eng.activation(TMP, TMP, AF.Ln)
                    eng.activation(TMP, TMP, AF.Exp, scale=-0.5).then_inc(s_ns, 1)

            if en == "vector":
                cur3, nxt3 = SA3, SB3
                for step in range(7):
                    s = 1 << step
                    for kh in range(2):
                        for i in range(2):
                            for j in range(2):
                                ot = nxt3[2 * (i * 2 + j) + kh]
                                a0 = cur3[2 * (i * 2 + 0) + kh]
                                a1 = cur3[2 * (i * 2 + 1) + kh]
                                b0 = cur3[2 * (0 * 2 + j) + kh]
                                b1 = cur3[2 * (1 * 2 + j) + kh]
                                eng.tensor_tensor(ot[:, :, s:], a0[:, :, s:],
                                                  b0[:, :, :128 - s], ALU.mult)
                                eng.tensor_tensor(TMP3[:, :, s:], a1[:, :, s:],
                                                  b1[:, :, :128 - s], ALU.mult)
                                eng.tensor_tensor(ot[:, :, s:], ot[:, :, s:],
                                                  TMP3[:, :, s:], ALU.add)
                        for cc in range(4):
                            ti = 2 * cc + kh
                            eng.tensor_copy(nxt3[ti][:, :, :s], cur3[ti][:, :, :s])
                    cur3, nxt3 = nxt3, cur3
                SL = cur3
                eng.drain()
                for kh in range(2):
                    for j in range(2):
                        eng.tensor_copy(P.wcar[:, kh * 2 + j:kh * 2 + j + 1],
                                        P.vcols_c[:, kh * 2 + j:kh * 2 + j + 1])
                eng.drain()
                for c in range(16):
                    for kh in range(2):
                        w0 = P.wcar[:, kh * 2 + 0:kh * 2 + 1]
                        w1 = P.wcar[:, kh * 2 + 1:kh * 2 + 2]
                        sv0 = P.wcar[:, 4 + c * 4 + kh * 2:4 + c * 4 + kh * 2 + 1]
                        sv1 = P.wcar[:, 5 + c * 4 + kh * 2:5 + c * 4 + kh * 2 + 1]
                        eng.tensor_copy(sv0, w0)
                        eng.tensor_copy(sv1, w1)
                        if c < 15:
                            tt0 = P.wcar[:, 68:69]
                            tt1 = P.wcar[:, 69:70]
                            tt2 = P.wcar[:, 70:71]
                            tt3 = P.wcar[:, 71:72]
                            Pm = [SL[2 * cc + kh][:, c, 127:128] for cc in range(4)]
                            eng.drain()
                            eng.tensor_tensor(tt0, Pm[0], sv0, ALU.mult)
                            eng.tensor_tensor(tt1, Pm[1], sv1, ALU.mult)
                            eng.tensor_tensor(tt2, Pm[2], sv0, ALU.mult)
                            eng.tensor_tensor(tt3, Pm[3], sv1, ALU.mult)
                            eng.drain()
                            eng.tensor_tensor(w0, tt0, tt1, ALU.add)
                            eng.tensor_tensor(w1, tt2, tt3, ALU.add)
                            eng.drain()
                eng.drain()
                for c in range(16):
                    for kh in range(2):
                        sv0 = P.wcar[:, 4 + c * 4 + kh * 2:4 + c * 4 + kh * 2 + 1]
                        sv1 = P.wcar[:, 5 + c * 4 + kh * 2:5 + c * 4 + kh * 2 + 1]
                        for i in range(2):
                            tmpd = TMP3[:, c, :]
                            eng.tensor_single_scalar(
                                tmpd, SL[2 * (i * 2 + 0) + kh][:, c, :], sv0, ALU.mult)
                            eng.drain()
                            ti = eng.scalar_tensor_tensor(
                                HBF3[i * 2 + kh][:, c, :], SL[2 * (i * 2 + 1) + kh][:, c, :],
                                sv1, tmpd, ALU.mult, ALU.add)
                ti.then_inc(s_hb, 1)

            if en == "sync":
                eng.wait_ge(s_hb, b + 1)
                eng.dma_start(out=v3(WSTG[0], 1024),
                              in_=t["wproj_l"].rearrange("(ic p) f -> p ic f", p=128)
                              ).then_inc(s_wp, 16)
            if en == "vector":
                eng.wait_ge(s_wp, 16 * (b + 1))
                eng.tensor_copy(WPJ, WSTG[0]).then_inc(s_wc2, 1)
            if en == "tensor":
                eng.wait_ge(s_wc2, b + 1)
                for oj in range(32):
                    ec, tc2 = oj // 4, oj % 4
                    oi = b * 32 + oj
                    ps = P.P4[:, (oj % 4) * 512:(oj % 4) * 512 + 512]
                    if oi >= 4:
                        eng.wait_ge(r_att.rd, oi - 3)
                    for ic in range(4):
                        ti = eng.matmul(ps, WPJ3[:, ic, ec * 128:ec * 128 + 128],
                                        HBF[ic][:, tc2 * 512:tc2 * 512 + 512],
                                        start=(ic == 0), stop=(ic == 3))
                    ti.then_inc(s_op, 1)
            if en == "vector":
                for oj in range(32):
                    ec, tc2 = oj // 4, oj % 4
                    oi = b * 32 + oj
                    eng.wait_ge(s_op, oi + 1)
                    r_att.produce_guard(eng, oi)
                    ps = P.P4[:, (oj % 4) * 512:(oj % 4) * 512 + 512]
                    r_att.filled_on(
                        eng.tensor_single_scalar(ATS[oi % 4], ps, P.oscale_c[:, ec:ec + 1],
                                                 ALU.mult), oi)
            if en == "gpsimd":
                for oj in range(32):
                    ec, tc2 = oj // 4, oj % 4
                    oi = b * 32 + oj
                    r_att.store_wait(eng, oi)
                    eng.dma_start(out=t[att][b * D + ec * 128:b * D + ec * 128 + 128,
                                             tc2 * 512:tc2 * 512 + 512],
                                  in_=ATS[oi % 4]).then_inc(r_att.st_sem(oi), 16)
        if en == "gpsimd":
            r_att.wait_all(eng, 128)
            eng.wait_ge(s_xst, 32 * 16)
    return emit



def ph_router0(P, xn32="XN32", gates="GATES0"):
    t = P.t
    rx = Ring(P, "r0x", 2)
    rg = StRing(P, "r0g", 2)
    s_wl = P.sem("r0_wl")
    s_mm = P.sem("r0_mm")
    s_mv = P.sem("r0_mv")
    s_ex = P.sem("r0_ex")

    def emit(en, eng):
        RTW = P.a_f32(P.GS, 0, 64).rearrange("p (dc e) -> p dc e", e=8)
        XT = [P.a_f32(P.A3, i * 1024, 1024).rearrange("p (dc u) -> p dc u", u=128)
              for i in range(2)]
        P8 = P.a_f32(P.A3, 2048, 8)
        E8 = P.a_f32(P.A3, 2056, 8)
        GT = [P.a_f32(P.A3, 2064 + i * 8, 8) for i in range(2)]
        nmax = P.cols[:, 0:1]
        msum = P.cols[:, 1:2]
        rs = P.cols[:, 2:3]
        m1 = P.cols[:, 3:4]
        if en == "sync":
            eng.dma_start(out=RTW, in_=t["router"].rearrange("(dc p) e -> p dc e", p=128)
                          ).then_inc(s_wl, 16)
            for i in range(64):
                b, tt = i // 16, i % 16
                if i >= 2:
                    eng.wait_ge(s_ex, i - 1)
                eng.dma_start(out=XT[i % 2],
                              in_=t[xn32][b * D:(b + 1) * D, tt * 128:tt * 128 + 128]
                              .rearrange("(dc p) u -> p dc u", p=128)
                              ).then_inc(rx.ld_sem(i), 16)
        if en == "tensor":
            eng.wait_ge(s_wl, 16)
            for i in range(64):
                rx.consume_wait(eng, i)
                if i > 0:
                    eng.wait_ge(s_ex, i)
                for dc in range(8):
                    ti = eng.matmul(P.PB4[:, 0:8], XT[i % 2][:, dc, :], RTW[:, dc, :],
                                    start=(dc == 0), stop=(dc == 7))
                ti.then_inc(s_mm, 1)
        if en == "vector":
            for i in range(64):
                eng.wait_ge(s_mm, i + 1)
                eng.reduce_max(nmax, P.PB4[:, 0:8], axis=AX.X, negate=True
                               ).then_inc(s_mv, 1)
                eng.wait_ge(s_ex, i + 1)
                eng.drain()
                eng.reciprocal(rs, msum)
                eng.drain()
                eng.tensor_single_scalar(P8, P8, rs, ALU.mult)
                eng.drain()
                eng.reduce_max(m1, P8, axis=AX.X)
                eng.drain()
                eng.tensor_single_scalar(E8, P8, m1, ALU.is_ge)
                eng.drain()
                eng.tensor_single_scalar(E8, E8, -2.0, ALU.mult)
                eng.drain()
                eng.tensor_tensor(E8, P8, E8, ALU.add)
                eng.drain()
                eng.reduce_max(m1, E8, axis=AX.X)
                eng.drain()
                eng.tensor_single_scalar(E8, P8, m1, ALU.is_ge)
                eng.drain()
                rg.produce_guard(eng, i)
                rg.filled_on(eng.tensor_tensor(GT[i % 2], P8, E8, ALU.mult), i)
        if en == "scalar":
            for i in range(64):
                eng.wait_ge(s_mm, i + 1)
                eng.wait_ge(s_mv, i + 1)
                eng.activation(P8, P.PB4[:, 0:8], AF.Exp, bias=nmax,
                               accum_out=msum).then_inc(s_ex, 1)
        if en == "gpsimd":
            for i in range(64):
                b, tt = i // 16, i % 16
                rg.store_wait(eng, i)
                eng.dma_start(out=t[gates][b * T + tt * 128:b * T + tt * 128 + 128, :],
                              in_=GT[i % 2]).then_inc(rg.st_sem(i), 16)
            rg.wait_all(eng, 64)
    return emit


def bankv(P, j):
    if j < 4:
        return P.P4[:, j * 512:(j + 1) * 512]
    return [P.PB4, P.PB5, P.PB6, P.PB7][j - 4][:]



def ph_ffn0(P, xnbf="XNBF", gates="GATES0", acc="ACC"):
    t = P.t
    s_upl = P.sem("f0_upl")
    s_upe = P.sem("f0_upe")
    s_um = P.sem("f0_um")
    s_gel = P.sem("f0_gel")
    s_dk = P.sem("f0_dk")
    s_ev = P.sem("f0_ev")
    rxn = Ring(P, "f0xn", 2)
    rga = Ring(P, "f0ga", 2)
    rac = Ring(P, "f0ac", 2)
    rdn = Ring(P, "f0dn", 2)
    rst = StRing(P, "f0st", 2)

    def emit(en, eng):
        UPW = P.A1[:].rearrange("p (dc f) -> p dc f", f=4096)
        XN = [P.a_bf(P.A2, i * 4096, 4096).rearrange("p (dc u) -> p dc u", u=512)
              for i in range(2)]
        GC = [P.a_f32(P.A2, 4096 + i * 32, 32) for i in range(2)]
        GC3 = [x.rearrange("p (q e) -> p q e", e=8) for x in GC]
        AC = [P.a_f32(P.A2, 4160 + i * 4096, 4096) for i in range(2)]
        AC3 = [x.rearrange("p (q u) -> p q u", u=1024) for x in AC]
        DN = [P.a_bf(P.A2, 24704 + i * 1024, 1024) for i in range(2)]
        GH = P.A3[:, 0:16384].rearrange("p (fc u) -> p fc u", u=512)

        for e in range(9):
            if en == "sync":
                if e > 0:
                    eng.wait_ge(s_upe, e)
                eng.dma_start(out=UPW, in_=t["UPbf"][e].rearrange("(dc p) f -> p dc f",
                                                                  p=128)
                              ).then_inc(s_upl, 16)
            for ci in range(16):
                gci = e * 16 + ci
                b, tc = ci // 4, ci % 4
                trow = b * T + tc * 512
                if en == "sync":
                    if gci >= 2:
                        eng.wait_ge(s_um, 32 * (gci - 1))
                    eng.dma_start(out=XN[gci % 2], in_=fmaj(t[xnbf], b, tc * 512, 512)
                                  ).then_inc(rxn.ld_sem(gci), 16)
                    if e > 0:
                        gj = gci - 16
                        if gj >= 2:
                            eng.wait_ge(s_ev, gci - 1)
                        eng.dma_start(out=GC3[gj % 2],
                                      in_=t[gates][trow:trow + 512, :]
                                      .rearrange("(q p) e -> p q e", p=128)
                                      ).then_inc(rga.ld_sem(gj), 16)
                        eng.wait_ge(rst.st[gci % 2], 16 * 4 * (gci // 2))
                        eng.dma_start(out=AC3[gj % 2],
                                      in_=t[acc][trow:trow + 512, :]
                                      .rearrange("(q p) u -> p q u", p=128)
                                      ).then_inc(rac.ld_sem(gj), 16)
                    for kc in range(32):
                        gdi = gci * 32 + kc
                        if gdi >= 2:
                            eng.wait_ge(s_dk, gdi - 1)
                        eng.dma_start(out=DN[gdi % 2],
                                      in_=t["DOWNbf"][e][kc * 128:kc * 128 + 128, :]
                                      ).then_inc(rdn.ld_sem(gdi), 16)
                if en == "tensor":
                    if ci == 0:
                        eng.wait_ge(s_upl, 16 * (e + 1))
                    rxn.consume_wait(eng, gci)
                    if gci > 0:
                        eng.wait_ge(s_ev, gci)
                    for fc in range(32):
                        g = gci * 32 + fc
                        ps = P.P4[:, (fc % 4) * 512:(fc % 4) * 512 + 512]
                        if fc >= 4:
                            eng.wait_ge(s_gel, g - 3)
                        for dc in range(8):
                            ti = eng.matmul(ps, UPW[:, dc, fc * 128:fc * 128 + 128],
                                            XN[gci % 2][:, dc, :],
                                            start=(dc == 0), stop=(dc == 7))
                        ti.then_inc(s_um, 1)
                    if ci == 15:
                        eng.wait_ge(s_um, 32 * (gci + 1))
                        eng.sem_inc(s_upe, 1)
                    eng.wait_ge(s_gel, 32 * (gci + 1))
                    for kc in range(32):
                        gdi = gci * 32 + kc
                        rdn.consume_wait(eng, gdi)
                        for tt in range(4):
                            for fh in range(2):
                                ti = eng.matmul(bankv(P, tt * 2 + fh),
                                                GH[:, kc, tt * 128:tt * 128 + 128],
                                                DN[gdi % 2][:, fh * 512:fh * 512 + 512],
                                                start=(kc == 0), stop=(kc == 31))
                        ti.then_inc(s_dk, 1)
                if en == "scalar":
                    for fc in range(32):
                        g = gci * 32 + fc
                        eng.wait_ge(s_um, g + 1)
                        ps = P.P4[:, (fc % 4) * 512:(fc % 4) * 512 + 512]
                        eng.activation(GH[:, fc, :], ps, AF.Gelu).then_inc(s_gel, 1)
                if en == "vector":
                    eng.wait_ge(s_dk, 32 * (gci + 1))
                    if e > 0:
                        rga.consume_wait(eng, gci - 16)
                        rac.consume_wait(eng, gci - 16)
                    if gci >= 2:
                        eng.wait_ge(rst.st[gci % 2], 16 * 4 * (gci // 2))
                    for tt in range(4):
                        for fh in range(2):
                            dst = AC3[gci % 2][:, tt, fh * 512:fh * 512 + 512]
                            bk = bankv(P, tt * 2 + fh)
                            if e == 0:
                                ti = eng.tensor_copy(dst, bk)
                            else:
                                gcol = GC[gci % 2][:, tt * 8 + e - 1:tt * 8 + e]
                                ti = eng.scalar_tensor_tensor(dst, bk, gcol, dst,
                                                              ALU.mult, ALU.add)
                    ti.then_inc(s_ev, 1)
                if en == "gpsimd":
                    eng.wait_ge(s_ev, gci + 1)
                    for tt in range(4):
                        ti = eng.dma_start(out=t[acc][trow + tt * 128:trow + tt * 128 + 128, :],
                                           in_=AC3[gci % 2][:, tt, :])
                        ti.then_inc(rst.st[gci % 2], 16)
        if en == "gpsimd":
            for j in range(2):
                eng.wait_ge(rst.st[j], 16 * 4 * 72)
    return emit



def ph_cmb0(P, xin="xin", xn="XN32", att="ATT", acc="ACC",
            x1="X1", xnout="XN32", xnbfout="XNBF"):
    t = P.t
    s_ld = P.sem("c0_ld")
    s_tr = P.sem("c0_tr")
    s_cv = P.sem("c0_cv")
    s_sq = P.sem("c0_sq")
    s_stt = P.sem("c0_stt")
    s_rv = P.sem("c0_rv")
    s_rs = P.sem("c0_rs")
    s_bc = P.sem("c0_bc")
    s_xn = P.sem("c0_xn")
    s_st = P.sem("c0_st")

    def emit(en, eng):
        C_X = P.a_f32(P.A2, 0, 4096)
        C_XN = P.a_f32(P.A2, 4096, 4096)
        C_AT = P.a_f32(P.A2, 8192, 4096)
        C_AC = P.a_f32(P.A1, 0, 4096)
        XSQ = P.a_f32(P.A1, 4096, 4096)
        XNF = P.a_f32(P.A1, 8192, 4096)
        XNB = P.a_bf(P.A1, 24576, 4096)
        C_X3 = v3(C_X, 512)
        C_XN3 = v3(C_XN, 512)
        C_AT3 = v3(C_AT, 512)
        C_AC3 = C_AC.rearrange("p (q u) -> p q u", u=1024)
        XSQ3 = v3(XSQ, 512)
        XNF3 = v3(XNF, 512)
        XNB3 = v3(XNB, 512)
        NRV, NRS = 3, 2
        for ci in range(16):
            b, tc = ci // 4, ci % 4
            c0 = tc * 512
            trow = b * T + c0
            rvb, rsb = NRV * ci, NRS * ci
            if en == "sync":
                if ci > 0:
                    eng.wait_ge(s_xn, ci)
                    eng.wait_ge(s_st, 48 * ci)
                eng.dma_start(out=C_X3, in_=fmaj(t[xin], b, c0, 512)).then_inc(s_ld, 16)
                eng.dma_start(out=C_XN3, in_=fmaj(t[xn], b, c0, 512)).then_inc(s_ld, 16)
                eng.dma_start(out=C_AT3, in_=fmaj(t[att], b, c0, 512)).then_inc(s_ld, 16)
                eng.dma_start(out=C_AC3, in_=t[acc][trow:trow + 512, :]
                              .rearrange("(q p) u -> p q u", p=128)).then_inc(s_ld, 16)
            if en == "tensor":
                eng.wait_ge(s_ld, 64 * (ci + 1))
                if ci > 0:
                    eng.wait_ge(s_xn, ci)
                for dc in range(8):
                    if ci * 8 + dc >= 4:
                        eng.wait_ge(s_cv, ci * 8 + dc - 3)
                    for tt in range(4):
                        ti = eng.transpose(
                            P.P4[:, (dc % 4) * 512 + tt * 128:(dc % 4) * 512 + tt * 128 + 128],
                            C_AC3[:, tt, dc * 128:dc * 128 + 128], P.ident_f[:])
                    ti.then_inc(s_tr, 1)
                eng.wait_ge(s_cv, ci * 8 + 8)
                for dc in range(8):
                    eng.matmul(P.PB4[0:1, :], P.onesc[:], C_X3[:, dc, :],
                               start=(dc == 0), stop=(dc == 7))
                eng.wait_ge(s_sq, ci + 1)
                for dc in range(8):
                    ti = eng.matmul(P.PB5[0:1, :], P.onesc[:], XSQ3[:, dc, :],
                                    start=(dc == 0), stop=(dc == 7))
                ti.then_inc(s_stt, 1)
                eng.wait_ge(s_rv, rvb + NRV)
                eng.wait_ge(s_rs, rsb + NRS)
                eng.matmul(P.PB6[:], P.onesr[:], P.row(0), start=True, stop=True)
                ti = eng.matmul(P.PB7[:], P.onesr[:], P.row(5), start=True, stop=True)
                ti.then_inc(s_bc, 1)
            if en == "vector":
                eng.wait_ge(s_ld, 64 * (ci + 1))
                for dc in range(8):
                    eng.wait_ge(s_tr, ci * 8 + dc + 1)
                    eng.scalar_tensor_tensor(C_X3[:, dc, :], C_XN3[:, dc, :], 2.0,
                                             C_X3[:, dc, :], ALU.mult, ALU.add)
                    eng.tensor_tensor(C_X3[:, dc, :], C_X3[:, dc, :], C_AT3[:, dc, :],
                                      ALU.add)
                    ti = eng.tensor_tensor(C_X3[:, dc, :], C_X3[:, dc, :],
                                           P.P4[:, (dc % 4) * 512:(dc % 4) * 512 + 512],
                                           ALU.add)
                    ti.then_inc(s_cv, 1)
                eng.wait_ge(s_stt, ci + 1)
                emit_ln_rows(P, en, eng, s_rv, s_rs, rvb, rsb, rms2=False)
                eng.wait_ge(s_bc, ci + 1)
                if ci > 0:
                    eng.wait_ge(s_st, 16 * 3 * ci)
                for dc in range(8):
                    eng.tensor_tensor(XNF3[:, dc, :], C_X3[:, dc, :], P.PB6[:], ALU.mult)
                    eng.tensor_tensor(XNF3[:, dc, :], XNF3[:, dc, :], P.PB7[:],
                                      ALU.subtract)
                    ti = eng.tensor_copy(XNB3[:, dc, :], XNF3[:, dc, :])
                ti.then_inc(s_xn, 1)
            if en == "scalar":
                eng.wait_ge(s_cv, ci * 8 + 8)
                eng.activation(XSQ, C_X, AF.Square).then_inc(s_sq, 1)
                emit_ln_rows(P, en, eng, s_rv, s_rs, rvb, rsb, rms2=False)
            if en == "gpsimd":
                eng.wait_ge(s_xn, ci + 1)
                eng.dma_start(out=fmaj(t[x1], b, c0, 512), in_=C_X3).then_inc(s_st, 16)
                eng.dma_start(out=fmaj(t[xnout], b, c0, 512), in_=XNF3).then_inc(s_st, 16)
                eng.dma_start(out=fmaj(t[xnbfout], b, c0, 512), in_=XNB3).then_inc(s_st, 16)
        if en == "gpsimd":
            eng.wait_ge(s_st, 16 * 3 * 16)
    return emit



def ph_rt1(P, xn32="XN32", gates="GATES1"):
    t = P.t
    rx = Ring(P, "r1x", 2)
    rg = StRing(P, "r1g", 2)
    s_wl = P.sem("r1_wl")
    s_mm = P.sem("r1_mm")
    s_sg = P.sem("r1_sg")
    s_gt = P.sem("r1_gt")

    def emit(en, eng):
        ESL = P.a_f32(P.A3, 0, 512).rearrange("p (dc e) -> p dc e", e=64)
        XT = [P.a_f32(P.A3, 512 + i * 1024, 1024).rearrange("p (dc u) -> p dc u", u=128)
              for i in range(2)]
        SEL = P.a_f32(P.A3, 2560, 64)
        SELW = P.a_f32(P.A3, 2624, 64)
        E64 = P.a_f32(P.A3, 2688, 64)
        GT = [P.a_f32(P.A3, 2752 + i * 64, 64) for i in range(2)]
        mj = P.cols[:, 0:1]
        if en == "sync":
            eng.dma_start(out=ESL, in_=t["esel_l"].rearrange("(dc p) e -> p dc e", p=128)
                          ).then_inc(s_wl, 16)
            for i in range(64):
                b, tt = i // 16, i % 16
                if i >= 2:
                    eng.wait_ge(s_sg, i - 1)
                eng.dma_start(out=XT[i % 2],
                              in_=t[xn32][b * D:(b + 1) * D, tt * 128:tt * 128 + 128]
                              .rearrange("(dc p) u -> p dc u", p=128)
                              ).then_inc(rx.ld_sem(i), 16)
        if en == "tensor":
            eng.wait_ge(s_wl, 16)
            for i in range(64):
                rx.consume_wait(eng, i)
                if i > 0:
                    eng.wait_ge(s_sg, i)
                for dc in range(8):
                    ti = eng.matmul(P.PB4[:, 0:64], XT[i % 2][:, dc, :], ESL[:, dc, :],
                                    start=(dc == 0), stop=(dc == 7))
                ti.then_inc(s_mm, 1)
        if en == "scalar":
            for i in range(64):
                eng.wait_ge(s_mm, i + 1)
                if i >= 1:
                    eng.wait_ge(rg.rd, i)
                eng.activation(SEL, P.PB4[:, 0:64], AF.Sigmoid).then_inc(s_sg, 1)
        if en == "vector":
            for i in range(64):
                eng.wait_ge(s_sg, i + 1)
                eng.tensor_copy(SELW, SEL)
                eng.drain()
                for j in range(3):
                    eng.reduce_max(mj, SELW, axis=AX.X)
                    eng.drain()
                    eng.tensor_single_scalar(E64, SELW, mj, ALU.is_ge)
                    eng.drain()
                    eng.tensor_single_scalar(E64, E64, -3.0, ALU.mult)
                    eng.drain()
                    eng.tensor_tensor(SELW, SELW, E64, ALU.add)
                    eng.drain()
                eng.reduce_max(mj, SELW, axis=AX.X)
                eng.drain()
                eng.tensor_single_scalar(E64, SEL, mj, ALU.is_ge)
                eng.drain()
                rg.produce_guard(eng, i)
                rg.filled_on(eng.tensor_tensor(GT[i % 2], SEL, E64, ALU.mult), i)
        if en == "gpsimd":
            for i in range(64):
                b, tt = i // 16, i % 16
                rg.store_wait(eng, i)
                eng.dma_start(out=t[gates][b * T + tt * 128:b * T + tt * 128 + 128, :],
                              in_=GT[i % 2]).then_inc(rg.st_sem(i), 16)
            rg.wait_all(eng, 64)
    return emit



def ph_attn1(P, xnbf="XNBF", att="ATT1"):
    t = P.t
    s_wql = P.sem("a1_wql")
    s_qm = P.sem("a1_qm")
    s_qe = P.sem("a1_qe")
    s_wol = P.sem("a1_wol")
    s_om = P.sem("a1_om")
    s_kt = P.sem("a1_kt")
    s_ke = P.sem("a1_ke")
    s_qt = P.sem("a1_qt")
    s_qs = P.sem("a1_qs")
    s_sc = P.sem("a1_sc")
    s_bi = P.sem("a1_bi")
    s_ex = P.sem("a1_ex")
    s_pn = P.sem("a1_pn")
    s_ptt = P.sem("a1_ptt")
    s_pte = P.sem("a1_pte")
    s_cx = P.sem("a1_cx")
    s_cf = P.sem("a1_cf")
    r_att = StRing(P, "a1_st", 2)
    rxq = Ring(P, "a1_xq", 2)
    s_rp = P.sem("a1_rp")

    def emit(en, eng):
        QT4 = P.A1[:, 0:16384].rearrange("p (tt h d) -> p tt h d", h=16, d=64)
        KT4 = P.A1[:, 16384:32768].rearrange("p (tt h d) -> p tt h d", h=16, d=64)
        VT3 = P.A2[:, 0:16384].rearrange("p (tt f) -> p tt f", f=1024)
        CF3 = P.A2[:, 16384:32768].rearrange("p (dc u) -> p dc u", u=2048)
        WQG = P.a_bf(P.A3, 0, 4096).rearrange("p (dc f) -> p dc f", f=512)
        WOU = P.a_bf(P.A3, 0, 8192).rearrange("p (dc f) -> p dc f", f=1024)
        XT1 = [P.a_bf(P.A3, 8192 + i * 1024, 1024).rearrange("p (dc u) -> p dc u", u=128)
               for i in range(2)]
        KH = P.a_bf(P.A3, 10240, 2048)
        QH = P.a_bf(P.A3, 12288, 128)
        PSB = P.a_bf(P.A3, 12416, 2048)
        PTS = P.a_bf(P.A3, 14464, 2048)
        ROPEC = P.a_f32(P.A3, 8256, 512)
        ROPES = P.a_f32(P.A3, 8768, 512)
        TMQ = P.a_f32(P.GS, 0, 256).rearrange("p (h d) -> p h d", d=32)
        TMQ2 = P.a_f32(P.GS, 256, 256).rearrange("p (h d) -> p h d", d=32)
        ATS = [P.a_f32(P.GS, 512 + i * 512, 512) for i in range(2)]
        PB6b = P.PB6[:].bitcast(BF16)
        PB7b = P.PB7[:].bitcast(BF16)
        m4 = P.cols[:, 4:8]
        m1 = P.cols[:, 8:9]
        bcol = P.cols[:, 9:10]
        scol = P.cols[:, 10:11]
        rcol = P.cols[:, 11:12]

        if en == "sync":
            eng.dma_start(out=ROPEC, in_=t["ropec"]).then_inc(s_rp, 16)
            eng.dma_start(out=ROPES, in_=t["ropes"]).then_inc(s_rp, 16)
        if en == "vector":
            eng.wait_ge(s_rp, 32)
        for b in range(BPC):
            for fg in range(6):
                qg = b * 6 + fg
                if en == "sync":
                    if qg > 0:
                        eng.wait_ge(s_qm, 16 * qg)
                    if b > 0:
                        eng.wait_ge(s_om, 32 * b)
                    eng.dma_start(out=WQG,
                                  in_=t["WQKVbf"][:, fg * 512:fg * 512 + 512]
                                  .rearrange("(dc p) f -> p dc f", p=128)
                                  ).then_inc(s_wql, 16)
                    for tt in range(16):
                        qi = qg * 16 + tt
                        if qi >= 2:
                            eng.wait_ge(s_qm, qi - 1)
                        eng.dma_start(out=XT1[qi % 2], in_=t[xnbf]
                                      [b * D:(b + 1) * D, tt * 128:tt * 128 + 128]
                                      .rearrange("(dc p) u -> p dc u", p=128)
                                      ).then_inc(rxq.ld_sem(qi), 16)
                if en == "tensor":
                    eng.wait_ge(s_wql, 16 * (qg + 1))
                    for tt in range(16):
                        qi = qg * 16 + tt
                        rxq.consume_wait(eng, qi)
                        if qi >= 4:
                            eng.wait_ge(s_qe, qi - 3)
                        ps = P.P4[:, (qi % 4) * 512:(qi % 4) * 512 + 512]
                        for dc in range(8):
                            ti = eng.matmul(ps, XT1[qi % 2][:, dc, :], WQG[:, dc, :],
                                            start=(dc == 0), stop=(dc == 7))
                        ti.then_inc(s_qm, 1)
                if en == "vector":
                    for tt in range(16):
                        qi = qg * 16 + tt
                        eng.wait_ge(s_qm, qi + 1)
                        ps = P.P4[:, (qi % 4) * 512:(qi % 4) * 512 + 512]
                        if fg < 4:
                            ps3 = ps.rearrange("p (h d) -> p h d", d=64)
                            x1 = ps3[:, :, 0:32]
                            x2 = ps3[:, :, 32:64]
                            Cb = ROPEC[:, tt * 32:tt * 32 + 32].unsqueeze(1) \
                                .broadcast_to([128, 8, 32])
                            Sb = ROPES[:, tt * 32:tt * 32 + 32].unsqueeze(1) \
                                .broadcast_to([128, 8, 32])
                            O4 = QT4 if fg < 2 else KT4
                            h0 = (fg % 2) * 8
                            eng.tensor_tensor(TMQ[:], x1, Cb, ALU.mult)
                            eng.tensor_tensor(TMQ2[:], x2, Sb, ALU.mult)
                            eng.drain()
                            eng.tensor_tensor(O4[:, tt, h0:h0 + 8, 0:32], TMQ[:], TMQ2[:],
                                              ALU.subtract)
                            eng.tensor_tensor(TMQ[:], x1, Sb, ALU.mult)
                            eng.tensor_tensor(TMQ2[:], x2, Cb, ALU.mult)
                            eng.drain()
                            ti = eng.tensor_tensor(O4[:, tt, h0:h0 + 8, 32:64], TMQ[:],
                                                   TMQ2[:], ALU.add)
                        else:
                            f0 = (fg - 4) * 512
                            ti = eng.tensor_copy(VT3[:, tt, f0:f0 + 512], ps)
                        ti.then_inc(s_qe, 1)
            for h in range(16):
                hb = b * 16 + h
                if en == "tensor":
                    if h == 0:
                        eng.wait_ge(s_qe, 96 * (b + 1))
                    for r in range(2):
                        kr = hb * 2 + r
                        if kr >= 1:
                            eng.wait_ge(s_ke, kr)
                        if hb >= 1:
                            eng.wait_ge(s_pte, 32 * hb)
                        for j in range(8):
                            ti = eng.transpose(PB6b[0:64, j * 128:j * 128 + 128],
                                               KT4[:, r * 8 + j, h, :], P.ident_b[:])
                        ti.then_inc(s_kt, 1)
                if en == "scalar":
                    for r in range(2):
                        kr = hb * 2 + r
                        eng.wait_ge(s_kt, kr + 1)
                        eng.copy(KH[0:64, r * 1024:r * 1024 + 1024],
                                 PB6b[0:64, 0:1024]).then_inc(s_ke, 1)
                for qt in range(16):
                    qq = hb * 16 + qt
                    if en == "tensor":
                        if qq >= 1:
                            eng.wait_ge(s_qs, qq)
                        ti = eng.transpose(PB7b[0:64, 0:128], QT4[:, qt, h, :],
                                           P.ident_b[:])
                        ti.then_inc(s_qt, 1)
                        eng.wait_ge(s_qs, qq + 1)
                        eng.wait_ge(s_ke, 2 * (hb + 1))
                        if qq >= 1:
                            eng.wait_ge(s_ex, qq)
                        for j in range(4):
                            ti = eng.matmul(P.P4[:, j * 512:j * 512 + 512], QH[0:64, :],
                                            KH[0:64, j * 512:j * 512 + 512],
                                            start=True, stop=True)
                        ti.then_inc(s_sc, 1)
                        eng.wait_ge(s_pn, qq + 1)
                        for r in range(2):
                            pr = qq * 2 + r
                            eng.wait_ge(s_pte, pr)
                            for j in range(8):
                                ti = eng.transpose(PB6b[:, j * 128:j * 128 + 128],
                                                   PSB[:, (r * 8 + j) * 128:(r * 8 + j) * 128 + 128],
                                                   P.ident_b[:])
                            ti.then_inc(s_ptt, 1)
                        eng.wait_ge(s_pte, qq * 2 + 2)
                        if qq >= 1:
                            eng.wait_ge(s_cf, qq)
                        for kc in range(16):
                            ti = eng.matmul(P.PB4[0:64, 0:128], VT3[:, kc, h * 64:h * 64 + 64],
                                            PTS[:, kc * 128:kc * 128 + 128],
                                            start=(kc == 0), stop=(kc == 15))
                        ti.then_inc(s_cx, 1)
                    if en == "scalar":
                        eng.wait_ge(s_qt, qq + 1)
                        eng.copy(QH[0:64, :], PB7b[0:64, 0:128]).then_inc(s_qs, 1)
                        eng.wait_ge(s_bi, qq + 1)
                        eng.activation(PSB[:], P.P4[:], AF.Exp, scale=0.125, bias=bcol,
                                       accum_out=scol).then_inc(s_ex, 1)
                        for r in range(2):
                            eng.wait_ge(s_ptt, qq * 2 + r + 1)
                            eng.copy(PTS[:, r * 1024:r * 1024 + 1024],
                                     PB6b[:, 0:1024]).then_inc(s_pte, 1)
                        eng.wait_ge(s_cx, qq + 1)
                        eng.copy(CF3[(h % 2) * 64:(h % 2) * 64 + 64, h // 2,
                                     qt * 128:qt * 128 + 128],
                                 P.PB4[0:64, 0:128]).then_inc(s_cf, 1)
                    if en == "vector":
                        eng.wait_ge(s_sc, qq + 1)
                        for j in range(4):
                            eng.reduce_max(m4[:, j:j + 1],
                                           P.P4[:, j * 512:j * 512 + 512], axis=AX.X)
                        eng.drain()
                        eng.reduce_max(m1, m4, axis=AX.X)
                        eng.drain()
                        eng.tensor_single_scalar(bcol, m1, -0.125, ALU.mult
                                                 ).then_inc(s_bi, 1)
                        eng.wait_ge(s_ex, qq + 1)
                        eng.drain()
                        eng.reciprocal(rcol, scol)
                        eng.drain()
                        eng.tensor_single_scalar(PSB[:], PSB[:], rcol, ALU.mult
                                                 ).then_inc(s_pn, 1)
            if en == "sync":
                eng.wait_ge(s_qm, 96 * (b + 1))
                eng.dma_start(out=WOU, in_=t["WOUTbf"].rearrange("(dc p) f -> p dc f",
                                                                 p=128)
                              ).then_inc(s_wol, 16)
            if en == "tensor":
                eng.wait_ge(s_wol, 16 * (b + 1))
                eng.wait_ge(s_cf, 256 * (b + 1))
                for oj in range(32):
                    ec, tc2 = oj // 4, oj % 4
                    oi = b * 32 + oj
                    ps = P.P4[:, (oj % 4) * 512:(oj % 4) * 512 + 512]
                    if oi >= 4:
                        eng.wait_ge(r_att.rd, oi - 3)
                    for dc in range(8):
                        ti = eng.matmul(ps, WOU[:, dc, ec * 128:ec * 128 + 128],
                                        CF3[:, dc, tc2 * 512:tc2 * 512 + 512],
                                        start=(dc == 0), stop=(dc == 7))
                    ti.then_inc(s_om, 1)
            if en == "vector":
                for oj in range(32):
                    ec, tc2 = oj // 4, oj % 4
                    oi = b * 32 + oj
                    eng.wait_ge(s_om, oi + 1)
                    r_att.produce_guard(eng, oi)
                    ps = P.P4[:, (oj % 4) * 512:(oj % 4) * 512 + 512]
                    r_att.filled_on(
                        eng.tensor_single_scalar(ATS[oi % 2], ps, P.bout_c[:, ec:ec + 1],
                                                 ALU.add), oi)
            if en == "gpsimd":
                for oj in range(32):
                    ec, tc2 = oj // 4, oj % 4
                    oi = b * 32 + oj
                    r_att.store_wait(eng, oi)
                    eng.dma_start(out=t[att][b * D + ec * 128:b * D + ec * 128 + 128,
                                             tc2 * 512:tc2 * 512 + 512],
                                  in_=ATS[oi % 2]).then_inc(r_att.st_sem(oi), 16)
        if en == "gpsimd":
            r_att.wait_all(eng, 128)
    return emit



def ph_ffn1(P, xnbf="XNBF", gates="GATES1", x1="X1", att="ATT1", out="out"):
    t = P.t
    s_xt = P.sem("f1_xt")
    s_xg = P.sem("f1_xg")
    s_xc = P.sem("f1_xc")
    s_hm = P.sem("f1_hm")
    s_hr = P.sem("f1_hr")
    s_tt = P.sem("f1_tt")
    s_hg = P.sem("f1_hg")
    s_bm = P.sem("f1_bm")
    s_ev = P.sem("f1_ev")
    s_ost = P.sem("f1_ost")
    rk = Ring(P, "f1_k", 2)
    rv = Ring(P, "f1_v", 2)

    def emit(en, eng):
        HGT = P.A1[:]
        XT = [P.a_bf(P.A3, i * 1024, 1024).rearrange("p (dc u) -> p dc u", u=128)
              for i in range(4)]
        KS = [P.a_bf(P.A3, 4096 + i * 1024, 1024).rearrange("p (dc e) -> p dc e", e=128)
              for i in range(2)]
        VS = [P.a_bf(P.A3, 6144 + i * 1024, 1024) for i in range(2)]
        GC = P.a_f32(P.A2, 12288, 256)
        X1C = P.a_f32(P.A2, 0, 4096)
        ATC = P.a_f32(P.A2, 4096, 4096)
        OCH = P.a_f32(P.A2, 8192, 4096)
        X1C3 = v3(X1C, 512)
        ATC3 = v3(ATC, 512)
        OCH3 = v3(OCH, 512)
        HG = [P.a_bf(P.GS, i * 128, 128) for i in range(2)]
        PB6b = P.PB6[:].bitcast(BF16)

        for ci in range(16):
            b, tc = ci // 4, ci % 4
            trow = b * T + tc * 512
            if en == "sync":
                if ci > 0:
                    eng.wait_ge(s_ev, ci)
                for tt in range(4):
                    eng.dma_start(out=XT[tt], in_=t[xnbf]
                                  [b * D:(b + 1) * D, tc * 512 + tt * 128:tc * 512 + tt * 128 + 128]
                                  .rearrange("(dc p) u -> p dc u", p=128)
                                  ).then_inc(s_xt, 16)
                eng.dma_start(out=GC.rearrange("p (q e) -> p q e", e=64),
                              in_=t[gates][trow:trow + 512, :]
                              .rearrange("(q p) e -> p q e", p=128)).then_inc(s_xg, 16)
                eng.dma_start(out=X1C3, in_=fmaj(t[x1], b, tc * 512, 512)).then_inc(s_xc, 16)
                eng.dma_start(out=ATC3, in_=fmaj(t[att], b, tc * 512, 512)).then_inc(s_xc, 16)
                for k in range(64):
                    kk = ci * 64 + k
                    if kk >= 2:
                        eng.wait_ge(s_hm, 4 * kk - 4)
                    eng.dma_start(out=KS[kk % 2], in_=t["KEYSbf"][k]
                                  .rearrange("(dc p) e -> p dc e", p=128)
                                  ).then_inc(rk.ld_sem(kk), 16)
                for k in range(64):
                    kk = ci * 64 + k
                    if kk >= 2:
                        eng.wait_ge(s_bm, kk - 1)
                    eng.dma_start(out=VS[kk % 2], in_=t["VALbf"][k]
                                  ).then_inc(rv.ld_sem(kk), 16)
            if en == "tensor":
                if ci > 0:
                    eng.wait_ge(s_ev, ci)
                eng.wait_ge(s_xt, 16 * 4 * (ci + 1))
                for k in range(64):
                    kk = ci * 64 + k
                    rk.consume_wait(eng, kk)
                    for tt in range(4):
                        idx = kk * 4 + tt
                        hb = [P.PB4, P.PB5][tt % 2]
                        if idx >= 2:
                            eng.wait_ge(s_hr, idx - 1)
                        for dc in range(8):
                            ti = eng.matmul(hb[:, 0:128], XT[tt][:, dc, :],
                                            KS[kk % 2][:, dc, :],
                                            start=(dc == 0), stop=(dc == 7))
                        ti.then_inc(s_hm, 1)
                    for tt in range(4):
                        idx = kk * 4 + tt
                        eng.wait_ge(s_hr, idx + 1)
                        if kk >= 1:
                            eng.wait_ge(s_hg, kk)
                        ti = eng.transpose(PB6b[:, tt * 128:tt * 128 + 128], HG[tt % 2][:],
                                           P.ident_b[:])
                        ti.then_inc(s_tt, 1)
                eng.wait_ge(s_hg, 64 * (ci + 1))
                for k in range(64):
                    kk = ci * 64 + k
                    rv.consume_wait(eng, kk)
                    for vc in range(8):
                        ti = eng.matmul(bankv(P, vc), VS[kk % 2][:, vc * 128:vc * 128 + 128],
                                        HGT[:, k * 512:k * 512 + 512],
                                        start=(k == 0), stop=(k == 63))
                    ti.then_inc(s_bm, 1)
            if en == "scalar":
                if ci == 0 or True:
                    eng.wait_ge(s_xg, 16 * (ci + 1))
                for k in range(64):
                    kk = ci * 64 + k
                    for tt in range(4):
                        idx = kk * 4 + tt
                        eng.wait_ge(s_hm, idx + 1)
                        if idx >= 2:
                            eng.wait_ge(s_tt, idx - 1)
                        hb = [P.PB4, P.PB5][tt % 2]
                        gcol = GC[:, tt * 64 + k:tt * 64 + k + 1]
                        eng.activation(HG[tt % 2][:], hb[:, 0:128], AF.Relu,
                                       scale=gcol).then_inc(s_hr, 1)
                    eng.wait_ge(s_tt, kk * 4 + 4)
                    eng.copy(HGT[:, k * 512:k * 512 + 512], PB6b[:, 0:512]
                             ).then_inc(s_hg, 1)
            if en == "vector":
                eng.wait_ge(s_bm, 64 * (ci + 1))
                eng.wait_ge(s_xc, 32 * (ci + 1))
                if ci > 0:
                    eng.wait_ge(s_ost, 16 * ci)
                for dc in range(8):
                    eng.tensor_tensor(OCH3[:, dc, :], X1C3[:, dc, :], ATC3[:, dc, :],
                                      ALU.add)
                    ti = eng.tensor_tensor(OCH3[:, dc, :], OCH3[:, dc, :],
                                           bankv(P, dc), ALU.add)
                ti.then_inc(s_ev, 1)
            if en == "gpsimd":
                eng.wait_ge(s_ev, ci + 1)
                eng.dma_start(out=fmaj(t[out], b, tc * 512, 512), in_=OCH3
                              ).then_inc(s_ost, 16)
        if en == "gpsimd":
            eng.wait_ge(s_ost, 16 * 16)
    return emit



PARTS = {
    "l0a": {
        "ins": ["xin", "wm_l", "vcols", "wproj_l", "oscale"],
        "outs": ["XN32", "XNBF", "ATT"],
        "scratch": [],
        "phases": lambda P: [ph_init(P), ph_l0a(P)],
    },
    "ffn0": {
        "ins": ["XN32", "XNBF", "router", "shup", "shdn", "rup", "rdn"],
        "outs": ["ACC", "GATES0"],
        "scratch": ["UPbf", "DOWNbf"],
        "phases": lambda P: [ph_init(P), ph_precast(P, PRECAST_FFN0, "f0"),
                             ph_router0(P), ph_ffn0(P)],
    },
    "cmb0": {
        "ins": ["xin", "XN32", "ATT", "ACC"],
        "outs": ["X1", "XN132", "XN1BF"],
        "scratch": [],
        "phases": lambda P: [ph_init(P), ph_cmb0(P, xnout="XN132", xnbfout="XN1BF")],
    },
    "attn1": {
        "ins": ["XN1BF", "wqkv_l", "wout_l", "bout", "ropec", "ropes"],
        "outs": ["ATT1"],
        "scratch": ["WQKVbf", "WOUTbf"],
        "phases": lambda P: [ph_init(P), ph_precast(P, PRECAST_ATTN1, "a1"),
                             ph_attn1(P, xnbf="XN1BF")],
    },
    "ffn1": {
        "ins": ["XN132", "XN1BF", "X1", "ATT1", "esel_l", "keysw", "valsw"],
        "outs": ["out", "GATES1"],
        "scratch": ["KEYSbf", "VALbf"],
        "phases": lambda P: [ph_init(P), ph_precast(P, PRECAST_FFN1, "f1"),
                             ph_rt1(P, xn32="XN132", gates="GATES1"),
                             ph_ffn1(P, xnbf="XN1BF", gates="GATES1")],
    },
}


def build_part(part):
    spec = PARTS[part]
    P = Prog(spec["ins"], spec["outs"], spec["scratch"])
    for ph in spec["phases"](P):
        P.add(ph)
    return P.finalize(), spec


def run_part(part, in_map_per_core, core_ids=None):
    nc, spec = build_part(part)
    core_ids = core_ids or list(range(len(in_map_per_core)))
    res = run_bass_kernel_spmd(nc, in_map_per_core, core_ids=core_ids)
    return res


def kernel(**inputs):
    w = prep_shared(inputs)
    x = inputs["L0_attn_x"]
    xin = [prep_core_x(x, c) for c in range(NCORES)]
    cores = list(range(NCORES))

    def launch(part, per_core_extra):
        nc, spec = build_part(part)
        in_maps = []
        for c in range(NCORES):
            m = {}
            for k in spec["ins"]:
                if k in w:
                    m[k] = w[k]
                elif k == "xin":
                    m[k] = xin[c]
                else:
                    m[k] = per_core_extra[k][c]
            in_maps.append(m)
        res = run_bass_kernel_spmd(nc, in_maps, core_ids=cores)
        return res.results

    r1 = launch("l0a", {})
    r2 = launch("ffn0", {"XN32": [r["XN32"] for r in r1],
                         "XNBF": [r["XNBF"] for r in r1]})
    r3 = launch("cmb0", {"XN32": [r["XN32"] for r in r1],
                         "ATT": [r["ATT"] for r in r1],
                         "ACC": [r["ACC"] for r in r2]})
    r4 = launch("attn1", {"XN1BF": [r["XN1BF"] for r in r3]})
    r5 = launch("ffn1", {"XN132": [r["XN132"] for r in r3],
                         "XN1BF": [r["XN1BF"] for r in r3],
                         "X1": [r["X1"] for r in r3],
                         "ATT1": [r["ATT1"] for r in r4]})
    out = np.empty((32, T, D), np.float32)
    for c in range(NCORES):
        out[c * BPC:(c + 1) * BPC] = np.asarray(r5[c]["out"], np.float32) \
            .reshape(BPC, D, T).transpose(0, 2, 1)
    return out


# version 04
# speedup vs baseline: 1.1566x; 1.1566x over previous
import contextlib

import numpy as np

import concourse.bass as bass
import concourse.mybir as mybir
from concourse.bass_utils import run_bass_kernel_spmd

F32 = mybir.dt.float32
BF16 = mybir.dt.bfloat16
AF = mybir.ActivationFunctionType
ALU = mybir.AluOpType
AX = mybir.AxisListType

NCORES = 8
BPC = 4
T = 2048
D = 1024
NTOK = BPC * T
EPS_LN = 1e-5
EPS_RMS = 1.1920929e-07



def _cc(a):
    return np.ascontiguousarray(a)


def prep_shared(inputs):
    w = {}
    wm = np.asarray(inputs["L0_attn_W_m"], np.float32)
    wm_perm = wm.reshape(256, 4, 1024).transpose(1, 0, 2).reshape(1024, 1024)
    w["wm_l"] = _cc(wm_perm.T)
    v = np.asarray(inputs["L0_attn_v_vec"], np.float32)[:, :, 0]
    w["vcols"] = _cc(v.reshape(2, 128, 2).transpose(0, 2, 1).reshape(4, 128))
    wp = np.asarray(inputs["L0_attn_W_proj"], np.float32)
    w["wproj_l"] = _cc(wp.T.reshape(256, 2, 1024).transpose(1, 0, 2).reshape(512, 1024))
    w["oscale"] = _cc(np.asarray(inputs["L0_attn_out_scale"], np.float32).reshape(8, 128))
    w["shup"] = _cc(np.asarray(inputs["L0_ffn_shared_up"], np.float32)[0])
    w["shdn"] = _cc(np.asarray(inputs["L0_ffn_shared_down"], np.float32)[0])
    w["rup"] = _cc(np.asarray(inputs["L0_ffn_routed_up"], np.float32))
    w["rdn"] = _cc(np.asarray(inputs["L0_ffn_routed_down"], np.float32))
    w["router"] = _cc(np.asarray(inputs["L0_ffn_router_w"], np.float32))
    w["wqkv_l"] = _cc(np.asarray(inputs["L1_attn_w_qkv"], np.float32).T)
    w["wout_l"] = _cc(np.asarray(inputs["L1_attn_w_out"], np.float32).T)
    w["bout"] = _cc(np.asarray(inputs["L1_attn_b_out"], np.float32).reshape(8, 128))
    w["esel_l"] = _cc(np.asarray(inputs["L1_ffn_expert_sel"], np.float32).T)
    w["keysw"] = _cc(np.asarray(inputs["L1_ffn_keys"], np.float32))
    w["valsw"] = _cc(np.asarray(inputs["L1_ffn_values"], np.float32))
    hd = 64
    inv = (1.0 / (np.float32(10000.0) ** (np.arange(0, hd, 2, dtype=np.float32)
                                          / np.float32(hd)))).astype(np.float32)
    tt = np.arange(T, dtype=np.float32)
    fr = tt[:, None] * inv[None, :]
    w["ropec"] = _cc(np.cos(fr).astype(np.float32).reshape(16, 128, 32)
                     .transpose(1, 0, 2).reshape(128, 512))
    w["ropes"] = _cc(np.sin(fr).astype(np.float32).reshape(16, 128, 32)
                     .transpose(1, 0, 2).reshape(128, 512))
    return w


def prep_core_x(x, core):
    xs = np.asarray(x, np.float32)[core * BPC:(core + 1) * BPC]
    return _cc(xs.transpose(0, 2, 1).reshape(BPC * D, T))


DRAM_SPEC = {
    "xin":   ([BPC * D, T], F32),
    "wm_l":  ([1024, 1024], F32),
    "vcols": ([4, 128], F32),
    "wproj_l": ([512, 1024], F32),
    "oscale": ([8, 128], F32),
    "shup":  ([1024, 4096], F32),
    "shdn":  ([4096, 1024], F32),
    "rup":   ([8, 1024, 4096], F32),
    "rdn":   ([8, 4096, 1024], F32),
    "router": ([1024, 8], F32),
    "wqkv_l": ([1024, 3072], F32),
    "wout_l": ([1024, 1024], F32),
    "bout":  ([8, 128], F32),
    "esel_l": ([1024, 64], F32),
    "keysw": ([64, 1024, 128], F32),
    "valsw": ([64, 128, 1024], F32),
    "ropec": ([128, 512], F32),
    "ropes": ([128, 512], F32),
    "XN32":  ([BPC * D, T], F32),
    "XNBF":  ([BPC * D, T], BF16),
    "ATT":   ([BPC * D, T], F32),
    "ACC":   ([NTOK, D], F32),
    "X1":    ([BPC * D, T], F32),
    "GATES0": ([NTOK, 8], F32),
    "GATES1": ([NTOK, 64], F32),
    "UPbf":  ([9, 1024, 4096], BF16),
    "DOWNbf": ([9, 4096, 1024], BF16),
    "WQKVbf": ([1024, 3072], BF16),
    "WOUTbf": ([1024, 1024], BF16),
    "KEYSbf": ([64, 1024, 128], BF16),
    "VALbf": ([64, 128, 1024], BF16),
    "XN132": ([BPC * D, T], F32),
    "XN1BF": ([BPC * D, T], BF16),
    "ATT1":  ([BPC * D, T], F32),
    "out":   ([BPC * D, T], F32),
}

PRECAST_FFN0 = ([("shup", None, "UPbf", 0)] +
                [("rup", e, "UPbf", e + 1) for e in range(8)] +
                [("shdn", None, "DOWNbf", 0)] +
                [("rdn", e, "DOWNbf", e + 1) for e in range(8)])
PRECAST_ATTN1 = [("wqkv_l", None, "WQKVbf", None), ("wout_l", None, "WOUTbf", None)]
PRECAST_FFN1 = [("keysw", None, "KEYSbf", None), ("valsw", None, "VALbf", None)]


class Prog:
    def __init__(self, ins, outs, scratch):
        self.nc = bass.Bass("TRN2", target_bir_lowering=False, debug=False,
                            detect_race_conditions=False)
        self.ctx = contextlib.ExitStack()
        self.t = {}
        for name in ins:
            shape, dt = DRAM_SPEC[name]
            self.t[name] = self.nc.dram_tensor(name, shape, dt, kind="ExternalInput").ap()
        for name in outs:
            shape, dt = DRAM_SPEC[name]
            self.t[name] = self.nc.dram_tensor(name, shape, dt, kind="ExternalOutput").ap()
        for name in scratch:
            shape, dt = DRAM_SPEC[name]
            self.t[name] = self.nc.dram_tensor(name, shape, dt, kind="Internal").ap()
        c, nc = self.ctx, self.nc
        self.A1 = c.enter_context(nc.sbuf_tensor("A1", [128, 32768], BF16))
        self.A2 = c.enter_context(nc.sbuf_tensor("A2", [128, 32768], BF16))
        self.A3 = c.enter_context(nc.sbuf_tensor("A3", [128, 20480], BF16))
        self.GS = c.enter_context(nc.sbuf_tensor("GS", [128, 4096], BF16))
        self.ident_b = c.enter_context(nc.sbuf_tensor("ident_b", [128, 128], BF16))
        self.ident_f = c.enter_context(nc.sbuf_tensor("ident_f", [128, 128], F32))
        self.onesc = c.enter_context(nc.sbuf_tensor("onesc", [128, 1], F32))
        self.onesr = c.enter_context(nc.sbuf_tensor("onesr", [1, 128], F32))
        self.rows = c.enter_context(nc.sbuf_tensor("rows", [1, 3072], F32))
        self.cols = c.enter_context(nc.sbuf_tensor("cols", [128, 16], F32))
        self.oscale_c = c.enter_context(nc.sbuf_tensor("oscale_c", [128, 8], F32))
        self.bout_c = c.enter_context(nc.sbuf_tensor("bout_c", [128, 8], F32))
        self.vcols_c = c.enter_context(nc.sbuf_tensor("vcols_c", [128, 4], F32))
        self.wcar = c.enter_context(nc.sbuf_tensor("wcar", [128, 96], F32))
        self.P4 = c.enter_context(nc.psum_tensor("P4", [128, 2048], F32))
        self.PB4 = c.enter_context(nc.psum_tensor("PB4", [128, 512], F32))
        self.PB5 = c.enter_context(nc.psum_tensor("PB5", [128, 512], F32))
        self.PB6 = c.enter_context(nc.psum_tensor("PB6", [128, 512], F32))
        self.PB7 = c.enter_context(nc.psum_tensor("PB7", [128, 512], F32))
        self.sems = {}
        self.bar = self.sem("bar")
        self.phases = []

    def sem(self, name):
        h = self.ctx.enter_context(self.nc.semaphore(name))
        self.sems[name] = h
        return h

    def add(self, emitfn):
        self.phases.append(emitfn)

    def finalize(self):
        nc, phases, bar = self.nc, self.phases, self.bar
        with nc.Block() as block:
            def mk(en):
                def body(eng):
                    for k, ph in enumerate(phases):
                        if k > 0:
                            eng.wait_ge(bar, 5 * k)
                        ph(en, eng)
                        eng.drain()
                        eng.sem_inc(bar, 1)
                return body
            block.sync(mk("sync"))
            block.tensor(mk("tensor"))
            block.vector(mk("vector"))
            block.scalar(mk("scalar"))
            block.gpsimd(mk("gpsimd"))
        self.ctx.close()
        return self.nc

    def a_f32(self, arena, start_f32, n_f32):
        return arena[:].bitcast(F32)[:, start_f32:start_f32 + n_f32]

    def a_bf(self, arena, start_bf, n_bf):
        return arena[:, start_bf:start_bf + n_bf]

    def row(self, i):
        return self.rows[0:1, i * 512:(i + 1) * 512]


class Ring:

    def __init__(self, P, name, n):
        self.n = n
        self.ld = [P.sem(f"{name}_ld{i}") for i in range(n)]

    def ld_sem(self, i):
        return self.ld[i % self.n]

    def consume_wait(self, eng, i):
        eng.wait_ge(self.ld[i % self.n], 16 * (i // self.n + 1))


class StRing:

    def __init__(self, P, name, n):
        self.n = n
        self.st = [P.sem(f"{name}_st{i}") for i in range(n)]
        self.rd = P.sem(f"{name}_rd")

    def produce_guard(self, eng, i):
        if i >= self.n:
            eng.wait_ge(self.st[i % self.n], 16 * ((i - self.n) // self.n + 1))

    def filled(self, eng, i):
        eng.sem_inc(self.rd, 1)

    def filled_on(self, instr, i):
        instr.then_inc(self.rd, 1)

    def store_wait(self, eng, i):
        eng.wait_ge(self.rd, i + 1)

    def st_sem(self, i):
        return self.st[i % self.n]

    def wait_all(self, eng, total):
        for j in range(self.n):
            cnt = (total - j + self.n - 1) // self.n if total > j else 0
            if cnt:
                eng.wait_ge(self.st[j], 16 * cnt)


def fmaj(dram_ap, b, c0, w):
    return dram_ap[b * D:(b + 1) * D, c0:c0 + w].rearrange("(dc p) u -> p dc u", p=128)


def v3(ap2d, u):
    return ap2d.rearrange("p (dc u) -> p dc u", u=u)



def ph_init(P):
    t = P.t
    sld = P.sem("ini_ld")
    sgi = P.sem("ini_g")

    def emit(en, eng):
        if en == "gpsimd":
            eng.memset(P.ident_b[:], 0.0)
            eng.memset(P.ident_f[:], 0.0).then_inc(sgi, 1)
            eng.wait_ge(sgi, 1)
            for idn in (P.ident_b, P.ident_f):
                eng.affine_select(out=idn[:], in_=idn[:], compare_op=ALU.not_equal,
                                  fill=1.0, base=0, pattern=[[-1, 128]], channel_multiplier=1)
        if en == "vector":
            eng.memset(P.onesc[:], 1.0)
            eng.memset(P.onesr[:], 1.0)
        if en == "sync":
            n = 0
            for name, dst, k in (("oscale", P.oscale_c, 8), ("bout", P.bout_c, 8),
                                 ("vcols", P.vcols_c, 4)):
                if name in t:
                    for c in range(k):
                        eng.dma_start(out=dst[:, c:c + 1],
                                      in_=t[name][c, :].unsqueeze(1)).then_inc(sld, 16)
                    n += k
            eng.wait_ge(sld, 16 * n)
    return emit



def ph_precast(P, jobs, tag):
    t = P.t
    CH = 1048576
    plan = []
    for (src, si, d, di) in jobs:
        shape = DRAM_SPEC[src][0]
        n = int(np.prod(shape[1:]) if si is not None else np.prod(shape))
        assert n % CH == 0, (src, n)
        for c in range(n // CH):
            plan.append((src, si, d, di, c))
    rin = Ring(P, f"pc{tag}i", 2)
    rout = StRing(P, f"pc{tag}o", 2)

    def flat_view(name, idx, c):
        ap = t[name]
        if idx is not None:
            ap = ap[idx]
        if len(ap.shape) == 2:
            ap = ap.rearrange("a b -> (a b)")
        elif len(ap.shape) == 3:
            ap = ap.rearrange("a b c -> (a b c)")
        return ap.rearrange("(c p m) -> c p m", p=128, m=8192)[c]

    def emit(en, eng):
        nch = len(plan)
        for i, (src, si, d, di, c) in enumerate(plan):
            slot = i % 2
            stg = P.a_f32(P.A1, slot * 8192, 8192)
            obf = P.a_bf(P.A2, slot * 8192, 8192)
            if en == "sync":
                if i >= 2:
                    eng.wait_ge(rout.rd, i - 1)
                eng.dma_start(out=stg, in_=flat_view(src, si, c)).then_inc(rin.ld_sem(i), 16)
            if en == "vector":
                rin.consume_wait(eng, i)
                rout.produce_guard(eng, i)
                rout.filled_on(eng.tensor_copy(obf, stg), i)
            if en == "gpsimd":
                rout.store_wait(eng, i)
                eng.dma_start(out=flat_view(d, di, c), in_=obf).then_inc(rout.st_sem(i), 16)
        if en == "gpsimd":
            rout.wait_all(eng, nch)
    return emit



def emit_ln_rows(P, en, eng, s_rv, s_rs, rvb, rsb, rms2):
    r_mean, r_msq, r_t2, r_var, r_A, r_B = (P.row(i) for i in range(6))
    r_A2, r_B2 = r_mean, r_var
    r_tmp, r_ve = r_t2, r_msq
    if en == "vector":
        eng.tensor_single_scalar(r_mean, P.PB4[0:1, :], 1.0 / 1024.0, ALU.mult
                                 ).then_inc(s_rv, 1)
        eng.tensor_single_scalar(r_msq, P.PB5[0:1, :], 1.0 / 1024.0, ALU.mult)
        eng.wait_ge(s_rs, rsb + 1)
        eng.tensor_tensor(r_var, r_msq, r_tmp, ALU.subtract)
        eng.tensor_single_scalar(r_ve, r_var, EPS_LN, ALU.add).then_inc(s_rv, 1)
        eng.wait_ge(s_rs, rsb + 2)
        eng.wait_ge(s_rv, rvb + 1)
        eng.tensor_tensor(r_B, r_mean, r_A, ALU.mult)
        if rms2:
            eng.tensor_tensor(r_t2, r_A, r_A, ALU.mult)
            eng.tensor_tensor(r_t2, r_var, r_t2, ALU.mult)
            eng.tensor_single_scalar(r_t2, r_t2, EPS_RMS, ALU.add).then_inc(s_rv, 1)
            eng.wait_ge(s_rs, rsb + 3)
            eng.tensor_tensor(r_A2, r_A, r_t2, ALU.mult)
            eng.tensor_tensor(r_B2, r_B, r_t2, ALU.mult).then_inc(s_rv, 1)
        else:
            eng.tensor_copy(P.row(0), r_A).then_inc(s_rv, 1)
    if en == "scalar":
        eng.wait_ge(s_rv, rvb + 1)
        eng.square(r_tmp, r_mean).then_inc(s_rs, 1)
        eng.wait_ge(s_rv, rvb + 2)
        eng.activation(r_A, r_ve, AF.Ln)
        eng.activation(r_A, r_A, AF.Exp, scale=-0.5).then_inc(s_rs, 1)
        if rms2:
            eng.wait_ge(s_rv, rvb + 3)
            eng.activation(r_t2, r_t2, AF.Ln)
            eng.activation(r_t2, r_t2, AF.Exp, scale=-0.5).then_inc(s_rs, 1)
    return (4, 3) if rms2 else (3, 2)



def ph_l0a(P, xn32="XN32", xnbf="XNBF", att="ATT"):
    t = P.t
    s_wld = P.sem("la_wld")
    s_wcv = P.sem("la_wcv")
    s_xld = P.sem("la_xld")
    s_sq = P.sem("la_sq")
    s_stt = P.sem("la_stt")
    s_rv = P.sem("la_rv")
    s_rs = P.sem("la_rs")
    s_bc = P.sem("la_bc")
    s_xn = P.sem("la_xn")
    s_mm = P.sem("la_mm")
    s_me = P.sem("la_me")
    s_xst = P.sem("la_xst")
    s_nv = P.sem("la_nv")
    s_ns = P.sem("la_ns")
    s_hb = P.sem("la_hb")
    s_wp = P.sem("la_wp")
    s_wc2 = P.sem("la_wc2")
    s_op = P.sem("la_op")
    NRV, NRS = 4, 3
    r_att = StRing(P, "la_att", 4)
    P.recycle_pool = [(s_xld, 256), (s_sq, 16), (s_stt, 16), (s_rv, 64), (s_rs, 48),
                      (s_bc, 16), (s_xn, 16), (s_mm, 128), (s_me, 128), (s_xst, 512),
                      (s_nv, 8), (s_ns, 8), (s_hb, 4), (s_op, 128), (s_wld, 32),
                      (s_wp, 64), (s_wcv, 2), (s_wc2, 4)]

    def emit(en, eng):
        SA = [P.a_f32(P.A1, i * 2048, 2048) for i in range(8)]
        SB = [P.a_f32(P.A2, i * 2048, 2048) for i in range(8)]
        SA3 = [x.rearrange("p (c u) -> p c u", u=128) for x in SA]
        SB3 = [x.rearrange("p (c u) -> p c u", u=128) for x in SB]
        WM = P.a_bf(P.A3, 0, 8192)
        HBF = [P.a_bf(P.A3, 8192 + i * 2048, 2048) for i in range(4)]
        HBF3 = [x.rearrange("p (c u) -> p c u", u=128) for x in HBF]
        WPJ = P.a_bf(P.A3, 16384, 4096)
        TMP = P.a_f32(P.GS, 0, 2048)
        TMP3 = TMP.rearrange("p (c u) -> p c u", u=128)
        XCH = P.a_f32(P.A2, 0, 4096)
        XSQ = P.a_f32(P.A2, 4096, 4096)
        XNF = P.a_f32(P.A2, 8192, 4096)
        XNB = P.a_bf(P.A2, 24576, 4096)
        XN2B = P.a_bf(P.A2, 28672, 4096)
        WSTG = [P.a_f32(P.A2, 0, 4096), P.a_f32(P.A2, 4096, 4096)]
        ATS = [P.a_f32(P.A2, i * 512, 512) for i in range(4)]
        XCH3 = XCH.rearrange("p (dc u) -> p dc u", u=512)
        XSQ3 = XSQ.rearrange("p (dc u) -> p dc u", u=512)
        XNF3 = XNF.rearrange("p (dc u) -> p dc u", u=512)
        XNB3 = XNB.rearrange("p (dc u) -> p dc u", u=512)
        XN2B3 = XN2B.rearrange("p (dc u) -> p dc u", u=512)
        WPJ3 = WPJ.rearrange("p (ic f) -> p ic f", f=1024)

        if en == "sync":
            for half in range(2):
                eng.dma_start(out=v3(WSTG[half], 1024),
                              in_=t["wm_l"][half * 512:(half + 1) * 512, :]
                              .rearrange("(dc p) f -> p dc f", p=128)).then_inc(s_wld, 16)
        if en == "vector":
            eng.wait_ge(s_wld, 32)
            for half in range(2):
                eng.tensor_copy(P.a_bf(P.A3, half * 4096, 4096), WSTG[half]).then_inc(s_wcv, 1)

        for b in range(BPC):
            for tc in range(4):
                ci = b * 4 + tc
                c0 = tc * 512
                rvb, rsb = NRV * ci, NRS * ci
                if en == "sync":
                    if ci == 0:
                        eng.wait_ge(s_wcv, 2)
                    else:
                        eng.wait_ge(s_xn, ci)
                    if tc == 0 and b > 0:
                        r_att.wait_all(eng, 32 * b)
                    eng.dma_start(out=XCH3, in_=fmaj(t["xin"], b, c0, 512)
                                  ).then_inc(s_xld, 16)
                if en == "scalar":
                    eng.wait_ge(s_xld, 16 * (ci + 1))
                    eng.activation(XSQ, XCH, AF.Square).then_inc(s_sq, 1)
                    emit_ln_rows(P, en, eng, s_rv, s_rs, rvb, rsb, rms2=True)
                if en == "tensor":
                    eng.wait_ge(s_xld, 16 * (ci + 1))
                    if ci > 0:
                        eng.wait_ge(s_xn, ci)
                    if b > 0 and tc == 0:
                        eng.wait_ge(r_att.rd, 32 * b)
                    for dc in range(8):
                        eng.matmul(P.PB4[0:1, :], P.onesc[:], XCH3[:, dc, :],
                                   start=(dc == 0), stop=(dc == 7))
                    eng.wait_ge(s_sq, ci + 1)
                    for dc in range(8):
                        ti = eng.matmul(P.PB5[0:1, :], P.onesc[:], XSQ3[:, dc, :],
                                        start=(dc == 0), stop=(dc == 7))
                    ti.then_inc(s_stt, 1)
                    eng.wait_ge(s_rv, rvb + NRV)
                    eng.wait_ge(s_rs, rsb + NRS)
                    eng.matmul(P.PB4[:], P.onesr[:], P.row(4), start=True, stop=True)
                    eng.matmul(P.PB5[:], P.onesr[:], P.row(5), start=True, stop=True)
                    eng.matmul(P.PB6[:], P.onesr[:], P.row(0), start=True, stop=True)
                    ti = eng.matmul(P.PB7[:], P.onesr[:], P.row(3), start=True, stop=True)
                    ti.then_inc(s_bc, 1)
                    eng.wait_ge(s_xn, ci + 1)
                    for fc in range(8):
                        g = ci * 8 + fc
                        ps = P.P4[:, (fc % 4) * 512:(fc % 4) * 512 + 512]
                        if g >= 4:
                            eng.wait_ge(s_me, g - 3)
                        for dc in range(8):
                            ti = eng.matmul(
                                ps, WM[:, dc * 1024 + fc * 128:dc * 1024 + fc * 128 + 128],
                                XN2B3[:, dc, :], start=(dc == 0), stop=(dc == 7))
                        ti.then_inc(s_mm, 1)
                if en == "vector":
                    eng.wait_ge(s_stt, ci + 1)
                    emit_ln_rows(P, en, eng, s_rv, s_rs, rvb, rsb, rms2=True)
                    eng.wait_ge(s_bc, ci + 1)
                    if ci > 0:
                        eng.wait_ge(s_xst, 32 * ci)
                    for dc in range(8):
                        eng.tensor_tensor(XNF3[:, dc, :], XCH3[:, dc, :], P.PB4[:], ALU.mult)
                        eng.tensor_tensor(XNF3[:, dc, :], XNF3[:, dc, :], P.PB5[:],
                                          ALU.subtract)
                        eng.tensor_copy(XNB3[:, dc, :], XNF3[:, dc, :])
                        eng.tensor_tensor(XSQ3[:, dc, :], XCH3[:, dc, :], P.PB6[:], ALU.mult)
                        ti = eng.tensor_tensor(XN2B3[:, dc, :], XSQ3[:, dc, :], P.PB7[:],
                                               ALU.subtract)
                    ti.then_inc(s_xn, 1)
                    for fc in range(8):
                        g = ci * 8 + fc
                        eng.wait_ge(s_mm, g + 1)
                        ps = P.P4[:, (fc % 4) * 512:(fc % 4) * 512 + 512]
                        eng.tensor_copy(SA[fc][:, c0:c0 + 512], ps).then_inc(s_me, 1)
                if en == "gpsimd":
                    eng.wait_ge(s_xn, ci + 1)
                    eng.dma_start(out=fmaj(t[xn32], b, c0, 512), in_=XNF3).then_inc(s_xst, 16)
                    eng.dma_start(out=fmaj(t[xnbf], b, c0, 512), in_=XNB3).then_inc(s_xst, 16)

            if en == "vector":
                eng.wait_ge(s_me, (b * 4 + 4) * 8)
                for kh in range(2):
                    U = SB[7]
                    eng.tensor_tensor(TMP, SA[kh], SA[kh], ALU.mult)
                    for c in range(1, 4):
                        eng.tensor_tensor(U, SA[2 * c + kh], SA[2 * c + kh], ALU.mult)
                        eng.tensor_tensor(TMP, TMP, U, ALU.add)
                    eng.tensor_single_scalar(TMP, TMP, 1e-12, ALU.max).then_inc(s_nv, 1)
                    eng.wait_ge(s_ns, b * 2 + kh + 1)
                    for c in range(4):
                        eng.tensor_tensor(SA[2 * c + kh], SA[2 * c + kh], TMP, ALU.mult)
            if en == "scalar":
                for kh in range(2):
                    eng.wait_ge(s_nv, b * 2 + kh + 1)
                    eng.activation(TMP, TMP, AF.Ln)
                    eng.activation(TMP, TMP, AF.Exp, scale=-0.5).then_inc(s_ns, 1)

            if en == "vector":
                cur3, nxt3 = SA3, SB3
                for step in range(7):
                    s = 1 << step
                    for kh in range(2):
                        for i in range(2):
                            for j in range(2):
                                ot = nxt3[2 * (i * 2 + j) + kh]
                                a0 = cur3[2 * (i * 2 + 0) + kh]
                                a1 = cur3[2 * (i * 2 + 1) + kh]
                                b0 = cur3[2 * (0 * 2 + j) + kh]
                                b1 = cur3[2 * (1 * 2 + j) + kh]
                                eng.tensor_tensor(ot[:, :, s:], a0[:, :, s:],
                                                  b0[:, :, :128 - s], ALU.mult)
                                eng.tensor_tensor(TMP3[:, :, s:], a1[:, :, s:],
                                                  b1[:, :, :128 - s], ALU.mult)
                                eng.tensor_tensor(ot[:, :, s:], ot[:, :, s:],
                                                  TMP3[:, :, s:], ALU.add)
                        for cc in range(4):
                            ti = 2 * cc + kh
                            eng.tensor_copy(nxt3[ti][:, :, :s], cur3[ti][:, :, :s])
                    cur3, nxt3 = nxt3, cur3
                SL = cur3
                eng.drain()
                for kh in range(2):
                    for j in range(2):
                        eng.tensor_copy(P.wcar[:, kh * 2 + j:kh * 2 + j + 1],
                                        P.vcols_c[:, kh * 2 + j:kh * 2 + j + 1])
                eng.drain()
                for c in range(16):
                    for kh in range(2):
                        w0 = P.wcar[:, kh * 2 + 0:kh * 2 + 1]
                        w1 = P.wcar[:, kh * 2 + 1:kh * 2 + 2]
                        sv0 = P.wcar[:, 4 + c * 4 + kh * 2:4 + c * 4 + kh * 2 + 1]
                        sv1 = P.wcar[:, 5 + c * 4 + kh * 2:5 + c * 4 + kh * 2 + 1]
                        eng.tensor_copy(sv0, w0)
                        eng.tensor_copy(sv1, w1)
                        if c < 15:
                            tt0 = P.wcar[:, 68:69]
                            tt1 = P.wcar[:, 69:70]
                            tt2 = P.wcar[:, 70:71]
                            tt3 = P.wcar[:, 71:72]
                            Pm = [SL[2 * cc + kh][:, c, 127:128] for cc in range(4)]
                            eng.drain()
                            eng.tensor_tensor(tt0, Pm[0], sv0, ALU.mult)
                            eng.tensor_tensor(tt1, Pm[1], sv1, ALU.mult)
                            eng.tensor_tensor(tt2, Pm[2], sv0, ALU.mult)
                            eng.tensor_tensor(tt3, Pm[3], sv1, ALU.mult)
                            eng.drain()
                            eng.tensor_tensor(w0, tt0, tt1, ALU.add)
                            eng.tensor_tensor(w1, tt2, tt3, ALU.add)
                            eng.drain()
                eng.drain()
                for c in range(16):
                    for kh in range(2):
                        sv0 = P.wcar[:, 4 + c * 4 + kh * 2:4 + c * 4 + kh * 2 + 1]
                        sv1 = P.wcar[:, 5 + c * 4 + kh * 2:5 + c * 4 + kh * 2 + 1]
                        for i in range(2):
                            tmpd = TMP3[:, c, :]
                            eng.tensor_single_scalar(
                                tmpd, SL[2 * (i * 2 + 0) + kh][:, c, :], sv0, ALU.mult)
                            eng.drain()
                            ti = eng.scalar_tensor_tensor(
                                HBF3[i * 2 + kh][:, c, :], SL[2 * (i * 2 + 1) + kh][:, c, :],
                                sv1, tmpd, ALU.mult, ALU.add)
                ti.then_inc(s_hb, 1)

            if en == "sync":
                eng.wait_ge(s_hb, b + 1)
                eng.dma_start(out=v3(WSTG[0], 1024),
                              in_=t["wproj_l"].rearrange("(ic p) f -> p ic f", p=128)
                              ).then_inc(s_wp, 16)
            if en == "vector":
                eng.wait_ge(s_wp, 16 * (b + 1))
                eng.tensor_copy(WPJ, WSTG[0]).then_inc(s_wc2, 1)
            if en == "tensor":
                eng.wait_ge(s_wc2, b + 1)
                for oj in range(32):
                    ec, tc2 = oj // 4, oj % 4
                    oi = b * 32 + oj
                    ps = P.P4[:, (oj % 4) * 512:(oj % 4) * 512 + 512]
                    if oi >= 4:
                        eng.wait_ge(r_att.rd, oi - 3)
                    for ic in range(4):
                        ti = eng.matmul(ps, WPJ3[:, ic, ec * 128:ec * 128 + 128],
                                        HBF[ic][:, tc2 * 512:tc2 * 512 + 512],
                                        start=(ic == 0), stop=(ic == 3))
                    ti.then_inc(s_op, 1)
            if en == "vector":
                for oj in range(32):
                    ec, tc2 = oj // 4, oj % 4
                    oi = b * 32 + oj
                    eng.wait_ge(s_op, oi + 1)
                    r_att.produce_guard(eng, oi)
                    ps = P.P4[:, (oj % 4) * 512:(oj % 4) * 512 + 512]
                    r_att.filled_on(
                        eng.tensor_single_scalar(ATS[oi % 4], ps, P.oscale_c[:, ec:ec + 1],
                                                 ALU.mult), oi)
            if en == "gpsimd":
                for oj in range(32):
                    ec, tc2 = oj // 4, oj % 4
                    oi = b * 32 + oj
                    r_att.store_wait(eng, oi)
                    eng.dma_start(out=t[att][b * D + ec * 128:b * D + ec * 128 + 128,
                                             tc2 * 512:tc2 * 512 + 512],
                                  in_=ATS[oi % 4]).then_inc(r_att.st_sem(oi), 16)
        if en == "gpsimd":
            r_att.wait_all(eng, 128)
            eng.wait_ge(s_xst, 32 * 16)
    return emit



def ph_router0(P, xn32="XN32", gates="GATES0"):
    t = P.t
    rx = Ring(P, "r0x", 2)
    rg = StRing(P, "r0g", 2)
    s_wl = P.sem("r0_wl")
    s_mm = P.sem("r0_mm")
    s_mv = P.sem("r0_mv")
    s_ex = P.sem("r0_ex")

    def emit(en, eng):
        RTW = P.a_f32(P.GS, 0, 64).rearrange("p (dc e) -> p dc e", e=8)
        XT = [P.a_f32(P.A3, i * 1024, 1024).rearrange("p (dc u) -> p dc u", u=128)
              for i in range(2)]
        P8 = P.a_f32(P.A3, 2048, 8)
        E8 = P.a_f32(P.A3, 2056, 8)
        GT = [P.a_f32(P.A3, 2064 + i * 8, 8) for i in range(2)]
        nmax = P.cols[:, 0:1]
        msum = P.cols[:, 1:2]
        rs = P.cols[:, 2:3]
        m1 = P.cols[:, 3:4]
        if en == "sync":
            eng.dma_start(out=RTW, in_=t["router"].rearrange("(dc p) e -> p dc e", p=128)
                          ).then_inc(s_wl, 16)
            for i in range(64):
                b, tt = i // 16, i % 16
                if i >= 2:
                    eng.wait_ge(s_ex, i - 1)
                eng.dma_start(out=XT[i % 2],
                              in_=t[xn32][b * D:(b + 1) * D, tt * 128:tt * 128 + 128]
                              .rearrange("(dc p) u -> p dc u", p=128)
                              ).then_inc(rx.ld_sem(i), 16)
        if en == "tensor":
            eng.wait_ge(s_wl, 16)
            for i in range(64):
                rx.consume_wait(eng, i)
                if i > 0:
                    eng.wait_ge(s_ex, i)
                for dc in range(8):
                    ti = eng.matmul(P.PB4[:, 0:8], XT[i % 2][:, dc, :], RTW[:, dc, :],
                                    start=(dc == 0), stop=(dc == 7))
                ti.then_inc(s_mm, 1)
        if en == "vector":
            for i in range(64):
                eng.wait_ge(s_mm, i + 1)
                eng.reduce_max(nmax, P.PB4[:, 0:8], axis=AX.X, negate=True
                               ).then_inc(s_mv, 1)
                eng.wait_ge(s_ex, i + 1)
                eng.drain()
                eng.reciprocal(rs, msum)
                eng.drain()
                eng.tensor_single_scalar(P8, P8, rs, ALU.mult)
                eng.drain()
                eng.reduce_max(m1, P8, axis=AX.X)
                eng.drain()
                eng.tensor_single_scalar(E8, P8, m1, ALU.is_ge)
                eng.drain()
                eng.tensor_single_scalar(E8, E8, -2.0, ALU.mult)
                eng.drain()
                eng.tensor_tensor(E8, P8, E8, ALU.add)
                eng.drain()
                eng.reduce_max(m1, E8, axis=AX.X)
                eng.drain()
                eng.tensor_single_scalar(E8, P8, m1, ALU.is_ge)
                eng.drain()
                rg.produce_guard(eng, i)
                rg.filled_on(eng.tensor_tensor(GT[i % 2], P8, E8, ALU.mult), i)
        if en == "scalar":
            for i in range(64):
                eng.wait_ge(s_mm, i + 1)
                eng.wait_ge(s_mv, i + 1)
                eng.activation(P8, P.PB4[:, 0:8], AF.Exp, bias=nmax,
                               accum_out=msum).then_inc(s_ex, 1)
        if en == "gpsimd":
            for i in range(64):
                b, tt = i // 16, i % 16
                rg.store_wait(eng, i)
                eng.dma_start(out=t[gates][b * T + tt * 128:b * T + tt * 128 + 128, :],
                              in_=GT[i % 2]).then_inc(rg.st_sem(i), 16)
            rg.wait_all(eng, 64)
    return emit


def bankv(P, j):
    if j < 4:
        return P.P4[:, j * 512:(j + 1) * 512]
    return [P.PB4, P.PB5, P.PB6, P.PB7][j - 4][:]



def ph_ffn0(P, xnbf="XNBF", gates="GATES0", acc="ACC"):
    t = P.t
    s_upl = P.sem("f0_upl")
    s_upe = P.sem("f0_upe")
    s_um = P.sem("f0_um")
    s_gel = P.sem("f0_gel")
    s_dk = P.sem("f0_dk")
    s_ev = P.sem("f0_ev")
    rxn = Ring(P, "f0xn", 2)
    rga = Ring(P, "f0ga", 2)
    rac = Ring(P, "f0ac", 2)
    rdn = Ring(P, "f0dn", 2)
    rst = StRing(P, "f0st", 2)

    def emit(en, eng):
        UPW = P.A1[:].rearrange("p (dc f) -> p dc f", f=4096)
        XN = [P.a_bf(P.A2, i * 4096, 4096).rearrange("p (dc u) -> p dc u", u=512)
              for i in range(2)]
        GC = [P.a_f32(P.A2, 4096 + i * 32, 32) for i in range(2)]
        GC3 = [x.rearrange("p (q e) -> p q e", e=8) for x in GC]
        AC = [P.a_f32(P.A2, 4160 + i * 4096, 4096) for i in range(2)]
        AC3 = [x.rearrange("p (q u) -> p q u", u=1024) for x in AC]
        DN = [P.a_bf(P.A2, 24704 + i * 1024, 1024) for i in range(2)]
        GH = P.A3[:, 0:16384].rearrange("p (fc u) -> p fc u", u=512)

        for e in range(9):
            if en == "sync":
                if e > 0:
                    eng.wait_ge(s_upe, e)
                eng.dma_start(out=UPW, in_=t["UPbf"][e].rearrange("(dc p) f -> p dc f",
                                                                  p=128)
                              ).then_inc(s_upl, 16)
            for ci in range(16):
                gci = e * 16 + ci
                b, tc = ci // 4, ci % 4
                trow = b * T + tc * 512
                if en == "sync":
                    if gci >= 2:
                        eng.wait_ge(s_um, 32 * (gci - 1))
                    eng.dma_start(out=XN[gci % 2], in_=fmaj(t[xnbf], b, tc * 512, 512)
                                  ).then_inc(rxn.ld_sem(gci), 16)
                    if e > 0:
                        gj = gci - 16
                        if gj >= 2:
                            eng.wait_ge(s_ev, gci - 1)
                        eng.dma_start(out=GC3[gj % 2],
                                      in_=t[gates][trow:trow + 512, :]
                                      .rearrange("(q p) e -> p q e", p=128)
                                      ).then_inc(rga.ld_sem(gj), 16)
                        eng.wait_ge(rst.st[gci % 2], 16 * 4 * (gci // 2))
                        eng.dma_start(out=AC3[gj % 2],
                                      in_=t[acc][trow:trow + 512, :]
                                      .rearrange("(q p) u -> p q u", p=128)
                                      ).then_inc(rac.ld_sem(gj), 16)
                    for kc in range(32):
                        gdi = gci * 32 + kc
                        if gdi >= 2:
                            eng.wait_ge(s_dk, gdi - 1)
                        eng.dma_start(out=DN[gdi % 2],
                                      in_=t["DOWNbf"][e][kc * 128:kc * 128 + 128, :]
                                      ).then_inc(rdn.ld_sem(gdi), 16)
                if en == "tensor":
                    if ci == 0:
                        eng.wait_ge(s_upl, 16 * (e + 1))
                    rxn.consume_wait(eng, gci)
                    if gci > 0:
                        eng.wait_ge(s_ev, gci)
                    for fc in range(32):
                        g = gci * 32 + fc
                        ps = P.P4[:, (fc % 4) * 512:(fc % 4) * 512 + 512]
                        if fc >= 4:
                            eng.wait_ge(s_gel, g - 3)
                        for dc in range(8):
                            ti = eng.matmul(ps, UPW[:, dc, fc * 128:fc * 128 + 128],
                                            XN[gci % 2][:, dc, :],
                                            start=(dc == 0), stop=(dc == 7))
                        ti.then_inc(s_um, 1)
                    if ci == 15:
                        eng.wait_ge(s_um, 32 * (gci + 1))
                        eng.sem_inc(s_upe, 1)
                    eng.wait_ge(s_gel, 32 * (gci + 1))
                    for kc in range(32):
                        gdi = gci * 32 + kc
                        rdn.consume_wait(eng, gdi)
                        for tt in range(4):
                            for fh in range(2):
                                ti = eng.matmul(bankv(P, tt * 2 + fh),
                                                GH[:, kc, tt * 128:tt * 128 + 128],
                                                DN[gdi % 2][:, fh * 512:fh * 512 + 512],
                                                start=(kc == 0), stop=(kc == 31))
                        ti.then_inc(s_dk, 1)
                if en == "scalar":
                    for fc in range(32):
                        g = gci * 32 + fc
                        eng.wait_ge(s_um, g + 1)
                        ps = P.P4[:, (fc % 4) * 512:(fc % 4) * 512 + 512]
                        eng.activation(GH[:, fc, :], ps, AF.Gelu).then_inc(s_gel, 1)
                if en == "vector":
                    eng.wait_ge(s_dk, 32 * (gci + 1))
                    if e > 0:
                        rga.consume_wait(eng, gci - 16)
                        rac.consume_wait(eng, gci - 16)
                    if gci >= 2:
                        eng.wait_ge(rst.st[gci % 2], 16 * 4 * (gci // 2))
                    for tt in range(4):
                        for fh in range(2):
                            dst = AC3[gci % 2][:, tt, fh * 512:fh * 512 + 512]
                            bk = bankv(P, tt * 2 + fh)
                            if e == 0:
                                ti = eng.tensor_copy(dst, bk)
                            else:
                                gcol = GC[gci % 2][:, tt * 8 + e - 1:tt * 8 + e]
                                ti = eng.scalar_tensor_tensor(dst, bk, gcol, dst,
                                                              ALU.mult, ALU.add)
                    ti.then_inc(s_ev, 1)
                if en == "gpsimd":
                    eng.wait_ge(s_ev, gci + 1)
                    for tt in range(4):
                        ti = eng.dma_start(out=t[acc][trow + tt * 128:trow + tt * 128 + 128, :],
                                           in_=AC3[gci % 2][:, tt, :])
                        ti.then_inc(rst.st[gci % 2], 16)
        if en == "gpsimd":
            for j in range(2):
                eng.wait_ge(rst.st[j], 16 * 4 * 72)
    return emit



def ph_cmb0(P, xin="xin", xn="XN32", att="ATT", acc="ACC",
            x1="X1", xnout="XN32", xnbfout="XNBF"):
    t = P.t
    s_ld = P.sem("c0_ld")
    s_tr = P.sem("c0_tr")
    s_cv = P.sem("c0_cv")
    s_sq = P.sem("c0_sq")
    s_stt = P.sem("c0_stt")
    s_rv = P.sem("c0_rv")
    s_rs = P.sem("c0_rs")
    s_bc = P.sem("c0_bc")
    s_xn = P.sem("c0_xn")
    s_st = P.sem("c0_st")

    def emit(en, eng):
        C_X = P.a_f32(P.A2, 0, 4096)
        C_XN = P.a_f32(P.A2, 4096, 4096)
        C_AT = P.a_f32(P.A2, 8192, 4096)
        C_AC = P.a_f32(P.A1, 0, 4096)
        XSQ = P.a_f32(P.A1, 4096, 4096)
        XNF = P.a_f32(P.A1, 8192, 4096)
        XNB = P.a_bf(P.A1, 24576, 4096)
        C_X3 = v3(C_X, 512)
        C_XN3 = v3(C_XN, 512)
        C_AT3 = v3(C_AT, 512)
        C_AC3 = C_AC.rearrange("p (q u) -> p q u", u=1024)
        XSQ3 = v3(XSQ, 512)
        XNF3 = v3(XNF, 512)
        XNB3 = v3(XNB, 512)
        NRV, NRS = 3, 2
        for ci in range(16):
            b, tc = ci // 4, ci % 4
            c0 = tc * 512
            trow = b * T + c0
            rvb, rsb = NRV * ci, NRS * ci
            if en == "sync":
                if ci > 0:
                    eng.wait_ge(s_xn, ci)
                    eng.wait_ge(s_st, 48 * ci)
                eng.dma_start(out=C_X3, in_=fmaj(t[xin], b, c0, 512)).then_inc(s_ld, 16)
                eng.dma_start(out=C_XN3, in_=fmaj(t[xn], b, c0, 512)).then_inc(s_ld, 16)
                eng.dma_start(out=C_AT3, in_=fmaj(t[att], b, c0, 512)).then_inc(s_ld, 16)
                eng.dma_start(out=C_AC3, in_=t[acc][trow:trow + 512, :]
                              .rearrange("(q p) u -> p q u", p=128)).then_inc(s_ld, 16)
            if en == "tensor":
                eng.wait_ge(s_ld, 64 * (ci + 1))
                if ci > 0:
                    eng.wait_ge(s_xn, ci)
                for dc in range(8):
                    if ci * 8 + dc >= 4:
                        eng.wait_ge(s_cv, ci * 8 + dc - 3)
                    for tt in range(4):
                        ti = eng.transpose(
                            P.P4[:, (dc % 4) * 512 + tt * 128:(dc % 4) * 512 + tt * 128 + 128],
                            C_AC3[:, tt, dc * 128:dc * 128 + 128], P.ident_f[:])
                    ti.then_inc(s_tr, 1)
                eng.wait_ge(s_cv, ci * 8 + 8)
                for dc in range(8):
                    eng.matmul(P.PB4[0:1, :], P.onesc[:], C_X3[:, dc, :],
                               start=(dc == 0), stop=(dc == 7))
                eng.wait_ge(s_sq, ci + 1)
                for dc in range(8):
                    ti = eng.matmul(P.PB5[0:1, :], P.onesc[:], XSQ3[:, dc, :],
                                    start=(dc == 0), stop=(dc == 7))
                ti.then_inc(s_stt, 1)
                eng.wait_ge(s_rv, rvb + NRV)
                eng.wait_ge(s_rs, rsb + NRS)
                eng.matmul(P.PB6[:], P.onesr[:], P.row(0), start=True, stop=True)
                ti = eng.matmul(P.PB7[:], P.onesr[:], P.row(5), start=True, stop=True)
                ti.then_inc(s_bc, 1)
            if en == "vector":
                eng.wait_ge(s_ld, 64 * (ci + 1))
                for dc in range(8):
                    eng.wait_ge(s_tr, ci * 8 + dc + 1)
                    eng.scalar_tensor_tensor(C_X3[:, dc, :], C_XN3[:, dc, :], 2.0,
                                             C_X3[:, dc, :], ALU.mult, ALU.add)
                    eng.tensor_tensor(C_X3[:, dc, :], C_X3[:, dc, :], C_AT3[:, dc, :],
                                      ALU.add)
                    ti = eng.tensor_tensor(C_X3[:, dc, :], C_X3[:, dc, :],
                                           P.P4[:, (dc % 4) * 512:(dc % 4) * 512 + 512],
                                           ALU.add)
                    ti.then_inc(s_cv, 1)
                eng.wait_ge(s_stt, ci + 1)
                emit_ln_rows(P, en, eng, s_rv, s_rs, rvb, rsb, rms2=False)
                eng.wait_ge(s_bc, ci + 1)
                if ci > 0:
                    eng.wait_ge(s_st, 16 * 3 * ci)
                for dc in range(8):
                    eng.tensor_tensor(XNF3[:, dc, :], C_X3[:, dc, :], P.PB6[:], ALU.mult)
                    eng.tensor_tensor(XNF3[:, dc, :], XNF3[:, dc, :], P.PB7[:],
                                      ALU.subtract)
                    ti = eng.tensor_copy(XNB3[:, dc, :], XNF3[:, dc, :])
                ti.then_inc(s_xn, 1)
            if en == "scalar":
                eng.wait_ge(s_cv, ci * 8 + 8)
                eng.activation(XSQ, C_X, AF.Square).then_inc(s_sq, 1)
                emit_ln_rows(P, en, eng, s_rv, s_rs, rvb, rsb, rms2=False)
            if en == "gpsimd":
                eng.wait_ge(s_xn, ci + 1)
                eng.dma_start(out=fmaj(t[x1], b, c0, 512), in_=C_X3).then_inc(s_st, 16)
                eng.dma_start(out=fmaj(t[xnout], b, c0, 512), in_=XNF3).then_inc(s_st, 16)
                eng.dma_start(out=fmaj(t[xnbfout], b, c0, 512), in_=XNB3).then_inc(s_st, 16)
        if en == "gpsimd":
            eng.wait_ge(s_st, 16 * 3 * 16)
    return emit



def ph_rt1(P, xn32="XN32", gates="GATES1"):
    t = P.t
    rx = Ring(P, "r1x", 2)
    rg = StRing(P, "r1g", 2)
    s_wl = P.sem("r1_wl")
    s_mm = P.sem("r1_mm")
    s_sg = P.sem("r1_sg")

    def emit(en, eng):
        ESL = P.a_f32(P.A3, 0, 512).rearrange("p (dc e) -> p dc e", e=64)
        XT = [P.a_f32(P.A3, 512 + i * 1024, 1024).rearrange("p (dc u) -> p dc u", u=128)
              for i in range(2)]
        SEL = P.a_f32(P.A3, 2560, 64)
        SELW = P.a_f32(P.A3, 2624, 64)
        E64 = P.a_f32(P.A3, 2688, 64)
        GT = [P.a_f32(P.A3, 2752 + i * 64, 64) for i in range(2)]
        mj = P.cols[:, 0:1]
        if en == "sync":
            eng.dma_start(out=ESL, in_=t["esel_l"].rearrange("(dc p) e -> p dc e", p=128)
                          ).then_inc(s_wl, 16)
            for i in range(64):
                b, tt = i // 16, i % 16
                if i >= 2:
                    eng.wait_ge(s_sg, i - 1)
                eng.dma_start(out=XT[i % 2],
                              in_=t[xn32][b * D:(b + 1) * D, tt * 128:tt * 128 + 128]
                              .rearrange("(dc p) u -> p dc u", p=128)
                              ).then_inc(rx.ld_sem(i), 16)
        if en == "tensor":
            eng.wait_ge(s_wl, 16)
            for i in range(64):
                rx.consume_wait(eng, i)
                if i > 0:
                    eng.wait_ge(s_sg, i)
                for dc in range(8):
                    ti = eng.matmul(P.PB4[:, 0:64], XT[i % 2][:, dc, :], ESL[:, dc, :],
                                    start=(dc == 0), stop=(dc == 7))
                ti.then_inc(s_mm, 1)
        if en == "scalar":
            for i in range(64):
                eng.wait_ge(s_mm, i + 1)
                if i >= 1:
                    eng.wait_ge(rg.rd, i)
                eng.activation(SEL, P.PB4[:, 0:64], AF.Sigmoid).then_inc(s_sg, 1)
        if en == "vector":
            for i in range(64):
                eng.wait_ge(s_sg, i + 1)
                eng.tensor_copy(SELW, SEL)
                eng.drain()
                for j in range(3):
                    eng.reduce_max(mj, SELW, axis=AX.X)
                    eng.drain()
                    eng.tensor_single_scalar(E64, SELW, mj, ALU.is_ge)
                    eng.drain()
                    eng.tensor_single_scalar(E64, E64, -3.0, ALU.mult)
                    eng.drain()
                    eng.tensor_tensor(SELW, SELW, E64, ALU.add)
                    eng.drain()
                eng.reduce_max(mj, SELW, axis=AX.X)
                eng.drain()
                eng.tensor_single_scalar(E64, SEL, mj, ALU.is_ge)
                eng.drain()
                rg.produce_guard(eng, i)
                rg.filled_on(eng.tensor_tensor(GT[i % 2], SEL, E64, ALU.mult), i)
        if en == "gpsimd":
            for i in range(64):
                b, tt = i // 16, i % 16
                rg.store_wait(eng, i)
                eng.dma_start(out=t[gates][b * T + tt * 128:b * T + tt * 128 + 128, :],
                              in_=GT[i % 2]).then_inc(rg.st_sem(i), 16)
            rg.wait_all(eng, 64)
    return emit



def ph_attn1(P, xnbf="XNBF", att="ATT1"):
    t = P.t
    s_wql = P.sem("a1_wql")
    s_qm = P.sem("a1_qm")
    s_qe = P.sem("a1_qe")
    s_wol = P.sem("a1_wol")
    s_om = P.sem("a1_om")
    s_kt = P.sem("a1_kt")
    s_ke = P.sem("a1_ke")
    s_qt = P.sem("a1_qt")
    s_qs = P.sem("a1_qs")
    s_sc = P.sem("a1_sc")
    s_bi = P.sem("a1_bi")
    s_ex = P.sem("a1_ex")
    s_pn = P.sem("a1_pn")
    s_ptt = P.sem("a1_ptt")
    s_pte = P.sem("a1_pte")
    s_cx = P.sem("a1_cx")
    s_cf = P.sem("a1_cf")
    r_att = StRing(P, "a1_st", 2)
    rxq = Ring(P, "a1_xq", 2)
    s_rp = P.sem("a1_rp")

    def emit(en, eng):
        QT4 = P.A1[:, 0:16384].rearrange("p (tt h d) -> p tt h d", h=16, d=64)
        KT4 = P.A1[:, 16384:32768].rearrange("p (tt h d) -> p tt h d", h=16, d=64)
        VT3 = P.A2[:, 0:16384].rearrange("p (tt f) -> p tt f", f=1024)
        CF3 = P.A2[:, 16384:32768].rearrange("p (dc u) -> p dc u", u=2048)
        WQG = P.a_bf(P.A3, 0, 4096).rearrange("p (dc f) -> p dc f", f=512)
        WOU = P.a_bf(P.A3, 0, 8192).rearrange("p (dc f) -> p dc f", f=1024)
        XT1 = [P.a_bf(P.A3, 8192 + i * 1024, 1024).rearrange("p (dc u) -> p dc u", u=128)
               for i in range(2)]
        KH = P.a_bf(P.A3, 10240, 2048)
        QH = P.a_bf(P.A3, 12288, 128)
        PSB = P.a_bf(P.A3, 12416, 2048)
        PTS = P.a_bf(P.A3, 14464, 2048)
        ROPEC = P.a_f32(P.A3, 8256, 512)
        ROPES = P.a_f32(P.A3, 8768, 512)
        TMQ = P.a_f32(P.GS, 0, 256).rearrange("p (h d) -> p h d", d=32)
        TMQ2 = P.a_f32(P.GS, 256, 256).rearrange("p (h d) -> p h d", d=32)
        ATS = [P.a_f32(P.GS, 512 + i * 512, 512) for i in range(2)]
        PB6b = P.PB6[:].bitcast(BF16)
        PB7b = P.PB7[:].bitcast(BF16)
        m4 = P.cols[:, 4:8]
        m1 = P.cols[:, 8:9]
        bcol = P.cols[:, 9:10]
        scol = P.cols[:, 10:11]
        rcol = P.cols[:, 11:12]

        if en == "sync":
            eng.dma_start(out=ROPEC, in_=t["ropec"]).then_inc(s_rp, 16)
            eng.dma_start(out=ROPES, in_=t["ropes"]).then_inc(s_rp, 16)
        if en == "vector":
            eng.wait_ge(s_rp, 32)
        for b in range(BPC):
            for fg in range(6):
                qg = b * 6 + fg
                if en == "sync":
                    if qg > 0:
                        eng.wait_ge(s_qm, 16 * qg)
                    if b > 0:
                        eng.wait_ge(s_om, 32 * b)
                    eng.dma_start(out=WQG,
                                  in_=t["WQKVbf"][:, fg * 512:fg * 512 + 512]
                                  .rearrange("(dc p) f -> p dc f", p=128)
                                  ).then_inc(s_wql, 16)
                    for tt in range(16):
                        qi = qg * 16 + tt
                        if qi >= 2:
                            eng.wait_ge(s_qm, qi - 1)
                        eng.dma_start(out=XT1[qi % 2], in_=t[xnbf]
                                      [b * D:(b + 1) * D, tt * 128:tt * 128 + 128]
                                      .rearrange("(dc p) u -> p dc u", p=128)
                                      ).then_inc(rxq.ld_sem(qi), 16)
                if en == "tensor":
                    eng.wait_ge(s_wql, 16 * (qg + 1))
                    for tt in range(16):
                        qi = qg * 16 + tt
                        rxq.consume_wait(eng, qi)
                        if qi >= 4:
                            eng.wait_ge(s_qe, qi - 3)
                        ps = P.P4[:, (qi % 4) * 512:(qi % 4) * 512 + 512]
                        for dc in range(8):
                            ti = eng.matmul(ps, XT1[qi % 2][:, dc, :], WQG[:, dc, :],
                                            start=(dc == 0), stop=(dc == 7))
                        ti.then_inc(s_qm, 1)
                if en == "vector":
                    for tt in range(16):
                        qi = qg * 16 + tt
                        eng.wait_ge(s_qm, qi + 1)
                        ps = P.P4[:, (qi % 4) * 512:(qi % 4) * 512 + 512]
                        if fg < 4:
                            ps3 = ps.rearrange("p (h d) -> p h d", d=64)
                            x1 = ps3[:, :, 0:32]
                            x2 = ps3[:, :, 32:64]
                            Cb = ROPEC[:, tt * 32:tt * 32 + 32].unsqueeze(1) \
                                .broadcast_to([128, 8, 32])
                            Sb = ROPES[:, tt * 32:tt * 32 + 32].unsqueeze(1) \
                                .broadcast_to([128, 8, 32])
                            O4 = QT4 if fg < 2 else KT4
                            h0 = (fg % 2) * 8
                            eng.tensor_tensor(TMQ[:], x1, Cb, ALU.mult)
                            eng.tensor_tensor(TMQ2[:], x2, Sb, ALU.mult)
                            eng.drain()
                            eng.tensor_tensor(O4[:, tt, h0:h0 + 8, 0:32], TMQ[:], TMQ2[:],
                                              ALU.subtract)
                            eng.tensor_tensor(TMQ[:], x1, Sb, ALU.mult)
                            eng.tensor_tensor(TMQ2[:], x2, Cb, ALU.mult)
                            eng.drain()
                            ti = eng.tensor_tensor(O4[:, tt, h0:h0 + 8, 32:64], TMQ[:],
                                                   TMQ2[:], ALU.add)
                        else:
                            f0 = (fg - 4) * 512
                            ti = eng.tensor_copy(VT3[:, tt, f0:f0 + 512], ps)
                        ti.then_inc(s_qe, 1)
            for h in range(16):
                hb = b * 16 + h
                if en == "tensor":
                    if h == 0:
                        eng.wait_ge(s_qe, 96 * (b + 1))
                    for r in range(2):
                        kr = hb * 2 + r
                        if kr >= 1:
                            eng.wait_ge(s_ke, kr)
                        if hb >= 1:
                            eng.wait_ge(s_pte, 32 * hb)
                        for j in range(8):
                            ti = eng.transpose(PB6b[0:64, j * 128:j * 128 + 128],
                                               KT4[:, r * 8 + j, h, :], P.ident_b[:])
                        ti.then_inc(s_kt, 1)
                if en == "scalar":
                    for r in range(2):
                        kr = hb * 2 + r
                        eng.wait_ge(s_kt, kr + 1)
                        eng.copy(KH[0:64, r * 1024:r * 1024 + 1024],
                                 PB6b[0:64, 0:1024]).then_inc(s_ke, 1)
                for qt in range(16):
                    qq = hb * 16 + qt
                    if en == "tensor":
                        if qq >= 1:
                            eng.wait_ge(s_qs, qq)
                        ti = eng.transpose(PB7b[0:64, 0:128], QT4[:, qt, h, :],
                                           P.ident_b[:])
                        ti.then_inc(s_qt, 1)
                        eng.wait_ge(s_qs, qq + 1)
                        eng.wait_ge(s_ke, 2 * (hb + 1))
                        if qq >= 1:
                            eng.wait_ge(s_ex, qq)
                        for j in range(4):
                            ti = eng.matmul(P.P4[:, j * 512:j * 512 + 512], QH[0:64, :],
                                            KH[0:64, j * 512:j * 512 + 512],
                                            start=True, stop=True)
                        ti.then_inc(s_sc, 1)
                        eng.wait_ge(s_pn, qq + 1)
                        for r in range(2):
                            pr = qq * 2 + r
                            eng.wait_ge(s_pte, pr)
                            for j in range(8):
                                ti = eng.transpose(PB6b[:, j * 128:j * 128 + 128],
                                                   PSB[:, (r * 8 + j) * 128:(r * 8 + j) * 128 + 128],
                                                   P.ident_b[:])
                            ti.then_inc(s_ptt, 1)
                        eng.wait_ge(s_pte, qq * 2 + 2)
                        if qq >= 1:
                            eng.wait_ge(s_cf, qq)
                        for kc in range(16):
                            ti = eng.matmul(P.PB4[0:64, 0:128], VT3[:, kc, h * 64:h * 64 + 64],
                                            PTS[:, kc * 128:kc * 128 + 128],
                                            start=(kc == 0), stop=(kc == 15))
                        ti.then_inc(s_cx, 1)
                    if en == "scalar":
                        eng.wait_ge(s_qt, qq + 1)
                        eng.copy(QH[0:64, :], PB7b[0:64, 0:128]).then_inc(s_qs, 1)
                        eng.wait_ge(s_bi, qq + 1)
                        eng.activation(PSB[:], P.P4[:], AF.Exp, scale=0.125, bias=bcol,
                                       accum_out=scol).then_inc(s_ex, 1)
                        for r in range(2):
                            eng.wait_ge(s_ptt, qq * 2 + r + 1)
                            eng.copy(PTS[:, r * 1024:r * 1024 + 1024],
                                     PB6b[:, 0:1024]).then_inc(s_pte, 1)
                        eng.wait_ge(s_cx, qq + 1)
                        eng.copy(CF3[(h % 2) * 64:(h % 2) * 64 + 64, h // 2,
                                     qt * 128:qt * 128 + 128],
                                 P.PB4[0:64, 0:128]).then_inc(s_cf, 1)
                    if en == "vector":
                        eng.wait_ge(s_sc, qq + 1)
                        for j in range(4):
                            eng.reduce_max(m4[:, j:j + 1],
                                           P.P4[:, j * 512:j * 512 + 512], axis=AX.X)
                        eng.drain()
                        eng.reduce_max(m1, m4, axis=AX.X)
                        eng.drain()
                        eng.tensor_single_scalar(bcol, m1, -0.125, ALU.mult
                                                 ).then_inc(s_bi, 1)
                        eng.wait_ge(s_ex, qq + 1)
                        eng.drain()
                        eng.reciprocal(rcol, scol)
                        eng.drain()
                        eng.tensor_single_scalar(PSB[:], PSB[:], rcol, ALU.mult
                                                 ).then_inc(s_pn, 1)
            if en == "sync":
                eng.wait_ge(s_qm, 96 * (b + 1))
                eng.dma_start(out=WOU, in_=t["WOUTbf"].rearrange("(dc p) f -> p dc f",
                                                                 p=128)
                              ).then_inc(s_wol, 16)
            if en == "tensor":
                eng.wait_ge(s_wol, 16 * (b + 1))
                eng.wait_ge(s_cf, 256 * (b + 1))
                for oj in range(32):
                    ec, tc2 = oj // 4, oj % 4
                    oi = b * 32 + oj
                    ps = P.P4[:, (oj % 4) * 512:(oj % 4) * 512 + 512]
                    if oi >= 4:
                        eng.wait_ge(r_att.rd, oi - 3)
                    for dc in range(8):
                        ti = eng.matmul(ps, WOU[:, dc, ec * 128:ec * 128 + 128],
                                        CF3[:, dc, tc2 * 512:tc2 * 512 + 512],
                                        start=(dc == 0), stop=(dc == 7))
                    ti.then_inc(s_om, 1)
            if en == "vector":
                for oj in range(32):
                    ec, tc2 = oj // 4, oj % 4
                    oi = b * 32 + oj
                    eng.wait_ge(s_om, oi + 1)
                    r_att.produce_guard(eng, oi)
                    ps = P.P4[:, (oj % 4) * 512:(oj % 4) * 512 + 512]
                    r_att.filled_on(
                        eng.tensor_single_scalar(ATS[oi % 2], ps, P.bout_c[:, ec:ec + 1],
                                                 ALU.add), oi)
            if en == "gpsimd":
                for oj in range(32):
                    ec, tc2 = oj // 4, oj % 4
                    oi = b * 32 + oj
                    r_att.store_wait(eng, oi)
                    eng.dma_start(out=t[att][b * D + ec * 128:b * D + ec * 128 + 128,
                                             tc2 * 512:tc2 * 512 + 512],
                                  in_=ATS[oi % 2]).then_inc(r_att.st_sem(oi), 16)
        if en == "gpsimd":
            r_att.wait_all(eng, 128)
    return emit



def ph_ffn1(P, xnbf="XNBF", gates="GATES1", x1="X1", att="ATT1", out="out"):
    t = P.t
    if getattr(P, "recycle_pool", None):
        pool = list(P.recycle_pool)
        def S(_name):
            return pool.pop(0)
    else:
        def S(_name):
            return (P.sem(_name), 0)
    h_xt, b_xt = S("f1_xt")
    h_xg, b_xg = S("f1_xg")
    h_xc, b_xc = S("f1_xc")
    h_hm, b_hm = S("f1_hm")
    h_hr, b_hr = S("f1_hr")
    h_tt, b_tt = S("f1_tt")
    h_hg, b_hg = S("f1_hg")
    h_bm, b_bm = S("f1_bm")
    h_ev, b_ev = S("f1_ev")
    h_ost, b_ost = S("f1_ost")
    h_k0, b_k0 = S("f1_k0")
    h_k1, b_k1 = S("f1_k1")
    h_v0, b_v0 = S("f1_v0")
    h_v1, b_v1 = S("f1_v1")
    kld = [(h_k0, b_k0), (h_k1, b_k1)]
    vld = [(h_v0, b_v0), (h_v1, b_v1)]

    def emit(en, eng):
        HGT = P.A1[:]
        XT = [P.a_bf(P.A3, i * 1024, 1024).rearrange("p (dc u) -> p dc u", u=128)
              for i in range(4)]
        KS = [P.a_bf(P.A3, 4096 + i * 1024, 1024).rearrange("p (dc e) -> p dc e", e=128)
              for i in range(2)]
        VS = [P.a_bf(P.A3, 6144 + i * 1024, 1024) for i in range(2)]
        GC = P.a_f32(P.A2, 12288, 256)
        X1C = P.a_f32(P.A2, 0, 4096)
        ATC = P.a_f32(P.A2, 4096, 4096)
        OCH = P.a_f32(P.A2, 8192, 4096)
        X1C3 = v3(X1C, 512)
        ATC3 = v3(ATC, 512)
        OCH3 = v3(OCH, 512)
        HG = [P.a_bf(P.GS, i * 128, 128) for i in range(2)]
        PB6b = P.PB6[:].bitcast(BF16)

        for ci in range(16):
            b, tc = ci // 4, ci % 4
            trow = b * T + tc * 512
            if en == "sync":
                if ci > 0:
                    eng.wait_ge(h_ev, b_ev + ci)
                for tt in range(4):
                    eng.dma_start(out=XT[tt], in_=t[xnbf]
                                  [b * D:(b + 1) * D,
                                   tc * 512 + tt * 128:tc * 512 + tt * 128 + 128]
                                  .rearrange("(dc p) u -> p dc u", p=128)
                                  ).then_inc(h_xt, 16)
                eng.dma_start(out=GC.rearrange("p (q e) -> p q e", e=64),
                              in_=t[gates][trow:trow + 512, :]
                              .rearrange("(q p) e -> p q e", p=128)).then_inc(h_xg, 16)
                eng.dma_start(out=X1C3, in_=fmaj(t[x1], b, tc * 512, 512)).then_inc(h_xc, 16)
                eng.dma_start(out=ATC3, in_=fmaj(t[att], b, tc * 512, 512)).then_inc(h_xc, 16)
                for k in range(64):
                    kk = ci * 64 + k
                    if kk >= 2:
                        eng.wait_ge(h_hm, b_hm + 4 * kk - 4)
                    hs, bs = kld[kk % 2]
                    eng.dma_start(out=KS[kk % 2], in_=t["KEYSbf"][k]
                                  .rearrange("(dc p) e -> p dc e", p=128)
                                  ).then_inc(hs, 16)
                for k in range(64):
                    kk = ci * 64 + k
                    if kk >= 2:
                        eng.wait_ge(h_bm, b_bm + kk - 1)
                    hs, bs = vld[kk % 2]
                    eng.dma_start(out=VS[kk % 2], in_=t["VALbf"][k]).then_inc(hs, 16)
            if en == "tensor":
                if ci > 0:
                    eng.wait_ge(h_ev, b_ev + ci)
                eng.wait_ge(h_xt, b_xt + 16 * 4 * (ci + 1))
                for k in range(64):
                    kk = ci * 64 + k
                    hs, bs = kld[kk % 2]
                    eng.wait_ge(hs, bs + 16 * (kk // 2 + 1))
                    for tt in range(4):
                        idx = kk * 4 + tt
                        hb = [P.PB4, P.PB5][tt % 2]
                        if idx >= 2:
                            eng.wait_ge(h_hr, b_hr + idx - 1)
                        for dc in range(8):
                            ti = eng.matmul(hb[:, 0:128], XT[tt][:, dc, :],
                                            KS[kk % 2][:, dc, :],
                                            start=(dc == 0), stop=(dc == 7))
                        ti.then_inc(h_hm, 1)
                    for tt in range(4):
                        idx = kk * 4 + tt
                        eng.wait_ge(h_hr, b_hr + idx + 1)
                        if kk >= 1:
                            eng.wait_ge(h_hg, b_hg + kk)
                        ti = eng.transpose(PB6b[:, tt * 128:tt * 128 + 128], HG[tt % 2][:],
                                           P.ident_b[:])
                        ti.then_inc(h_tt, 1)
                eng.wait_ge(h_hg, b_hg + 64 * (ci + 1))
                for k in range(64):
                    kk = ci * 64 + k
                    hs, bs = vld[kk % 2]
                    eng.wait_ge(hs, bs + 16 * (kk // 2 + 1))
                    for vc in range(8):
                        ti = eng.matmul(bankv(P, vc), VS[kk % 2][:, vc * 128:vc * 128 + 128],
                                        HGT[:, k * 512:k * 512 + 512],
                                        start=(k == 0), stop=(k == 63))
                    ti.then_inc(h_bm, 1)
            if en == "scalar":
                eng.wait_ge(h_xg, b_xg + 16 * (ci + 1))
                for k in range(64):
                    kk = ci * 64 + k
                    for tt in range(4):
                        idx = kk * 4 + tt
                        eng.wait_ge(h_hm, b_hm + idx + 1)
                        if idx >= 2:
                            eng.wait_ge(h_tt, b_tt + idx - 1)
                        hb = [P.PB4, P.PB5][tt % 2]
                        gcol = GC[:, tt * 64 + k:tt * 64 + k + 1]
                        eng.activation(HG[tt % 2][:], hb[:, 0:128], AF.Relu,
                                       scale=gcol).then_inc(h_hr, 1)
                    eng.wait_ge(h_tt, b_tt + kk * 4 + 4)
                    eng.copy(HGT[:, k * 512:k * 512 + 512], PB6b[:, 0:512]
                             ).then_inc(h_hg, 1)
            if en == "vector":
                eng.wait_ge(h_bm, b_bm + 64 * (ci + 1))
                eng.wait_ge(h_xc, b_xc + 32 * (ci + 1))
                if ci > 0:
                    eng.wait_ge(h_ost, b_ost + 16 * ci)
                for dc in range(8):
                    eng.tensor_tensor(OCH3[:, dc, :], X1C3[:, dc, :], ATC3[:, dc, :],
                                      ALU.add)
                    ti = eng.tensor_tensor(OCH3[:, dc, :], OCH3[:, dc, :],
                                           bankv(P, dc), ALU.add)
                ti.then_inc(h_ev, 1)
            if en == "gpsimd":
                eng.wait_ge(h_ev, b_ev + ci + 1)
                eng.dma_start(out=fmaj(t[out], b, tc * 512, 512), in_=OCH3
                              ).then_inc(h_ost, 16)
        if en == "gpsimd":
            eng.wait_ge(h_ost, b_ost + 16 * 16)
    return emit



PARTS = {
    "l0a": {
        "ins": ["xin", "wm_l", "vcols", "wproj_l", "oscale"],
        "outs": ["XN32", "XNBF", "ATT"],
        "scratch": [],
        "phases": lambda P: [ph_init(P), ph_l0a(P)],
    },
    "ffn0": {
        "ins": ["XN32", "XNBF", "router", "shup", "shdn", "rup", "rdn"],
        "outs": ["ACC", "GATES0"],
        "scratch": ["UPbf", "DOWNbf"],
        "phases": lambda P: [ph_init(P), ph_precast(P, PRECAST_FFN0, "f0"),
                             ph_router0(P), ph_ffn0(P)],
    },
    "cmb0": {
        "ins": ["xin", "XN32", "ATT", "ACC"],
        "outs": ["X1", "XN132", "XN1BF"],
        "scratch": [],
        "phases": lambda P: [ph_init(P), ph_cmb0(P, xnout="XN132", xnbfout="XN1BF")],
    },
    "attn1": {
        "ins": ["XN1BF", "wqkv_l", "wout_l", "bout", "ropec", "ropes"],
        "outs": ["ATT1"],
        "scratch": ["WQKVbf", "WOUTbf"],
        "phases": lambda P: [ph_init(P), ph_precast(P, PRECAST_ATTN1, "a1"),
                             ph_attn1(P, xnbf="XN1BF")],
    },
    "ffn1": {
        "ins": ["XN132", "XN1BF", "X1", "ATT1", "esel_l", "keysw", "valsw"],
        "outs": ["out", "GATES1"],
        "scratch": ["KEYSbf", "VALbf"],
        "phases": lambda P: [ph_init(P), ph_precast(P, PRECAST_FFN1, "f1"),
                             ph_rt1(P, xn32="XN132", gates="GATES1"),
                             ph_ffn1(P, xnbf="XN1BF", gates="GATES1")],
    },
    "all": {
        "ins": ["xin", "wm_l", "vcols", "wproj_l", "oscale", "shup", "shdn", "rup",
                "rdn", "router", "wqkv_l", "wout_l", "bout", "esel_l", "keysw",
                "valsw", "ropec", "ropes"],
        "outs": ["out"],
        "scratch": ["XN32", "XNBF", "ATT", "ACC", "X1", "GATES0", "GATES1",
                    "UPbf", "DOWNbf", "WQKVbf", "WOUTbf", "KEYSbf", "VALbf"],
        "phases": lambda P: [
            ph_init(P),
            ph_precast(P, PRECAST_FFN0 + PRECAST_ATTN1 + PRECAST_FFN1, "all"),
            ph_l0a(P),
            ph_router0(P),
            ph_ffn0(P),
            ph_cmb0(P, xnout="XN32", xnbfout="XNBF"),
            ph_rt1(P, xn32="XN32", gates="GATES1"),
            ph_attn1(P, xnbf="XNBF", att="ATT"),
            ph_ffn1(P, xnbf="XNBF", gates="GATES1", att="ATT"),
        ],
    },
}


def build_part(part):
    spec = PARTS[part]
    P = Prog(spec["ins"], spec["outs"], spec["scratch"])
    for ph in spec["phases"](P):
        P.add(ph)
    return P.finalize(), spec


def run_part(part, in_map_per_core, core_ids=None):
    nc, spec = build_part(part)
    core_ids = core_ids or list(range(len(in_map_per_core)))
    res = run_bass_kernel_spmd(nc, in_map_per_core, core_ids=core_ids)
    return res


def kernel(**inputs):
    w = prep_shared(inputs)
    x = inputs["L0_attn_x"]
    nc, spec = build_part("all")
    in_maps = []
    for c in range(NCORES):
        m = {k: w[k] for k in spec["ins"] if k != "xin"}
        m["xin"] = prep_core_x(x, c)
        in_maps.append(m)
    res = run_bass_kernel_spmd(nc, in_maps, core_ids=list(range(NCORES)))
    out = np.empty((32, T, D), np.float32)
    for c in range(NCORES):
        out[c * BPC:(c + 1) * BPC] = np.asarray(res.results[c]["out"], np.float32) \
            .reshape(BPC, D, T).transpose(0, 2, 1)
    return out


# version 05
# speedup vs baseline: 1.2817x; 1.1081x over previous
import contextlib

import numpy as np

import concourse.bass as bass
import concourse.mybir as mybir
from concourse.bass_utils import run_bass_kernel_spmd

F32 = mybir.dt.float32
BF16 = mybir.dt.bfloat16
AF = mybir.ActivationFunctionType
ALU = mybir.AluOpType
AX = mybir.AxisListType

NCORES = 8
BPC = 4
T = 2048
D = 1024
NTOK = BPC * T
EPS_LN = 1e-5
EPS_RMS = 1.1920929e-07



def _cc(a):
    return np.ascontiguousarray(a)


def prep_shared(inputs):
    w = {}
    wm = np.asarray(inputs["L0_attn_W_m"], np.float32)
    wm_perm = wm.reshape(256, 4, 1024).transpose(1, 0, 2).reshape(1024, 1024)
    w["wm_l"] = _cc(wm_perm.T)
    v = np.asarray(inputs["L0_attn_v_vec"], np.float32)[:, :, 0]
    w["vcols"] = _cc(v.reshape(2, 128, 2).transpose(0, 2, 1).reshape(4, 128))
    wp = np.asarray(inputs["L0_attn_W_proj"], np.float32)
    w["wproj_l"] = _cc(wp.T.reshape(256, 2, 1024).transpose(1, 0, 2).reshape(512, 1024))
    w["oscale"] = _cc(np.asarray(inputs["L0_attn_out_scale"], np.float32).reshape(8, 128))
    w["shup"] = _cc(np.asarray(inputs["L0_ffn_shared_up"], np.float32)[0])
    w["shdn"] = _cc(np.asarray(inputs["L0_ffn_shared_down"], np.float32)[0])
    w["rup"] = _cc(np.asarray(inputs["L0_ffn_routed_up"], np.float32))
    w["rdn"] = _cc(np.asarray(inputs["L0_ffn_routed_down"], np.float32))
    w["router"] = _cc(np.asarray(inputs["L0_ffn_router_w"], np.float32))
    w["wqkv_l"] = _cc(np.asarray(inputs["L1_attn_w_qkv"], np.float32).T)
    w["wout_l"] = _cc(np.asarray(inputs["L1_attn_w_out"], np.float32).T)
    w["bout"] = _cc(np.asarray(inputs["L1_attn_b_out"], np.float32).reshape(8, 128))
    w["esel_l"] = _cc(np.asarray(inputs["L1_ffn_expert_sel"], np.float32).T)
    w["keysw"] = _cc(np.asarray(inputs["L1_ffn_keys"], np.float32))
    w["valsw"] = _cc(np.asarray(inputs["L1_ffn_values"], np.float32))
    hd = 64
    inv = (1.0 / (np.float32(10000.0) ** (np.arange(0, hd, 2, dtype=np.float32)
                                          / np.float32(hd)))).astype(np.float32)
    tt = np.arange(T, dtype=np.float32)
    fr = tt[:, None] * inv[None, :]
    w["ropec"] = _cc(np.cos(fr).astype(np.float32).reshape(16, 128, 32)
                     .transpose(1, 0, 2).reshape(128, 512))
    w["ropes"] = _cc(np.sin(fr).astype(np.float32).reshape(16, 128, 32)
                     .transpose(1, 0, 2).reshape(128, 512))
    return w


def prep_core_x(x, core):
    xs = np.asarray(x, np.float32)[core * BPC:(core + 1) * BPC]
    return _cc(xs.transpose(0, 2, 1).reshape(BPC * D, T))


DRAM_SPEC = {
    "xin":   ([BPC * D, T], F32),
    "wm_l":  ([1024, 1024], F32),
    "vcols": ([4, 128], F32),
    "wproj_l": ([512, 1024], F32),
    "oscale": ([8, 128], F32),
    "shup":  ([1024, 4096], F32),
    "shdn":  ([4096, 1024], F32),
    "rup":   ([8, 1024, 4096], F32),
    "rdn":   ([8, 4096, 1024], F32),
    "router": ([1024, 8], F32),
    "wqkv_l": ([1024, 3072], F32),
    "wout_l": ([1024, 1024], F32),
    "bout":  ([8, 128], F32),
    "esel_l": ([1024, 64], F32),
    "keysw": ([64, 1024, 128], F32),
    "valsw": ([64, 128, 1024], F32),
    "ropec": ([128, 512], F32),
    "ropes": ([128, 512], F32),
    "XN32":  ([BPC * D, T], F32),
    "XNBF":  ([BPC * D, T], BF16),
    "ATT":   ([BPC * D, T], F32),
    "ACC":   ([NTOK, D], F32),
    "X1":    ([BPC * D, T], F32),
    "GATES0": ([NTOK, 8], F32),
    "GATES1": ([NTOK, 64], F32),
    "UPbf":  ([9, 1024, 4096], BF16),
    "DOWNbf": ([9, 4096, 1024], BF16),
    "WQKVbf": ([1024, 3072], BF16),
    "WOUTbf": ([1024, 1024], BF16),
    "KEYSbf": ([64, 1024, 128], BF16),
    "VALbf": ([64, 128, 1024], BF16),
    "XN132": ([BPC * D, T], F32),
    "XN1BF": ([BPC * D, T], BF16),
    "ATT1":  ([BPC * D, T], F32),
    "out":   ([BPC * D, T], F32),
}

PRECAST_FFN0 = ([("shup", None, "UPbf", 0)] +
                [("rup", e, "UPbf", e + 1) for e in range(8)] +
                [("shdn", None, "DOWNbf", 0)] +
                [("rdn", e, "DOWNbf", e + 1) for e in range(8)])
PRECAST_ATTN1 = [("wqkv_l", None, "WQKVbf", None), ("wout_l", None, "WOUTbf", None)]
PRECAST_FFN1 = [("keysw", None, "KEYSbf", None), ("valsw", None, "VALbf", None)]


class Prog:
    def __init__(self, ins, outs, scratch):
        self.nc = bass.Bass("TRN2", target_bir_lowering=False, debug=False,
                            detect_race_conditions=False)
        self.ctx = contextlib.ExitStack()
        self.t = {}
        for name in ins:
            shape, dt = DRAM_SPEC[name]
            self.t[name] = self.nc.dram_tensor(name, shape, dt, kind="ExternalInput").ap()
        for name in outs:
            shape, dt = DRAM_SPEC[name]
            self.t[name] = self.nc.dram_tensor(name, shape, dt, kind="ExternalOutput").ap()
        for name in scratch:
            shape, dt = DRAM_SPEC[name]
            self.t[name] = self.nc.dram_tensor(name, shape, dt, kind="Internal").ap()
        c, nc = self.ctx, self.nc
        self.A1 = c.enter_context(nc.sbuf_tensor("A1", [128, 32768], BF16))
        self.A2 = c.enter_context(nc.sbuf_tensor("A2", [128, 32768], BF16))
        self.A3 = c.enter_context(nc.sbuf_tensor("A3", [128, 20480], BF16))
        self.GS = c.enter_context(nc.sbuf_tensor("GS", [128, 4096], BF16))
        self.ident_b = c.enter_context(nc.sbuf_tensor("ident_b", [128, 128], BF16))
        self.ident_f = c.enter_context(nc.sbuf_tensor("ident_f", [128, 128], F32))
        self.onesc = c.enter_context(nc.sbuf_tensor("onesc", [128, 1], F32))
        self.onesr = c.enter_context(nc.sbuf_tensor("onesr", [1, 128], F32))
        self.rows = c.enter_context(nc.sbuf_tensor("rows", [1, 3072], F32))
        self.cols = c.enter_context(nc.sbuf_tensor("cols", [128, 16], F32))
        self.oscale_c = c.enter_context(nc.sbuf_tensor("oscale_c", [128, 8], F32))
        self.bout_c = c.enter_context(nc.sbuf_tensor("bout_c", [128, 8], F32))
        self.vcols_c = c.enter_context(nc.sbuf_tensor("vcols_c", [128, 4], F32))
        self.wcar = c.enter_context(nc.sbuf_tensor("wcar", [128, 96], F32))
        self.P4 = c.enter_context(nc.psum_tensor("P4", [128, 2048], F32))
        self.P4b = c.enter_context(nc.psum_tensor("P4b", [128, 2048], F32))
        self.PB4 = self.P4b[:, 0:512]
        self.PB5 = self.P4b[:, 512:1024]
        self.PB6 = self.P4b[:, 1024:1536]
        self.PB7 = self.P4b[:, 1536:2048]
        self.sems = {}
        self.bar = self.sem("bar")
        self.phases = []

    def sem(self, name):
        h = self.ctx.enter_context(self.nc.semaphore(name))
        self.sems[name] = h
        return h

    def add(self, emitfn):
        self.phases.append(emitfn)

    def finalize(self):
        nc, phases, bar = self.nc, self.phases, self.bar
        with nc.Block() as block:
            def mk(en):
                def body(eng):
                    for k, ph in enumerate(phases):
                        if k > 0:
                            eng.wait_ge(bar, 5 * k)
                        ph(en, eng)
                        eng.drain()
                        eng.sem_inc(bar, 1)
                return body
            block.sync(mk("sync"))
            block.tensor(mk("tensor"))
            block.vector(mk("vector"))
            block.scalar(mk("scalar"))
            block.gpsimd(mk("gpsimd"))
        self.ctx.close()
        return self.nc

    def a_f32(self, arena, start_f32, n_f32):
        return arena[:].bitcast(F32)[:, start_f32:start_f32 + n_f32]

    def a_bf(self, arena, start_bf, n_bf):
        return arena[:, start_bf:start_bf + n_bf]

    def row(self, i):
        return self.rows[0:1, i * 512:(i + 1) * 512]


class Ring:

    def __init__(self, P, name, n):
        self.n = n
        self.ld = [P.sem(f"{name}_ld{i}") for i in range(n)]

    def ld_sem(self, i):
        return self.ld[i % self.n]

    def consume_wait(self, eng, i):
        eng.wait_ge(self.ld[i % self.n], 16 * (i // self.n + 1))


class StRing:

    def __init__(self, P, name, n):
        self.n = n
        self.st = [P.sem(f"{name}_st{i}") for i in range(n)]
        self.rd = P.sem(f"{name}_rd")

    def produce_guard(self, eng, i):
        if i >= self.n:
            eng.wait_ge(self.st[i % self.n], 16 * ((i - self.n) // self.n + 1))

    def filled(self, eng, i):
        eng.sem_inc(self.rd, 1)

    def filled_on(self, instr, i):
        instr.then_inc(self.rd, 1)

    def store_wait(self, eng, i):
        eng.wait_ge(self.rd, i + 1)

    def st_sem(self, i):
        return self.st[i % self.n]

    def wait_all(self, eng, total):
        for j in range(self.n):
            cnt = (total - j + self.n - 1) // self.n if total > j else 0
            if cnt:
                eng.wait_ge(self.st[j], 16 * cnt)


def fmaj(dram_ap, b, c0, w):
    return dram_ap[b * D:(b + 1) * D, c0:c0 + w].rearrange("(dc p) u -> p dc u", p=128)


def v3(ap2d, u):
    return ap2d.rearrange("p (dc u) -> p dc u", u=u)



def ph_init(P):
    t = P.t
    sld = P.sem("ini_ld")
    sgi = P.sem("ini_g")

    def emit(en, eng):
        if en == "gpsimd":
            eng.memset(P.ident_b[:], 0.0)
            eng.memset(P.ident_f[:], 0.0).then_inc(sgi, 1)
            eng.wait_ge(sgi, 1)
            for idn in (P.ident_b, P.ident_f):
                eng.affine_select(out=idn[:], in_=idn[:], compare_op=ALU.not_equal,
                                  fill=1.0, base=0, pattern=[[-1, 128]], channel_multiplier=1)
        if en == "vector":
            eng.memset(P.onesc[:], 1.0)
            eng.memset(P.onesr[:], 1.0)
        if en == "sync":
            n = 0
            for name, dst, k in (("oscale", P.oscale_c, 8), ("bout", P.bout_c, 8),
                                 ("vcols", P.vcols_c, 4)):
                if name in t:
                    for c in range(k):
                        eng.dma_start(out=dst[:, c:c + 1],
                                      in_=t[name][c, :].unsqueeze(1)).then_inc(sld, 16)
                    n += k
            eng.wait_ge(sld, 16 * n)
    return emit



def ph_precast(P, jobs, tag):
    t = P.t
    CH = 1048576
    plan = []
    for (src, si, d, di) in jobs:
        shape = DRAM_SPEC[src][0]
        n = int(np.prod(shape[1:]) if si is not None else np.prod(shape))
        assert n % CH == 0, (src, n)
        for c in range(n // CH):
            plan.append((src, si, d, di, c))
    rin = Ring(P, f"pc{tag}i", 2)
    rout = StRing(P, f"pc{tag}o", 2)

    def flat_view(name, idx, c):
        ap = t[name]
        if idx is not None:
            ap = ap[idx]
        if len(ap.shape) == 2:
            ap = ap.rearrange("a b -> (a b)")
        elif len(ap.shape) == 3:
            ap = ap.rearrange("a b c -> (a b c)")
        return ap.rearrange("(c p m) -> c p m", p=128, m=8192)[c]

    def emit(en, eng):
        nch = len(plan)
        for i, (src, si, d, di, c) in enumerate(plan):
            slot = i % 2
            stg = P.a_f32(P.A1, slot * 8192, 8192)
            obf = P.a_bf(P.A2, slot * 8192, 8192)
            if en == "sync":
                if i >= 2:
                    eng.wait_ge(rout.rd, i - 1)
                eng.dma_start(out=stg, in_=flat_view(src, si, c)).then_inc(rin.ld_sem(i), 16)
            if en == "vector":
                rin.consume_wait(eng, i)
                rout.produce_guard(eng, i)
                rout.filled_on(eng.tensor_copy(obf, stg), i)
            if en == "gpsimd":
                rout.store_wait(eng, i)
                eng.dma_start(out=flat_view(d, di, c), in_=obf).then_inc(rout.st_sem(i), 16)
        if en == "gpsimd":
            rout.wait_all(eng, nch)
    return emit



def emit_ln_rows(P, en, eng, s_rv, s_rs, rvb, rsb, rms2):
    r_mean, r_msq, r_t2, r_var, r_A, r_B = (P.row(i) for i in range(6))
    r_A2, r_B2 = r_mean, r_var
    r_tmp, r_ve = r_t2, r_msq
    if en == "vector":
        eng.tensor_single_scalar(r_mean, P.PB4[0:1, :], 1.0 / 1024.0, ALU.mult
                                 ).then_inc(s_rv, 1)
        eng.tensor_single_scalar(r_msq, P.PB5[0:1, :], 1.0 / 1024.0, ALU.mult)
        eng.wait_ge(s_rs, rsb + 1)
        eng.tensor_tensor(r_var, r_msq, r_tmp, ALU.subtract)
        eng.tensor_single_scalar(r_ve, r_var, EPS_LN, ALU.add).then_inc(s_rv, 1)
        eng.wait_ge(s_rs, rsb + 2)
        eng.wait_ge(s_rv, rvb + 1)
        eng.tensor_tensor(r_B, r_mean, r_A, ALU.mult)
        if rms2:
            eng.tensor_tensor(r_t2, r_A, r_A, ALU.mult)
            eng.tensor_tensor(r_t2, r_var, r_t2, ALU.mult)
            eng.tensor_single_scalar(r_t2, r_t2, EPS_RMS, ALU.add).then_inc(s_rv, 1)
            eng.wait_ge(s_rs, rsb + 3)
            eng.tensor_tensor(r_A2, r_A, r_t2, ALU.mult)
            eng.tensor_tensor(r_B2, r_B, r_t2, ALU.mult).then_inc(s_rv, 1)
        else:
            eng.tensor_copy(P.row(0), r_A).then_inc(s_rv, 1)
    if en == "scalar":
        eng.wait_ge(s_rv, rvb + 1)
        eng.square(r_tmp, r_mean).then_inc(s_rs, 1)
        eng.wait_ge(s_rv, rvb + 2)
        eng.activation(r_A, r_ve, AF.Ln)
        eng.activation(r_A, r_A, AF.Exp, scale=-0.5).then_inc(s_rs, 1)
        if rms2:
            eng.wait_ge(s_rv, rvb + 3)
            eng.activation(r_t2, r_t2, AF.Ln)
            eng.activation(r_t2, r_t2, AF.Exp, scale=-0.5).then_inc(s_rs, 1)
    return (4, 3) if rms2 else (3, 2)



def ph_l0a(P, xn32="XN32", xnbf="XNBF", att="ATT"):
    t = P.t
    s_wld = P.sem("la_wld")
    s_wcv = P.sem("la_wcv")
    s_xld = P.sem("la_xld")
    s_sq = P.sem("la_sq")
    s_stt = P.sem("la_stt")
    s_rv = P.sem("la_rv")
    s_rs = P.sem("la_rs")
    s_bc = P.sem("la_bc")
    s_xn = P.sem("la_xn")
    s_mm = P.sem("la_mm")
    s_me = P.sem("la_me")
    s_xst = P.sem("la_xst")
    s_nv = P.sem("la_nv")
    s_ns = P.sem("la_ns")
    s_hb = P.sem("la_hb")
    s_wp = P.sem("la_wp")
    s_wc2 = P.sem("la_wc2")
    s_op = P.sem("la_op")
    NRV, NRS = 4, 3
    r_att = StRing(P, "la_att", 4)
    P.recycle_pool = [(s_xld, 256), (s_sq, 16), (s_stt, 16), (s_rv, 64), (s_rs, 48),
                      (s_bc, 16), (s_xn, 16), (s_mm, 128), (s_me, 128), (s_xst, 512),
                      (s_nv, 8), (s_ns, 8), (s_hb, 4), (s_op, 128), (s_wld, 32),
                      (s_wp, 64), (s_wcv, 2), (s_wc2, 4)]

    def emit(en, eng):
        SA = [P.a_f32(P.A1, i * 2048, 2048) for i in range(8)]
        SB = [P.a_f32(P.A2, i * 2048, 2048) for i in range(8)]
        SA3 = [x.rearrange("p (c u) -> p c u", u=128) for x in SA]
        SB3 = [x.rearrange("p (c u) -> p c u", u=128) for x in SB]
        WM = P.a_bf(P.A3, 0, 8192)
        HBF = [P.a_bf(P.A3, 8192 + i * 2048, 2048) for i in range(4)]
        HBF3 = [x.rearrange("p (c u) -> p c u", u=128) for x in HBF]
        WPJ = P.a_bf(P.A3, 16384, 4096)
        TMP = P.a_f32(P.GS, 0, 2048)
        TMP3 = TMP.rearrange("p (c u) -> p c u", u=128)
        XCH = P.a_f32(P.A2, 0, 4096)
        XSQ = P.a_f32(P.A2, 4096, 4096)
        XNF = P.a_f32(P.A2, 8192, 4096)
        XNB = P.a_bf(P.A2, 24576, 4096)
        XN2B = P.a_bf(P.A2, 28672, 4096)
        WSTG = [P.a_f32(P.A2, 0, 4096), P.a_f32(P.A2, 4096, 4096)]
        ATS = [P.a_f32(P.A2, i * 512, 512) for i in range(4)]
        XCH3 = XCH.rearrange("p (dc u) -> p dc u", u=512)
        XSQ3 = XSQ.rearrange("p (dc u) -> p dc u", u=512)
        XNF3 = XNF.rearrange("p (dc u) -> p dc u", u=512)
        XNB3 = XNB.rearrange("p (dc u) -> p dc u", u=512)
        XN2B3 = XN2B.rearrange("p (dc u) -> p dc u", u=512)
        WPJ3 = WPJ.rearrange("p (ic f) -> p ic f", f=1024)

        if en == "sync":
            for half in range(2):
                eng.dma_start(out=v3(WSTG[half], 1024),
                              in_=t["wm_l"][half * 512:(half + 1) * 512, :]
                              .rearrange("(dc p) f -> p dc f", p=128)).then_inc(s_wld, 16)
        if en == "vector":
            eng.wait_ge(s_wld, 32)
            for half in range(2):
                eng.tensor_copy(P.a_bf(P.A3, half * 4096, 4096), WSTG[half]).then_inc(s_wcv, 1)

        for b in range(BPC):
            for tc in range(4):
                ci = b * 4 + tc
                c0 = tc * 512
                rvb, rsb = NRV * ci, NRS * ci
                if en == "sync":
                    if ci == 0:
                        eng.wait_ge(s_wcv, 2)
                    else:
                        eng.wait_ge(s_xn, ci)
                    if tc == 0 and b > 0:
                        r_att.wait_all(eng, 32 * b)
                    eng.dma_start(out=XCH3, in_=fmaj(t["xin"], b, c0, 512)
                                  ).then_inc(s_xld, 16)
                if en == "scalar":
                    eng.wait_ge(s_xld, 16 * (ci + 1))
                    eng.activation(XSQ, XCH, AF.Square).then_inc(s_sq, 1)
                    emit_ln_rows(P, en, eng, s_rv, s_rs, rvb, rsb, rms2=True)
                if en == "tensor":
                    eng.wait_ge(s_xld, 16 * (ci + 1))
                    if ci > 0:
                        eng.wait_ge(s_xn, ci)
                    if b > 0 and tc == 0:
                        eng.wait_ge(r_att.rd, 32 * b)
                    for dc in range(8):
                        eng.matmul(P.PB4[0:1, :], P.onesc[:], XCH3[:, dc, :],
                                   start=(dc == 0), stop=(dc == 7))
                    eng.wait_ge(s_sq, ci + 1)
                    for dc in range(8):
                        ti = eng.matmul(P.PB5[0:1, :], P.onesc[:], XSQ3[:, dc, :],
                                        start=(dc == 0), stop=(dc == 7))
                    ti.then_inc(s_stt, 1)
                    eng.wait_ge(s_rv, rvb + NRV)
                    eng.wait_ge(s_rs, rsb + NRS)
                    eng.matmul(P.PB4[:], P.onesr[:], P.row(4), start=True, stop=True)
                    eng.matmul(P.PB5[:], P.onesr[:], P.row(5), start=True, stop=True)
                    eng.matmul(P.PB6[:], P.onesr[:], P.row(0), start=True, stop=True)
                    ti = eng.matmul(P.PB7[:], P.onesr[:], P.row(3), start=True, stop=True)
                    ti.then_inc(s_bc, 1)
                    eng.wait_ge(s_xn, ci + 1)
                    for fc in range(8):
                        g = ci * 8 + fc
                        ps = P.P4[:, (fc % 4) * 512:(fc % 4) * 512 + 512]
                        if g >= 4:
                            eng.wait_ge(s_me, g - 3)
                        for dc in range(8):
                            ti = eng.matmul(
                                ps, WM[:, dc * 1024 + fc * 128:dc * 1024 + fc * 128 + 128],
                                XN2B3[:, dc, :], start=(dc == 0), stop=(dc == 7))
                        ti.then_inc(s_mm, 1)
                if en == "vector":
                    eng.wait_ge(s_stt, ci + 1)
                    emit_ln_rows(P, en, eng, s_rv, s_rs, rvb, rsb, rms2=True)
                    eng.wait_ge(s_bc, ci + 1)
                    if ci > 0:
                        eng.wait_ge(s_xst, 32 * ci)
                    for dc in range(8):
                        eng.tensor_tensor(XNF3[:, dc, :], XCH3[:, dc, :], P.PB4[:], ALU.mult)
                        eng.tensor_tensor(XNF3[:, dc, :], XNF3[:, dc, :], P.PB5[:],
                                          ALU.subtract)
                        eng.tensor_copy(XNB3[:, dc, :], XNF3[:, dc, :])
                        eng.tensor_tensor(XSQ3[:, dc, :], XCH3[:, dc, :], P.PB6[:], ALU.mult)
                        ti = eng.tensor_tensor(XN2B3[:, dc, :], XSQ3[:, dc, :], P.PB7[:],
                                               ALU.subtract)
                    ti.then_inc(s_xn, 1)
                    for fc in range(8):
                        g = ci * 8 + fc
                        eng.wait_ge(s_mm, g + 1)
                        ps = P.P4[:, (fc % 4) * 512:(fc % 4) * 512 + 512]
                        eng.tensor_copy(SA[fc][:, c0:c0 + 512], ps).then_inc(s_me, 1)
                if en == "gpsimd":
                    eng.wait_ge(s_xn, ci + 1)
                    eng.dma_start(out=fmaj(t[xn32], b, c0, 512), in_=XNF3).then_inc(s_xst, 16)
                    eng.dma_start(out=fmaj(t[xnbf], b, c0, 512), in_=XNB3).then_inc(s_xst, 16)

            if en == "vector":
                eng.wait_ge(s_me, (b * 4 + 4) * 8)
                for kh in range(2):
                    U = SB[7]
                    eng.tensor_tensor(TMP, SA[kh], SA[kh], ALU.mult)
                    for c in range(1, 4):
                        eng.tensor_tensor(U, SA[2 * c + kh], SA[2 * c + kh], ALU.mult)
                        eng.tensor_tensor(TMP, TMP, U, ALU.add)
                    eng.tensor_single_scalar(TMP, TMP, 1e-12, ALU.max).then_inc(s_nv, 1)
                    eng.wait_ge(s_ns, b * 2 + kh + 1)
                    for c in range(4):
                        eng.tensor_tensor(SA[2 * c + kh], SA[2 * c + kh], TMP, ALU.mult)
            if en == "scalar":
                for kh in range(2):
                    eng.wait_ge(s_nv, b * 2 + kh + 1)
                    eng.activation(TMP, TMP, AF.Ln)
                    eng.activation(TMP, TMP, AF.Exp, scale=-0.5).then_inc(s_ns, 1)

            if en == "vector":
                cur3, nxt3 = SA3, SB3
                for step in range(7):
                    s = 1 << step
                    for kh in range(2):
                        for i in range(2):
                            for j in range(2):
                                ot = nxt3[2 * (i * 2 + j) + kh]
                                a0 = cur3[2 * (i * 2 + 0) + kh]
                                a1 = cur3[2 * (i * 2 + 1) + kh]
                                b0 = cur3[2 * (0 * 2 + j) + kh]
                                b1 = cur3[2 * (1 * 2 + j) + kh]
                                eng.tensor_tensor(ot[:, :, s:], a0[:, :, s:],
                                                  b0[:, :, :128 - s], ALU.mult)
                                eng.tensor_tensor(TMP3[:, :, s:], a1[:, :, s:],
                                                  b1[:, :, :128 - s], ALU.mult)
                                eng.tensor_tensor(ot[:, :, s:], ot[:, :, s:],
                                                  TMP3[:, :, s:], ALU.add)
                        for cc in range(4):
                            ti = 2 * cc + kh
                            eng.tensor_copy(nxt3[ti][:, :, :s], cur3[ti][:, :, :s])
                    cur3, nxt3 = nxt3, cur3
                SL = cur3
                eng.drain()
                for kh in range(2):
                    for j in range(2):
                        eng.tensor_copy(P.wcar[:, kh * 2 + j:kh * 2 + j + 1],
                                        P.vcols_c[:, kh * 2 + j:kh * 2 + j + 1])
                eng.drain()
                for c in range(16):
                    for kh in range(2):
                        w0 = P.wcar[:, kh * 2 + 0:kh * 2 + 1]
                        w1 = P.wcar[:, kh * 2 + 1:kh * 2 + 2]
                        sv0 = P.wcar[:, 4 + c * 4 + kh * 2:4 + c * 4 + kh * 2 + 1]
                        sv1 = P.wcar[:, 5 + c * 4 + kh * 2:5 + c * 4 + kh * 2 + 1]
                        eng.tensor_copy(sv0, w0)
                        eng.tensor_copy(sv1, w1)
                        if c < 15:
                            tt0 = P.wcar[:, 68:69]
                            tt1 = P.wcar[:, 69:70]
                            tt2 = P.wcar[:, 70:71]
                            tt3 = P.wcar[:, 71:72]
                            Pm = [SL[2 * cc + kh][:, c, 127:128] for cc in range(4)]
                            eng.drain()
                            eng.tensor_tensor(tt0, Pm[0], sv0, ALU.mult)
                            eng.tensor_tensor(tt1, Pm[1], sv1, ALU.mult)
                            eng.tensor_tensor(tt2, Pm[2], sv0, ALU.mult)
                            eng.tensor_tensor(tt3, Pm[3], sv1, ALU.mult)
                            eng.drain()
                            eng.tensor_tensor(w0, tt0, tt1, ALU.add)
                            eng.tensor_tensor(w1, tt2, tt3, ALU.add)
                            eng.drain()
                eng.drain()
                for c in range(16):
                    for kh in range(2):
                        sv0 = P.wcar[:, 4 + c * 4 + kh * 2:4 + c * 4 + kh * 2 + 1]
                        sv1 = P.wcar[:, 5 + c * 4 + kh * 2:5 + c * 4 + kh * 2 + 1]
                        for i in range(2):
                            tmpd = TMP3[:, c, :]
                            eng.tensor_single_scalar(
                                tmpd, SL[2 * (i * 2 + 0) + kh][:, c, :], sv0, ALU.mult)
                            eng.drain()
                            ti = eng.scalar_tensor_tensor(
                                HBF3[i * 2 + kh][:, c, :], SL[2 * (i * 2 + 1) + kh][:, c, :],
                                sv1, tmpd, ALU.mult, ALU.add)
                ti.then_inc(s_hb, 1)

            if en == "sync":
                eng.wait_ge(s_hb, b + 1)
                eng.dma_start(out=v3(WSTG[0], 1024),
                              in_=t["wproj_l"].rearrange("(ic p) f -> p ic f", p=128)
                              ).then_inc(s_wp, 16)
            if en == "vector":
                eng.wait_ge(s_wp, 16 * (b + 1))
                eng.tensor_copy(WPJ, WSTG[0]).then_inc(s_wc2, 1)
            if en == "tensor":
                eng.wait_ge(s_wc2, b + 1)
                for oj in range(32):
                    ec, tc2 = oj // 4, oj % 4
                    oi = b * 32 + oj
                    ps = P.P4[:, (oj % 4) * 512:(oj % 4) * 512 + 512]
                    if oi >= 4:
                        eng.wait_ge(r_att.rd, oi - 3)
                    for ic in range(4):
                        ti = eng.matmul(ps, WPJ3[:, ic, ec * 128:ec * 128 + 128],
                                        HBF[ic][:, tc2 * 512:tc2 * 512 + 512],
                                        start=(ic == 0), stop=(ic == 3))
                    ti.then_inc(s_op, 1)
            if en == "vector":
                for oj in range(32):
                    ec, tc2 = oj // 4, oj % 4
                    oi = b * 32 + oj
                    eng.wait_ge(s_op, oi + 1)
                    r_att.produce_guard(eng, oi)
                    ps = P.P4[:, (oj % 4) * 512:(oj % 4) * 512 + 512]
                    r_att.filled_on(
                        eng.tensor_single_scalar(ATS[oi % 4], ps, P.oscale_c[:, ec:ec + 1],
                                                 ALU.mult), oi)
            if en == "gpsimd":
                for oj in range(32):
                    ec, tc2 = oj // 4, oj % 4
                    oi = b * 32 + oj
                    r_att.store_wait(eng, oi)
                    eng.dma_start(out=t[att][b * D + ec * 128:b * D + ec * 128 + 128,
                                             tc2 * 512:tc2 * 512 + 512],
                                  in_=ATS[oi % 4]).then_inc(r_att.st_sem(oi), 16)
        if en == "gpsimd":
            r_att.wait_all(eng, 128)
            eng.wait_ge(s_xst, 32 * 16)
    return emit



def ph_router0(P, xn32="XN32", gates="GATES0"):
    t = P.t
    rx = Ring(P, "r0x", 2)
    rg = StRing(P, "r0g", 2)
    s_wl = P.sem("r0_wl")
    s_mm = P.sem("r0_mm")
    s_mv = P.sem("r0_mv")
    s_ex = P.sem("r0_ex")

    def emit(en, eng):
        RTW = P.a_f32(P.GS, 0, 64).rearrange("p (dc e) -> p dc e", e=8)
        XT = [P.a_f32(P.A3, i * 1024, 1024).rearrange("p (dc u) -> p dc u", u=128)
              for i in range(2)]
        P8 = P.a_f32(P.A3, 2048, 8)
        E8 = P.a_f32(P.A3, 2056, 8)
        GT = [P.a_f32(P.A3, 2064 + i * 8, 8) for i in range(2)]
        nmax = P.cols[:, 0:1]
        msum = P.cols[:, 1:2]
        rs = P.cols[:, 2:3]
        m1 = P.cols[:, 3:4]
        if en == "sync":
            eng.dma_start(out=RTW, in_=t["router"].rearrange("(dc p) e -> p dc e", p=128)
                          ).then_inc(s_wl, 16)
            for i in range(64):
                b, tt = i // 16, i % 16
                if i >= 2:
                    eng.wait_ge(s_ex, i - 1)
                eng.dma_start(out=XT[i % 2],
                              in_=t[xn32][b * D:(b + 1) * D, tt * 128:tt * 128 + 128]
                              .rearrange("(dc p) u -> p dc u", p=128)
                              ).then_inc(rx.ld_sem(i), 16)
        if en == "tensor":
            eng.wait_ge(s_wl, 16)
            for i in range(64):
                rx.consume_wait(eng, i)
                if i > 0:
                    eng.wait_ge(s_ex, i)
                for dc in range(8):
                    ti = eng.matmul(P.PB4[:, 0:8], XT[i % 2][:, dc, :], RTW[:, dc, :],
                                    start=(dc == 0), stop=(dc == 7))
                ti.then_inc(s_mm, 1)
        if en == "vector":
            for i in range(64):
                eng.wait_ge(s_mm, i + 1)
                eng.reduce_max(nmax, P.PB4[:, 0:8], axis=AX.X, negate=True
                               ).then_inc(s_mv, 1)
                eng.wait_ge(s_ex, i + 1)
                eng.drain()
                eng.reciprocal(rs, msum)
                eng.drain()
                eng.tensor_single_scalar(P8, P8, rs, ALU.mult)
                eng.drain()
                eng.reduce_max(m1, P8, axis=AX.X)
                eng.drain()
                eng.tensor_single_scalar(E8, P8, m1, ALU.is_ge)
                eng.drain()
                eng.tensor_single_scalar(E8, E8, -2.0, ALU.mult)
                eng.drain()
                eng.tensor_tensor(E8, P8, E8, ALU.add)
                eng.drain()
                eng.reduce_max(m1, E8, axis=AX.X)
                eng.drain()
                eng.tensor_single_scalar(E8, P8, m1, ALU.is_ge)
                eng.drain()
                rg.produce_guard(eng, i)
                rg.filled_on(eng.tensor_tensor(GT[i % 2], P8, E8, ALU.mult), i)
        if en == "scalar":
            for i in range(64):
                eng.wait_ge(s_mm, i + 1)
                eng.wait_ge(s_mv, i + 1)
                eng.activation(P8, P.PB4[:, 0:8], AF.Exp, bias=nmax,
                               accum_out=msum).then_inc(s_ex, 1)
        if en == "gpsimd":
            for i in range(64):
                b, tt = i // 16, i % 16
                rg.store_wait(eng, i)
                eng.dma_start(out=t[gates][b * T + tt * 128:b * T + tt * 128 + 128, :],
                              in_=GT[i % 2]).then_inc(rg.st_sem(i), 16)
            rg.wait_all(eng, 64)
    return emit


def bankv(P, j):
    if j < 4:
        return P.P4[:, j * 512:(j + 1) * 512]
    return [P.PB4, P.PB5, P.PB6, P.PB7][j - 4][:]



def ph_ffn0(P, xnbf="XNBF", gates="GATES0", acc="ACC"):
    t = P.t
    s_upl = P.sem("f0_upl")
    s_upe = P.sem("f0_upe")
    s_um = P.sem("f0_um")
    s_gel = P.sem("f0_gel")
    s_dk = P.sem("f0_dk")
    s_ev = P.sem("f0_ev")
    s_ev4 = P.sem("f0_ev4")
    rxn = Ring(P, "f0xn", 2)
    rga = Ring(P, "f0ga", 2)
    rac = Ring(P, "f0ac", 2)
    rdn = Ring(P, "f0dn", 2)
    rst = StRing(P, "f0st", 2)

    def emit(en, eng):
        UPW = P.A1[:].rearrange("p (dc f) -> p dc f", f=4096)
        XN = [P.a_bf(P.A2, i * 4096, 4096).rearrange("p (dc u) -> p dc u", u=512)
              for i in range(2)]
        GC = [P.a_f32(P.A2, 4096 + i * 32, 32) for i in range(2)]
        GC3 = [x.rearrange("p (q e) -> p q e", e=8) for x in GC]
        AC = [P.a_f32(P.A2, 4160 + i * 4096, 4096) for i in range(2)]
        AC3 = [x.rearrange("p (q u) -> p q u", u=1024) for x in AC]
        DN = [P.a_bf(P.A2, 24704 + i * 1024, 1024) for i in range(2)]
        GH = P.A3[:, 0:16384].rearrange("p (fc u) -> p fc u", u=512)

        for e in range(9):
            if en == "sync":
                if e > 0:
                    eng.wait_ge(s_upe, e)
                eng.dma_start(out=UPW, in_=t["UPbf"][e].rearrange("(dc p) f -> p dc f",
                                                                  p=128)
                              ).then_inc(s_upl, 16)
            for ci in range(16):
                gci = e * 16 + ci
                b, tc = ci // 4, ci % 4
                trow = b * T + tc * 512
                if en == "sync":
                    if gci >= 2:
                        eng.wait_ge(s_um, 32 * (gci - 1))
                    eng.dma_start(out=XN[gci % 2], in_=fmaj(t[xnbf], b, tc * 512, 512)
                                  ).then_inc(rxn.ld_sem(gci), 16)
                    if e > 0:
                        gj = gci - 16
                        if gj >= 2:
                            eng.wait_ge(s_ev, gci - 1)
                        eng.dma_start(out=GC3[gj % 2],
                                      in_=t[gates][trow:trow + 512, :]
                                      .rearrange("(q p) e -> p q e", p=128)
                                      ).then_inc(rga.ld_sem(gj), 16)
                        eng.wait_ge(rst.st[gci % 2], 16 * 4 * (gci // 2))
                        eng.dma_start(out=AC3[gj % 2],
                                      in_=t[acc][trow:trow + 512, :]
                                      .rearrange("(q p) u -> p q u", p=128)
                                      ).then_inc(rac.ld_sem(gj), 16)
                    for kc in range(32):
                        gdi = gci * 32 + kc
                        if gdi >= 2:
                            eng.wait_ge(s_dk, gdi - 1)
                        eng.dma_start(out=DN[gdi % 2],
                                      in_=t["DOWNbf"][e][kc * 128:kc * 128 + 128, :]
                                      ).then_inc(rdn.ld_sem(gdi), 16)
                if en == "tensor":
                    if ci == 0:
                        eng.wait_ge(s_upl, 16 * (e + 1))
                    rxn.consume_wait(eng, gci)
                    if gci > 0:
                        eng.wait_ge(s_ev4, gci)
                    for fc in range(32):
                        g = gci * 32 + fc
                        ps = P.P4[:, (fc % 4) * 512:(fc % 4) * 512 + 512]
                        if fc >= 4:
                            eng.wait_ge(s_gel, g - 3)
                        for dc in range(8):
                            ti = eng.matmul(ps, UPW[:, dc, fc * 128:fc * 128 + 128],
                                            XN[gci % 2][:, dc, :],
                                            start=(dc == 0), stop=(dc == 7))
                        ti.then_inc(s_um, 1)
                    if ci == 15:
                        eng.wait_ge(s_um, 32 * (gci + 1))
                        eng.sem_inc(s_upe, 1)
                    if gci > 0:
                        eng.wait_ge(s_ev, gci)
                    eng.wait_ge(s_gel, 32 * (gci + 1))
                    for kc in range(32):
                        gdi = gci * 32 + kc
                        rdn.consume_wait(eng, gdi)
                        for tt in range(4):
                            for fh in range(2):
                                ti = eng.matmul(bankv(P, tt * 2 + fh),
                                                GH[:, kc, tt * 128:tt * 128 + 128],
                                                DN[gdi % 2][:, fh * 512:fh * 512 + 512],
                                                start=(kc == 0), stop=(kc == 31))
                        ti.then_inc(s_dk, 1)
                if en == "scalar":
                    for fc in range(32):
                        g = gci * 32 + fc
                        eng.wait_ge(s_um, g + 1)
                        ps = P.P4[:, (fc % 4) * 512:(fc % 4) * 512 + 512]
                        eng.activation(GH[:, fc, :], ps, AF.Gelu).then_inc(s_gel, 1)
                if en == "vector":
                    eng.wait_ge(s_dk, 32 * (gci + 1))
                    if e > 0:
                        rga.consume_wait(eng, gci - 16)
                        rac.consume_wait(eng, gci - 16)
                    if gci >= 2:
                        eng.wait_ge(rst.st[gci % 2], 16 * 4 * (gci // 2))
                    for tt in range(4):
                        for fh in range(2):
                            dst = AC3[gci % 2][:, tt, fh * 512:fh * 512 + 512]
                            bk = bankv(P, tt * 2 + fh)
                            if e == 0:
                                ti = eng.tensor_copy(dst, bk)
                            else:
                                gcol = GC[gci % 2][:, tt * 8 + e - 1:tt * 8 + e]
                                ti = eng.scalar_tensor_tensor(dst, bk, gcol, dst,
                                                              ALU.mult, ALU.add)
                        if tt == 1:
                            ti.then_inc(s_ev4, 1)
                    ti.then_inc(s_ev, 1)
                if en == "gpsimd":
                    eng.wait_ge(s_ev, gci + 1)
                    for tt in range(4):
                        ti = eng.dma_start(out=t[acc][trow + tt * 128:trow + tt * 128 + 128, :],
                                           in_=AC3[gci % 2][:, tt, :])
                        ti.then_inc(rst.st[gci % 2], 16)
        if en == "gpsimd":
            for j in range(2):
                eng.wait_ge(rst.st[j], 16 * 4 * 72)
    return emit



def ph_cmb0(P, xin="xin", xn="XN32", att="ATT", acc="ACC",
            x1="X1", xnout="XN32", xnbfout="XNBF"):
    t = P.t
    s_ld = P.sem("c0_ld")
    s_tr = P.sem("c0_tr")
    s_cv = P.sem("c0_cv")
    s_sq = P.sem("c0_sq")
    s_stt = P.sem("c0_stt")
    s_rv = P.sem("c0_rv")
    s_rs = P.sem("c0_rs")
    s_bc = P.sem("c0_bc")
    s_xn = P.sem("c0_xn")
    s_st = P.sem("c0_st")

    def emit(en, eng):
        C_X = P.a_f32(P.A2, 0, 4096)
        C_XN = P.a_f32(P.A2, 4096, 4096)
        C_AT = P.a_f32(P.A2, 8192, 4096)
        C_AC = P.a_f32(P.A1, 0, 4096)
        XSQ = P.a_f32(P.A1, 4096, 4096)
        XNF = P.a_f32(P.A1, 8192, 4096)
        XNB = P.a_bf(P.A1, 24576, 4096)
        C_X3 = v3(C_X, 512)
        C_XN3 = v3(C_XN, 512)
        C_AT3 = v3(C_AT, 512)
        C_AC3 = C_AC.rearrange("p (q u) -> p q u", u=1024)
        XSQ3 = v3(XSQ, 512)
        XNF3 = v3(XNF, 512)
        XNB3 = v3(XNB, 512)
        NRV, NRS = 3, 2
        for ci in range(16):
            b, tc = ci // 4, ci % 4
            c0 = tc * 512
            trow = b * T + c0
            rvb, rsb = NRV * ci, NRS * ci
            if en == "sync":
                if ci > 0:
                    eng.wait_ge(s_xn, ci)
                    eng.wait_ge(s_st, 48 * ci)
                eng.dma_start(out=C_X3, in_=fmaj(t[xin], b, c0, 512)).then_inc(s_ld, 16)
                eng.dma_start(out=C_XN3, in_=fmaj(t[xn], b, c0, 512)).then_inc(s_ld, 16)
                eng.dma_start(out=C_AT3, in_=fmaj(t[att], b, c0, 512)).then_inc(s_ld, 16)
                eng.dma_start(out=C_AC3, in_=t[acc][trow:trow + 512, :]
                              .rearrange("(q p) u -> p q u", p=128)).then_inc(s_ld, 16)
            if en == "tensor":
                eng.wait_ge(s_ld, 64 * (ci + 1))
                if ci > 0:
                    eng.wait_ge(s_xn, ci)
                for dc in range(8):
                    if ci * 8 + dc >= 4:
                        eng.wait_ge(s_cv, ci * 8 + dc - 3)
                    for tt in range(4):
                        ti = eng.transpose(
                            P.P4[:, (dc % 4) * 512 + tt * 128:(dc % 4) * 512 + tt * 128 + 128],
                            C_AC3[:, tt, dc * 128:dc * 128 + 128], P.ident_f[:])
                    ti.then_inc(s_tr, 1)
                eng.wait_ge(s_cv, ci * 8 + 8)
                for dc in range(8):
                    eng.matmul(P.PB4[0:1, :], P.onesc[:], C_X3[:, dc, :],
                               start=(dc == 0), stop=(dc == 7))
                eng.wait_ge(s_sq, ci + 1)
                for dc in range(8):
                    ti = eng.matmul(P.PB5[0:1, :], P.onesc[:], XSQ3[:, dc, :],
                                    start=(dc == 0), stop=(dc == 7))
                ti.then_inc(s_stt, 1)
                eng.wait_ge(s_rv, rvb + NRV)
                eng.wait_ge(s_rs, rsb + NRS)
                eng.matmul(P.PB6[:], P.onesr[:], P.row(0), start=True, stop=True)
                ti = eng.matmul(P.PB7[:], P.onesr[:], P.row(5), start=True, stop=True)
                ti.then_inc(s_bc, 1)
            if en == "vector":
                eng.wait_ge(s_ld, 64 * (ci + 1))
                for dc in range(8):
                    eng.wait_ge(s_tr, ci * 8 + dc + 1)
                    eng.scalar_tensor_tensor(C_X3[:, dc, :], C_XN3[:, dc, :], 2.0,
                                             C_X3[:, dc, :], ALU.mult, ALU.add)
                    eng.tensor_tensor(C_X3[:, dc, :], C_X3[:, dc, :], C_AT3[:, dc, :],
                                      ALU.add)
                    ti = eng.tensor_tensor(C_X3[:, dc, :], C_X3[:, dc, :],
                                           P.P4[:, (dc % 4) * 512:(dc % 4) * 512 + 512],
                                           ALU.add)
                    ti.then_inc(s_cv, 1)
                eng.wait_ge(s_stt, ci + 1)
                emit_ln_rows(P, en, eng, s_rv, s_rs, rvb, rsb, rms2=False)
                eng.wait_ge(s_bc, ci + 1)
                if ci > 0:
                    eng.wait_ge(s_st, 16 * 3 * ci)
                for dc in range(8):
                    eng.tensor_tensor(XNF3[:, dc, :], C_X3[:, dc, :], P.PB6[:], ALU.mult)
                    eng.tensor_tensor(XNF3[:, dc, :], XNF3[:, dc, :], P.PB7[:],
                                      ALU.subtract)
                    ti = eng.tensor_copy(XNB3[:, dc, :], XNF3[:, dc, :])
                ti.then_inc(s_xn, 1)
            if en == "scalar":
                eng.wait_ge(s_cv, ci * 8 + 8)
                eng.activation(XSQ, C_X, AF.Square).then_inc(s_sq, 1)
                emit_ln_rows(P, en, eng, s_rv, s_rs, rvb, rsb, rms2=False)
            if en == "gpsimd":
                eng.wait_ge(s_xn, ci + 1)
                eng.dma_start(out=fmaj(t[x1], b, c0, 512), in_=C_X3).then_inc(s_st, 16)
                eng.dma_start(out=fmaj(t[xnout], b, c0, 512), in_=XNF3).then_inc(s_st, 16)
                eng.dma_start(out=fmaj(t[xnbfout], b, c0, 512), in_=XNB3).then_inc(s_st, 16)
        if en == "gpsimd":
            eng.wait_ge(s_st, 16 * 3 * 16)
    return emit



def ph_rt1(P, xn32="XN32", gates="GATES1"):
    t = P.t
    rx = Ring(P, "r1x", 2)
    rg = StRing(P, "r1g", 2)
    s_wl = P.sem("r1_wl")
    s_mm = P.sem("r1_mm")
    s_sg = P.sem("r1_sg")

    def emit(en, eng):
        ESL = P.a_f32(P.A3, 0, 512).rearrange("p (dc e) -> p dc e", e=64)
        XT = [P.a_f32(P.A3, 512 + i * 1024, 1024).rearrange("p (dc u) -> p dc u", u=128)
              for i in range(2)]
        SEL = P.a_f32(P.A3, 2560, 64)
        SELW = P.a_f32(P.A3, 2624, 64)
        E64 = P.a_f32(P.A3, 2688, 64)
        GT = [P.a_f32(P.A3, 2752 + i * 64, 64) for i in range(2)]
        mj = P.cols[:, 0:1]
        if en == "sync":
            eng.dma_start(out=ESL, in_=t["esel_l"].rearrange("(dc p) e -> p dc e", p=128)
                          ).then_inc(s_wl, 16)
            for i in range(64):
                b, tt = i // 16, i % 16
                if i >= 2:
                    eng.wait_ge(s_sg, i - 1)
                eng.dma_start(out=XT[i % 2],
                              in_=t[xn32][b * D:(b + 1) * D, tt * 128:tt * 128 + 128]
                              .rearrange("(dc p) u -> p dc u", p=128)
                              ).then_inc(rx.ld_sem(i), 16)
        if en == "tensor":
            eng.wait_ge(s_wl, 16)
            for i in range(64):
                rx.consume_wait(eng, i)
                if i > 0:
                    eng.wait_ge(s_sg, i)
                for dc in range(8):
                    ti = eng.matmul(P.PB4[:, 0:64], XT[i % 2][:, dc, :], ESL[:, dc, :],
                                    start=(dc == 0), stop=(dc == 7))
                ti.then_inc(s_mm, 1)
        if en == "scalar":
            for i in range(64):
                eng.wait_ge(s_mm, i + 1)
                if i >= 1:
                    eng.wait_ge(rg.rd, i)
                eng.activation(SEL, P.PB4[:, 0:64], AF.Sigmoid).then_inc(s_sg, 1)
        if en == "vector":
            for i in range(64):
                eng.wait_ge(s_sg, i + 1)
                eng.tensor_copy(SELW, SEL)
                eng.drain()
                for j in range(3):
                    eng.reduce_max(mj, SELW, axis=AX.X)
                    eng.drain()
                    eng.tensor_single_scalar(E64, SELW, mj, ALU.is_ge)
                    eng.drain()
                    eng.tensor_single_scalar(E64, E64, -3.0, ALU.mult)
                    eng.drain()
                    eng.tensor_tensor(SELW, SELW, E64, ALU.add)
                    eng.drain()
                eng.reduce_max(mj, SELW, axis=AX.X)
                eng.drain()
                eng.tensor_single_scalar(E64, SEL, mj, ALU.is_ge)
                eng.drain()
                rg.produce_guard(eng, i)
                rg.filled_on(eng.tensor_tensor(GT[i % 2], SEL, E64, ALU.mult), i)
        if en == "gpsimd":
            for i in range(64):
                b, tt = i // 16, i % 16
                rg.store_wait(eng, i)
                eng.dma_start(out=t[gates][b * T + tt * 128:b * T + tt * 128 + 128, :],
                              in_=GT[i % 2]).then_inc(rg.st_sem(i), 16)
            rg.wait_all(eng, 64)
    return emit



def ph_attn1(P, xnbf="XNBF", att="ATT1"):
    t = P.t
    s_wql = P.sem("a1_wql")
    s_qm = P.sem("a1_qm")
    s_qe = P.sem("a1_qe")
    s_wol = P.sem("a1_wol")
    s_om = P.sem("a1_om")
    s_kt = P.sem("a1_kt")
    s_ke = P.sem("a1_ke")
    s_qt = P.sem("a1_qt")
    s_qs = P.sem("a1_qs")
    s_sc = P.sem("a1_sc")
    s_ex = P.sem("a1_ex")
    s_pn = P.sem("a1_pn")
    s_ptt = P.sem("a1_ptt")
    s_pte = P.sem("a1_pte")
    s_cx = P.sem("a1_cx")
    s_cf = P.sem("a1_cf")
    r_att = StRing(P, "a1_st", 2)
    rxq = Ring(P, "a1_xq", 2)
    s_rp = P.sem("a1_rp")

    def emit(en, eng):
        QT4 = P.A1[:, 0:16384].rearrange("p (tt h d) -> p tt h d", h=16, d=64)
        KT4 = P.A1[:, 16384:32768].rearrange("p (tt h d) -> p tt h d", h=16, d=64)
        VT3 = P.A2[:, 0:16384].rearrange("p (tt f) -> p tt f", f=1024)
        CF3 = P.A2[:, 16384:32768].rearrange("p (dc u) -> p dc u", u=2048)
        WQG = P.a_bf(P.A3, 0, 4096).rearrange("p (dc f) -> p dc f", f=512)
        WOU = P.a_bf(P.A3, 0, 8192).rearrange("p (dc f) -> p dc f", f=1024)
        XT1 = [P.a_bf(P.A3, 8192 + i * 1024, 1024).rearrange("p (dc u) -> p dc u", u=128)
               for i in range(2)]
        KH = P.a_bf(P.A3, 10240, 2048)
        QH = P.a_bf(P.A3, 12288, 128)
        PSB = [P.a_bf(P.A3, 12416, 2048), P.a_bf(P.GS, 1024, 2048)]
        PTS = P.a_bf(P.A3, 14464, 2048)
        ROPEC = P.a_f32(P.A3, 8256, 512)
        ROPES = P.a_f32(P.A3, 8768, 512)
        TMQ = P.a_f32(P.GS, 0, 256).rearrange("p (h d) -> p h d", d=32)
        TMQ2 = P.a_f32(P.GS, 256, 256).rearrange("p (h d) -> p h d", d=32)
        ATS = [P.a_f32(P.A1, i * 512, 512) for i in range(2)]
        REG = [P.P4[:], P.P4b[:]]
        REGb = [r.bitcast(BF16) for r in REG]
        sc4 = P.cols[:, 4:8]
        scol = [P.cols[:, 8:9], P.cols[:, 9:10]]
        rcol = [P.cols[:, 10:11], P.cols[:, 11:12]]

        if en == "sync":
            eng.dma_start(out=ROPEC, in_=t["ropec"]).then_inc(s_rp, 16)
            eng.dma_start(out=ROPES, in_=t["ropes"]).then_inc(s_rp, 16)
        if en == "vector":
            eng.wait_ge(s_rp, 32)
        for b in range(BPC):
            for fg in range(6):
                qg = b * 6 + fg
                if en == "sync":
                    if qg > 0:
                        eng.wait_ge(s_qm, 16 * qg)
                    if b > 0:
                        eng.wait_ge(s_om, 32 * b)
                    eng.dma_start(out=WQG,
                                  in_=t["WQKVbf"][:, fg * 512:fg * 512 + 512]
                                  .rearrange("(dc p) f -> p dc f", p=128)
                                  ).then_inc(s_wql, 16)
                    for tt in range(16):
                        qi = qg * 16 + tt
                        if qi >= 2:
                            eng.wait_ge(s_qm, qi - 1)
                        eng.dma_start(out=XT1[qi % 2], in_=t[xnbf]
                                      [b * D:(b + 1) * D, tt * 128:tt * 128 + 128]
                                      .rearrange("(dc p) u -> p dc u", p=128)
                                      ).then_inc(rxq.ld_sem(qi), 16)
                if en == "tensor":
                    eng.wait_ge(s_wql, 16 * (qg + 1))
                    for tt in range(16):
                        qi = qg * 16 + tt
                        rxq.consume_wait(eng, qi)
                        if qi >= 4:
                            eng.wait_ge(s_qe, qi - 3)
                        ps = P.P4[:, (qi % 4) * 512:(qi % 4) * 512 + 512]
                        for dc in range(8):
                            ti = eng.matmul(ps, XT1[qi % 2][:, dc, :], WQG[:, dc, :],
                                            start=(dc == 0), stop=(dc == 7))
                        ti.then_inc(s_qm, 1)
                if en == "vector":
                    if fg == 0 and b > 0:
                        r_att.wait_all(eng, 32 * b)
                    for tt in range(16):
                        qi = qg * 16 + tt
                        eng.wait_ge(s_qm, qi + 1)
                        ps = P.P4[:, (qi % 4) * 512:(qi % 4) * 512 + 512]
                        if fg < 4:
                            ps3 = ps.rearrange("p (h d) -> p h d", d=64)
                            x1 = ps3[:, :, 0:32]
                            x2 = ps3[:, :, 32:64]
                            Cb = ROPEC[:, tt * 32:tt * 32 + 32].unsqueeze(1) \
                                .broadcast_to([128, 8, 32])
                            Sb = ROPES[:, tt * 32:tt * 32 + 32].unsqueeze(1) \
                                .broadcast_to([128, 8, 32])
                            O4 = QT4 if fg < 2 else KT4
                            h0 = (fg % 2) * 8
                            eng.tensor_tensor(TMQ[:], x1, Cb, ALU.mult)
                            eng.tensor_tensor(TMQ2[:], x2, Sb, ALU.mult)
                            eng.drain()
                            eng.tensor_tensor(O4[:, tt, h0:h0 + 8, 0:32], TMQ[:], TMQ2[:],
                                              ALU.subtract)
                            eng.tensor_tensor(TMQ[:], x1, Sb, ALU.mult)
                            eng.tensor_tensor(TMQ2[:], x2, Cb, ALU.mult)
                            eng.drain()
                            ti = eng.tensor_tensor(O4[:, tt, h0:h0 + 8, 32:64], TMQ[:],
                                                   TMQ2[:], ALU.add)
                        else:
                            f0 = (fg - 4) * 512
                            ti = eng.tensor_copy(VT3[:, tt, f0:f0 + 512], ps)
                        ti.then_inc(s_qe, 1)
            for h in range(16):
                hb = b * 16 + h

                def pt_ctx_backlog(qt1, eng):
                    qq1 = hb * 16 + qt1
                    p1 = qt1 % 2
                    eng.wait_ge(s_pn, qq1 + 1)
                    for r in range(2):
                        for j in range(8):
                            kc = r * 8 + j
                            ti = eng.transpose(
                                REGb[p1][:, r * 1024 + j * 128:r * 1024 + j * 128 + 128],
                                PSB[p1][:, kc * 128:kc * 128 + 128], P.ident_b[:])
                        ti.then_inc(s_ptt, 1)
                    eng.wait_ge(s_pte, 2 * qq1 + 2)
                    for kc in range(16):
                        ti = eng.matmul(REG[p1][0:64, 1024:1152],
                                        VT3[:, kc, h * 64:h * 64 + 64],
                                        PTS[:, kc * 128:kc * 128 + 128],
                                        start=(kc == 0), stop=(kc == 15))
                    ti.then_inc(s_cx, 1)

                if en == "tensor":
                    if h == 0:
                        eng.wait_ge(s_qe, 96 * (b + 1))
                    qq0 = hb * 16
                    if hb > 0:
                        eng.wait_ge(s_ex, qq0)
                        eng.wait_ge(s_pte, 2 * qq0)
                        eng.wait_ge(s_cf, qq0)
                    for r in range(2):
                        kr = hb * 2 + r
                        if kr >= 1:
                            eng.wait_ge(s_ke, kr)
                        for j in range(8):
                            ti = eng.transpose(REGb[0][0:64, j * 128:j * 128 + 128],
                                               KT4[:, r * 8 + j, h, :], P.ident_b[:])
                        ti.then_inc(s_kt, 1)
                if en == "scalar":
                    for r in range(2):
                        kr = hb * 2 + r
                        eng.wait_ge(s_kt, kr + 1)
                        eng.copy(KH[0:64, r * 1024:r * 1024 + 1024],
                                 REGb[0][0:64, 0:1024]).then_inc(s_ke, 1)
                for qt in range(16):
                    qq = hb * 16 + qt
                    p = qt % 2
                    if en == "tensor":
                        if qq >= 2:
                            eng.wait_ge(s_ex, qq - 1)
                            eng.wait_ge(s_pte, 2 * qq - 2)
                            eng.wait_ge(s_cf, qq - 1)
                        eng.wait_ge(s_ke, 2 * (hb + 1))
                        ti = eng.transpose(REGb[p][0:64, 0:128], QT4[:, qt, h, :],
                                           P.ident_b[:])
                        ti.then_inc(s_qt, 1)
                        eng.wait_ge(s_qs, qq + 1)
                        for j in range(4):
                            ti = eng.matmul(REG[p][:, j * 512:j * 512 + 512], QH[0:64, :],
                                            KH[0:64, j * 512:j * 512 + 512],
                                            start=True, stop=True)
                        ti.then_inc(s_sc, 1)
                        if qt >= 1:
                            pt_ctx_backlog(qt - 1, eng)
                    if en == "scalar":
                        eng.wait_ge(s_qt, qq + 1)
                        eng.copy(QH[0:64, :], REGb[p][0:64, 0:128]).then_inc(s_qs, 1)
                        eng.wait_ge(s_sc, qq + 1)
                        if p == 0:
                            eng.activation(PSB[0][:], REG[0], AF.Exp, scale=0.125,
                                           accum_out=scol[0]).then_inc(s_ex, 1)
                        else:
                            for j in range(4):
                                ti = eng.activation(
                                    PSB[1][:, j * 512:j * 512 + 512],
                                    REG[1][:, j * 512:j * 512 + 512], AF.Exp,
                                    scale=0.125, accum_out=sc4[:, j:j + 1])
                            ti.then_inc(s_ex, 1)
                        if qt >= 1:
                            eng.wait_ge(s_ptt, 2 * (qq - 1) + 1)
                            eng.copy(PTS[:, 0:1024],
                                     REGb[(qt - 1) % 2][:, 0:1024]).then_inc(s_pte, 1)
                    if en == "vector":
                        eng.wait_ge(s_ex, qq + 1)
                        eng.drain()
                        if p == 0:
                            eng.reciprocal(rcol[0], scol[0])
                        else:
                            eng.reduce_sum(scol[1], sc4, axis=AX.X)
                            eng.drain()
                            eng.reciprocal(rcol[1], scol[1])
                        eng.drain()
                        eng.tensor_single_scalar(PSB[p][:], PSB[p][:], rcol[p], ALU.mult
                                                 ).then_inc(s_pn, 1)
                        if qt >= 1:
                            eng.wait_ge(s_ptt, 2 * (qq - 1) + 2)
                            eng.tensor_copy(PTS[:, 1024:2048],
                                            REGb[(qt - 1) % 2][:, 1024:2048]
                                            ).then_inc(s_pte, 1)
                            eng.wait_ge(s_cx, qq)
                            eng.tensor_copy(CF3[(h % 2) * 64:(h % 2) * 64 + 64, h // 2,
                                                (qt - 1) * 128:(qt - 1) * 128 + 128],
                                            REG[(qt - 1) % 2][0:64, 1024:1152]
                                            ).then_inc(s_cf, 1)
                qq = hb * 16 + 15
                if en == "tensor":
                    pt_ctx_backlog(15, eng)
                if en == "scalar":
                    eng.wait_ge(s_ptt, 2 * qq + 1)
                    eng.copy(PTS[:, 0:1024], REGb[15 % 2][:, 0:1024]).then_inc(s_pte, 1)
                if en == "vector":
                    eng.wait_ge(s_ptt, 2 * qq + 2)
                    eng.tensor_copy(PTS[:, 1024:2048], REGb[15 % 2][:, 1024:2048]
                                    ).then_inc(s_pte, 1)
                    eng.wait_ge(s_cx, qq + 1)
                    eng.tensor_copy(CF3[(h % 2) * 64:(h % 2) * 64 + 64, h // 2,
                                        15 * 128:15 * 128 + 128],
                                    REG[15 % 2][0:64, 1024:1152]).then_inc(s_cf, 1)
            if en == "sync":
                eng.wait_ge(s_qm, 96 * (b + 1))
                eng.dma_start(out=WOU, in_=t["WOUTbf"].rearrange("(dc p) f -> p dc f",
                                                                 p=128)
                              ).then_inc(s_wol, 16)
            if en == "tensor":
                eng.wait_ge(s_wol, 16 * (b + 1))
                eng.wait_ge(s_cf, 256 * (b + 1))
                for oj in range(32):
                    ec, tc2 = oj // 4, oj % 4
                    oi = b * 32 + oj
                    ps = P.P4[:, (oj % 4) * 512:(oj % 4) * 512 + 512]
                    if oi >= 4:
                        eng.wait_ge(r_att.rd, oi - 3)
                    for dc in range(8):
                        ti = eng.matmul(ps, WOU[:, dc, ec * 128:ec * 128 + 128],
                                        CF3[:, dc, tc2 * 512:tc2 * 512 + 512],
                                        start=(dc == 0), stop=(dc == 7))
                    ti.then_inc(s_om, 1)
            if en == "vector":
                for oj in range(32):
                    ec, tc2 = oj // 4, oj % 4
                    oi = b * 32 + oj
                    eng.wait_ge(s_om, oi + 1)
                    r_att.produce_guard(eng, oi)
                    ps = P.P4[:, (oj % 4) * 512:(oj % 4) * 512 + 512]
                    r_att.filled_on(
                        eng.tensor_single_scalar(ATS[oi % 2], ps, P.bout_c[:, ec:ec + 1],
                                                 ALU.add), oi)
            if en == "gpsimd":
                for oj in range(32):
                    ec, tc2 = oj // 4, oj % 4
                    oi = b * 32 + oj
                    r_att.store_wait(eng, oi)
                    eng.dma_start(out=t[att][b * D + ec * 128:b * D + ec * 128 + 128,
                                             tc2 * 512:tc2 * 512 + 512],
                                  in_=ATS[oi % 2]).then_inc(r_att.st_sem(oi), 16)
        if en == "gpsimd":
            r_att.wait_all(eng, 128)
    return emit



def ph_ffn1(P, xnbf="XNBF", gates="GATES1", x1="X1", att="ATT1", out="out"):
    t = P.t
    if getattr(P, "recycle_pool", None):
        pool = list(P.recycle_pool)
        def S(_name):
            return pool.pop(0)
    else:
        def S(_name):
            return (P.sem(_name), 0)
    h_xt, b_xt = S("f1_xt")
    h_xg, b_xg = S("f1_xg")
    h_xc, b_xc = S("f1_xc")
    h_hm, b_hm = S("f1_hm")
    h_hr, b_hr = S("f1_hr")
    h_tt, b_tt = S("f1_tt")
    h_hg, b_hg = S("f1_hg")
    h_bm, b_bm = S("f1_bm")
    h_ev, b_ev = S("f1_ev")
    h_ost, b_ost = S("f1_ost")
    h_k0, b_k0 = S("f1_k0")
    h_k1, b_k1 = S("f1_k1")
    h_v0, b_v0 = S("f1_v0")
    h_v1, b_v1 = S("f1_v1")
    kld = [(h_k0, b_k0), (h_k1, b_k1)]
    vld = [(h_v0, b_v0), (h_v1, b_v1)]

    def emit(en, eng):
        HGT = P.A1[:]
        XT = [P.a_bf(P.A3, i * 1024, 1024).rearrange("p (dc u) -> p dc u", u=128)
              for i in range(4)]
        KS = [P.a_bf(P.A3, 4096 + i * 1024, 1024).rearrange("p (dc e) -> p dc e", e=128)
              for i in range(2)]
        VS = [P.a_bf(P.A3, 6144 + i * 1024, 1024) for i in range(2)]
        GC = P.a_f32(P.A2, 12288, 256)
        X1C = P.a_f32(P.A2, 0, 4096)
        ATC = P.a_f32(P.A2, 4096, 4096)
        OCH = P.a_f32(P.A2, 8192, 4096)
        X1C3 = v3(X1C, 512)
        ATC3 = v3(ATC, 512)
        OCH3 = v3(OCH, 512)
        HG = [P.a_bf(P.GS, i * 128, 128) for i in range(2)]
        PB6b = P.PB6[:].bitcast(BF16)

        for ci in range(16):
            b, tc = ci // 4, ci % 4
            trow = b * T + tc * 512
            if en == "sync":
                if ci > 0:
                    eng.wait_ge(h_ev, b_ev + ci)
                for tt in range(4):
                    eng.dma_start(out=XT[tt], in_=t[xnbf]
                                  [b * D:(b + 1) * D,
                                   tc * 512 + tt * 128:tc * 512 + tt * 128 + 128]
                                  .rearrange("(dc p) u -> p dc u", p=128)
                                  ).then_inc(h_xt, 16)
                eng.dma_start(out=GC.rearrange("p (q e) -> p q e", e=64),
                              in_=t[gates][trow:trow + 512, :]
                              .rearrange("(q p) e -> p q e", p=128)).then_inc(h_xg, 16)
                eng.dma_start(out=X1C3, in_=fmaj(t[x1], b, tc * 512, 512)).then_inc(h_xc, 16)
                eng.dma_start(out=ATC3, in_=fmaj(t[att], b, tc * 512, 512)).then_inc(h_xc, 16)
                for k in range(64):
                    kk = ci * 64 + k
                    if kk >= 2:
                        eng.wait_ge(h_hm, b_hm + 4 * kk - 4)
                    hs, bs = kld[kk % 2]
                    eng.dma_start(out=KS[kk % 2], in_=t["KEYSbf"][k]
                                  .rearrange("(dc p) e -> p dc e", p=128)
                                  ).then_inc(hs, 16)
                for k in range(64):
                    kk = ci * 64 + k
                    if kk >= 2:
                        eng.wait_ge(h_bm, b_bm + kk - 1)
                    hs, bs = vld[kk % 2]
                    eng.dma_start(out=VS[kk % 2], in_=t["VALbf"][k]).then_inc(hs, 16)
            if en == "tensor":
                if ci > 0:
                    eng.wait_ge(h_ev, b_ev + ci)
                eng.wait_ge(h_xt, b_xt + 16 * 4 * (ci + 1))
                for k in range(64):
                    kk = ci * 64 + k
                    hs, bs = kld[kk % 2]
                    eng.wait_ge(hs, bs + 16 * (kk // 2 + 1))
                    for tt in range(4):
                        idx = kk * 4 + tt
                        hb = [P.PB4, P.PB5][tt % 2]
                        if idx >= 2:
                            eng.wait_ge(h_hr, b_hr + idx - 1)
                        for dc in range(8):
                            ti = eng.matmul(hb[:, 0:128], XT[tt][:, dc, :],
                                            KS[kk % 2][:, dc, :],
                                            start=(dc == 0), stop=(dc == 7))
                        ti.then_inc(h_hm, 1)
                    for tt in range(4):
                        idx = kk * 4 + tt
                        eng.wait_ge(h_hr, b_hr + idx + 1)
                        if kk >= 1:
                            eng.wait_ge(h_hg, b_hg + kk)
                        ti = eng.transpose(PB6b[:, tt * 128:tt * 128 + 128], HG[tt % 2][:],
                                           P.ident_b[:])
                        ti.then_inc(h_tt, 1)
                eng.wait_ge(h_hg, b_hg + 64 * (ci + 1))
                for k in range(64):
                    kk = ci * 64 + k
                    hs, bs = vld[kk % 2]
                    eng.wait_ge(hs, bs + 16 * (kk // 2 + 1))
                    for vc in range(8):
                        ti = eng.matmul(bankv(P, vc), VS[kk % 2][:, vc * 128:vc * 128 + 128],
                                        HGT[:, k * 512:k * 512 + 512],
                                        start=(k == 0), stop=(k == 63))
                    ti.then_inc(h_bm, 1)
            if en == "scalar":
                eng.wait_ge(h_xg, b_xg + 16 * (ci + 1))
                for k in range(64):
                    kk = ci * 64 + k
                    for tt in range(4):
                        idx = kk * 4 + tt
                        eng.wait_ge(h_hm, b_hm + idx + 1)
                        if idx >= 2:
                            eng.wait_ge(h_tt, b_tt + idx - 1)
                        hb = [P.PB4, P.PB5][tt % 2]
                        gcol = GC[:, tt * 64 + k:tt * 64 + k + 1]
                        eng.activation(HG[tt % 2][:], hb[:, 0:128], AF.Relu,
                                       scale=gcol).then_inc(h_hr, 1)
                    eng.wait_ge(h_tt, b_tt + kk * 4 + 4)
                    eng.copy(HGT[:, k * 512:k * 512 + 512], PB6b[:, 0:512]
                             ).then_inc(h_hg, 1)
            if en == "vector":
                eng.wait_ge(h_bm, b_bm + 64 * (ci + 1))
                eng.wait_ge(h_xc, b_xc + 32 * (ci + 1))
                if ci > 0:
                    eng.wait_ge(h_ost, b_ost + 16 * ci)
                for dc in range(8):
                    eng.tensor_tensor(OCH3[:, dc, :], X1C3[:, dc, :], ATC3[:, dc, :],
                                      ALU.add)
                    ti = eng.tensor_tensor(OCH3[:, dc, :], OCH3[:, dc, :],
                                           bankv(P, dc), ALU.add)
                ti.then_inc(h_ev, 1)
            if en == "gpsimd":
                eng.wait_ge(h_ev, b_ev + ci + 1)
                eng.dma_start(out=fmaj(t[out], b, tc * 512, 512), in_=OCH3
                              ).then_inc(h_ost, 16)
        if en == "gpsimd":
            eng.wait_ge(h_ost, b_ost + 16 * 16)
    return emit



PARTS = {
    "l0a": {
        "ins": ["xin", "wm_l", "vcols", "wproj_l", "oscale"],
        "outs": ["XN32", "XNBF", "ATT"],
        "scratch": [],
        "phases": lambda P: [ph_init(P), ph_l0a(P)],
    },
    "ffn0": {
        "ins": ["XN32", "XNBF", "router", "shup", "shdn", "rup", "rdn"],
        "outs": ["ACC", "GATES0"],
        "scratch": ["UPbf", "DOWNbf"],
        "phases": lambda P: [ph_init(P), ph_precast(P, PRECAST_FFN0, "f0"),
                             ph_router0(P), ph_ffn0(P)],
    },
    "cmb0": {
        "ins": ["xin", "XN32", "ATT", "ACC"],
        "outs": ["X1", "XN132", "XN1BF"],
        "scratch": [],
        "phases": lambda P: [ph_init(P), ph_cmb0(P, xnout="XN132", xnbfout="XN1BF")],
    },
    "attn1": {
        "ins": ["XN1BF", "wqkv_l", "wout_l", "bout", "ropec", "ropes"],
        "outs": ["ATT1"],
        "scratch": ["WQKVbf", "WOUTbf"],
        "phases": lambda P: [ph_init(P), ph_precast(P, PRECAST_ATTN1, "a1"),
                             ph_attn1(P, xnbf="XN1BF")],
    },
    "ffn1": {
        "ins": ["XN132", "XN1BF", "X1", "ATT1", "esel_l", "keysw", "valsw"],
        "outs": ["out", "GATES1"],
        "scratch": ["KEYSbf", "VALbf"],
        "phases": lambda P: [ph_init(P), ph_precast(P, PRECAST_FFN1, "f1"),
                             ph_rt1(P, xn32="XN132", gates="GATES1"),
                             ph_ffn1(P, xnbf="XN1BF", gates="GATES1")],
    },
    "all": {
        "ins": ["xin", "wm_l", "vcols", "wproj_l", "oscale", "shup", "shdn", "rup",
                "rdn", "router", "wqkv_l", "wout_l", "bout", "esel_l", "keysw",
                "valsw", "ropec", "ropes"],
        "outs": ["out"],
        "scratch": ["XN32", "XNBF", "ATT", "ACC", "X1", "GATES0", "GATES1",
                    "UPbf", "DOWNbf", "WQKVbf", "WOUTbf", "KEYSbf", "VALbf"],
        "phases": lambda P: [
            ph_init(P),
            ph_precast(P, PRECAST_FFN0 + PRECAST_ATTN1 + PRECAST_FFN1, "all"),
            ph_l0a(P),
            ph_router0(P),
            ph_ffn0(P),
            ph_cmb0(P, xnout="XN32", xnbfout="XNBF"),
            ph_rt1(P, xn32="XN32", gates="GATES1"),
            ph_attn1(P, xnbf="XNBF", att="ATT"),
            ph_ffn1(P, xnbf="XNBF", gates="GATES1", att="ATT"),
        ],
    },
}


def build_part(part):
    spec = PARTS[part]
    P = Prog(spec["ins"], spec["outs"], spec["scratch"])
    for ph in spec["phases"](P):
        P.add(ph)
    return P.finalize(), spec


def run_part(part, in_map_per_core, core_ids=None):
    nc, spec = build_part(part)
    core_ids = core_ids or list(range(len(in_map_per_core)))
    res = run_bass_kernel_spmd(nc, in_map_per_core, core_ids=core_ids)
    return res


def kernel(**inputs):
    w = prep_shared(inputs)
    x = inputs["L0_attn_x"]
    nc, spec = build_part("all")
    in_maps = []
    for c in range(NCORES):
        m = {k: w[k] for k in spec["ins"] if k != "xin"}
        m["xin"] = prep_core_x(x, c)
        in_maps.append(m)
    res = run_bass_kernel_spmd(nc, in_maps, core_ids=list(range(NCORES)))
    out = np.empty((32, T, D), np.float32)
    for c in range(NCORES):
        out[c * BPC:(c + 1) * BPC] = np.asarray(res.results[c]["out"], np.float32) \
            .reshape(BPC, D, T).transpose(0, 2, 1)
    return out


# version 06 (final)
# speedup vs baseline: 1.3541x; 1.0564x over previous
import contextlib

import numpy as np

import concourse.bass as bass
import concourse.mybir as mybir
from concourse.bass_utils import run_bass_kernel_spmd

F32 = mybir.dt.float32
BF16 = mybir.dt.bfloat16
AF = mybir.ActivationFunctionType
ALU = mybir.AluOpType
AX = mybir.AxisListType

NCORES = 8
BPC = 4
T = 2048
D = 1024
NTOK = BPC * T
EPS_LN = 1e-5
EPS_RMS = 1.1920929e-07



def _cc(a):
    return np.ascontiguousarray(a)


def prep_shared(inputs):
    w = {}
    wm = np.asarray(inputs["L0_attn_W_m"], np.float32)
    wm_perm = wm.reshape(256, 4, 1024).transpose(1, 0, 2).reshape(1024, 1024)
    w["wm_l"] = _cc(wm_perm.T)
    v = np.asarray(inputs["L0_attn_v_vec"], np.float32)[:, :, 0]
    w["vcols"] = _cc(v.reshape(2, 128, 2).transpose(0, 2, 1).reshape(4, 128))
    wp = np.asarray(inputs["L0_attn_W_proj"], np.float32)
    w["wproj_l"] = _cc(wp.T.reshape(256, 2, 1024).transpose(1, 0, 2).reshape(512, 1024))
    w["oscale"] = _cc(np.asarray(inputs["L0_attn_out_scale"], np.float32).reshape(8, 128))
    w["shup"] = _cc(np.asarray(inputs["L0_ffn_shared_up"], np.float32)[0])
    w["shdn"] = _cc(np.asarray(inputs["L0_ffn_shared_down"], np.float32)[0])
    w["rup"] = _cc(np.asarray(inputs["L0_ffn_routed_up"], np.float32))
    w["rdn"] = _cc(np.asarray(inputs["L0_ffn_routed_down"], np.float32))
    w["router"] = _cc(np.asarray(inputs["L0_ffn_router_w"], np.float32))
    w["wqkv_l"] = _cc(np.asarray(inputs["L1_attn_w_qkv"], np.float32).T)
    w["wout_l"] = _cc(np.asarray(inputs["L1_attn_w_out"], np.float32).T)
    w["bout"] = _cc(np.asarray(inputs["L1_attn_b_out"], np.float32).reshape(8, 128))
    w["esel_l"] = _cc(np.asarray(inputs["L1_ffn_expert_sel"], np.float32).T)
    w["keysw"] = _cc(np.asarray(inputs["L1_ffn_keys"], np.float32))
    w["valsw"] = _cc(np.asarray(inputs["L1_ffn_values"], np.float32))
    hd = 64
    inv = (1.0 / (np.float32(10000.0) ** (np.arange(0, hd, 2, dtype=np.float32)
                                          / np.float32(hd)))).astype(np.float32)
    tt = np.arange(T, dtype=np.float32)
    fr = tt[:, None] * inv[None, :]
    w["ropec"] = _cc(np.cos(fr).astype(np.float32).reshape(16, 128, 32)
                     .transpose(1, 0, 2).reshape(128, 512))
    w["ropes"] = _cc(np.sin(fr).astype(np.float32).reshape(16, 128, 32)
                     .transpose(1, 0, 2).reshape(128, 512))
    sel = np.zeros((64, 8192), np.float32)
    for e in range(64):
        sel[e, e * 128:(e + 1) * 128] = 1.0
    w["sel64"] = sel
    return w


def prep_core_x(x, core):
    xs = np.asarray(x, np.float32)[core * BPC:(core + 1) * BPC]
    return _cc(xs.transpose(0, 2, 1).reshape(BPC * D, T))


DRAM_SPEC = {
    "xin":   ([BPC * D, T], F32),
    "wm_l":  ([1024, 1024], F32),
    "vcols": ([4, 128], F32),
    "wproj_l": ([512, 1024], F32),
    "oscale": ([8, 128], F32),
    "shup":  ([1024, 4096], F32),
    "shdn":  ([4096, 1024], F32),
    "rup":   ([8, 1024, 4096], F32),
    "rdn":   ([8, 4096, 1024], F32),
    "router": ([1024, 8], F32),
    "wqkv_l": ([1024, 3072], F32),
    "wout_l": ([1024, 1024], F32),
    "bout":  ([8, 128], F32),
    "esel_l": ([1024, 64], F32),
    "keysw": ([64, 1024, 128], F32),
    "valsw": ([64, 128, 1024], F32),
    "ropec": ([128, 512], F32),
    "ropes": ([128, 512], F32),
    "sel64": ([64, 8192], F32),
    "XN32":  ([BPC * D, T], F32),
    "XNBF":  ([BPC * D, T], BF16),
    "ATT":   ([BPC * D, T], F32),
    "ACC":   ([NTOK, D], F32),
    "X1":    ([BPC * D, T], F32),
    "GATES0": ([NTOK, 8], F32),
    "GATES1": ([NTOK, 64], F32),
    "UPbf":  ([9, 1024, 4096], BF16),
    "DOWNbf": ([9, 4096, 1024], BF16),
    "WQKVbf": ([1024, 3072], BF16),
    "WOUTbf": ([1024, 1024], BF16),
    "KEYSbf": ([64, 1024, 128], BF16),
    "VALbf": ([64, 128, 1024], BF16),
    "XN132": ([BPC * D, T], F32),
    "XN1BF": ([BPC * D, T], BF16),
    "ATT1":  ([BPC * D, T], F32),
    "out":   ([BPC * D, T], F32),
}

PRECAST_FFN0 = ([("shup", None, "UPbf", 0)] +
                [("rup", e, "UPbf", e + 1) for e in range(8)] +
                [("shdn", None, "DOWNbf", 0)] +
                [("rdn", e, "DOWNbf", e + 1) for e in range(8)])
PRECAST_ATTN1 = [("wqkv_l", None, "WQKVbf", None), ("wout_l", None, "WOUTbf", None)]
PRECAST_FFN1 = [("keysw", None, "KEYSbf", None), ("valsw", None, "VALbf", None)]


class Prog:
    def __init__(self, ins, outs, scratch):
        self.nc = bass.Bass("TRN2", target_bir_lowering=False, debug=False,
                            detect_race_conditions=False)
        self.ctx = contextlib.ExitStack()
        self.t = {}
        for name in ins:
            shape, dt = DRAM_SPEC[name]
            self.t[name] = self.nc.dram_tensor(name, shape, dt, kind="ExternalInput").ap()
        for name in outs:
            shape, dt = DRAM_SPEC[name]
            self.t[name] = self.nc.dram_tensor(name, shape, dt, kind="ExternalOutput").ap()
        for name in scratch:
            shape, dt = DRAM_SPEC[name]
            self.t[name] = self.nc.dram_tensor(name, shape, dt, kind="Internal").ap()
        c, nc = self.ctx, self.nc
        self.A1 = c.enter_context(nc.sbuf_tensor("A1", [128, 32768], BF16))
        self.A2 = c.enter_context(nc.sbuf_tensor("A2", [128, 32768], BF16))
        self.A3 = c.enter_context(nc.sbuf_tensor("A3", [128, 20480], BF16))
        self.GS = c.enter_context(nc.sbuf_tensor("GS", [128, 4096], BF16))
        self.ident_b = c.enter_context(nc.sbuf_tensor("ident_b", [128, 128], BF16))
        self.ident_f = c.enter_context(nc.sbuf_tensor("ident_f", [128, 128], F32))
        self.onesc = c.enter_context(nc.sbuf_tensor("onesc", [128, 1], F32))
        self.onesb = c.enter_context(nc.sbuf_tensor("onesb", [128, 1], BF16))
        self.onesrb = c.enter_context(nc.sbuf_tensor("onesrb", [1, 64], F32))
        self.onesr = c.enter_context(nc.sbuf_tensor("onesr", [1, 128], F32))
        self.rows = c.enter_context(nc.sbuf_tensor("rows", [1, 3072], F32))
        self.cols = c.enter_context(nc.sbuf_tensor("cols", [128, 16], F32))
        self.oscale_c = c.enter_context(nc.sbuf_tensor("oscale_c", [128, 8], F32))
        self.bout_c = c.enter_context(nc.sbuf_tensor("bout_c", [128, 8], F32))
        self.vcols_c = c.enter_context(nc.sbuf_tensor("vcols_c", [128, 4], F32))
        self.wcar = c.enter_context(nc.sbuf_tensor("wcar", [128, 96], F32))
        self.P4 = c.enter_context(nc.psum_tensor("P4", [128, 2048], F32))
        self.P4b = c.enter_context(nc.psum_tensor("P4b", [128, 2048], F32))
        self.PB4 = self.P4b[:, 0:512]
        self.PB5 = self.P4b[:, 512:1024]
        self.PB6 = self.P4b[:, 1024:1536]
        self.PB7 = self.P4b[:, 1536:2048]
        self.sems = {}
        self.bar = self.sem("bar")
        self.phases = []

    def sem(self, name):
        h = self.ctx.enter_context(self.nc.semaphore(name))
        self.sems[name] = h
        return h

    def add(self, emitfn):
        self.phases.append(emitfn)

    def finalize(self):
        nc, phases, bar = self.nc, self.phases, self.bar
        with nc.Block() as block:
            def mk(en):
                def body(eng):
                    for k, ph in enumerate(phases):
                        if k > 0:
                            eng.wait_ge(bar, 5 * k)
                        ph(en, eng)
                        eng.drain()
                        eng.sem_inc(bar, 1)
                return body
            block.sync(mk("sync"))
            block.tensor(mk("tensor"))
            block.vector(mk("vector"))
            block.scalar(mk("scalar"))
            block.gpsimd(mk("gpsimd"))
        self.ctx.close()
        return self.nc

    def a_f32(self, arena, start_f32, n_f32):
        return arena[:].bitcast(F32)[:, start_f32:start_f32 + n_f32]

    def a_bf(self, arena, start_bf, n_bf):
        return arena[:, start_bf:start_bf + n_bf]

    def row(self, i):
        return self.rows[0:1, i * 512:(i + 1) * 512]


class Ring:

    def __init__(self, P, name, n):
        self.n = n
        self.ld = [P.sem(f"{name}_ld{i}") for i in range(n)]

    def ld_sem(self, i):
        return self.ld[i % self.n]

    def consume_wait(self, eng, i):
        eng.wait_ge(self.ld[i % self.n], 16 * (i // self.n + 1))


class StRing:

    def __init__(self, P, name, n):
        self.n = n
        self.st = [P.sem(f"{name}_st{i}") for i in range(n)]
        self.rd = P.sem(f"{name}_rd")

    def produce_guard(self, eng, i):
        if i >= self.n:
            eng.wait_ge(self.st[i % self.n], 16 * ((i - self.n) // self.n + 1))

    def filled(self, eng, i):
        eng.sem_inc(self.rd, 1)

    def filled_on(self, instr, i):
        instr.then_inc(self.rd, 1)

    def store_wait(self, eng, i):
        eng.wait_ge(self.rd, i + 1)

    def st_sem(self, i):
        return self.st[i % self.n]

    def wait_all(self, eng, total):
        for j in range(self.n):
            cnt = (total - j + self.n - 1) // self.n if total > j else 0
            if cnt:
                eng.wait_ge(self.st[j], 16 * cnt)


def fmaj(dram_ap, b, c0, w):
    return dram_ap[b * D:(b + 1) * D, c0:c0 + w].rearrange("(dc p) u -> p dc u", p=128)


def v3(ap2d, u):
    return ap2d.rearrange("p (dc u) -> p dc u", u=u)



def ph_init(P):
    t = P.t
    sld = P.sem("ini_ld")
    sgi = P.sem("ini_g")

    def emit(en, eng):
        if en == "gpsimd":
            eng.memset(P.ident_b[:], 0.0)
            eng.memset(P.ident_f[:], 0.0).then_inc(sgi, 1)
            eng.wait_ge(sgi, 1)
            for idn in (P.ident_b, P.ident_f):
                eng.affine_select(out=idn[:], in_=idn[:], compare_op=ALU.not_equal,
                                  fill=1.0, base=0, pattern=[[-1, 128]], channel_multiplier=1)
        if en == "vector":
            eng.memset(P.onesc[:], 1.0)
            eng.memset(P.onesr[:], 1.0)
            eng.memset(P.onesb[:], 1.0)
            eng.memset(P.onesrb[:], 1.0)
        if en == "sync":
            n = 0
            for name, dst, k in (("oscale", P.oscale_c, 8), ("bout", P.bout_c, 8),
                                 ("vcols", P.vcols_c, 4)):
                if name in t:
                    for c in range(k):
                        eng.dma_start(out=dst[:, c:c + 1],
                                      in_=t[name][c, :].unsqueeze(1)).then_inc(sld, 16)
                    n += k
            eng.wait_ge(sld, 16 * n)
    return emit



def ph_precast(P, jobs, tag):
    t = P.t
    CH = 1048576
    plan = []
    for (src, si, d, di) in jobs:
        shape = DRAM_SPEC[src][0]
        n = int(np.prod(shape[1:]) if si is not None else np.prod(shape))
        assert n % CH == 0, (src, n)
        for c in range(n // CH):
            plan.append((src, si, d, di, c))
    rin = Ring(P, f"pc{tag}i", 2)
    rout = StRing(P, f"pc{tag}o", 2)

    def flat_view(name, idx, c):
        ap = t[name]
        if idx is not None:
            ap = ap[idx]
        if len(ap.shape) == 2:
            ap = ap.rearrange("a b -> (a b)")
        elif len(ap.shape) == 3:
            ap = ap.rearrange("a b c -> (a b c)")
        return ap.rearrange("(c p m) -> c p m", p=128, m=8192)[c]

    def emit(en, eng):
        nch = len(plan)
        for i, (src, si, d, di, c) in enumerate(plan):
            slot = i % 2
            stg = P.a_f32(P.A1, slot * 8192, 8192)
            obf = P.a_bf(P.A2, slot * 8192, 8192)
            if en == "sync":
                if i >= 2:
                    eng.wait_ge(rout.rd, i - 1)
                eng.dma_start(out=stg, in_=flat_view(src, si, c)).then_inc(rin.ld_sem(i), 16)
            if en == "vector":
                rin.consume_wait(eng, i)
                rout.produce_guard(eng, i)
                rout.filled_on(eng.tensor_copy(obf, stg), i)
            if en == "gpsimd":
                rout.store_wait(eng, i)
                eng.dma_start(out=flat_view(d, di, c), in_=obf).then_inc(rout.st_sem(i), 16)
        if en == "gpsimd":
            rout.wait_all(eng, nch)
    return emit



def emit_ln_rows(P, en, eng, s_rv, s_rs, rvb, rsb, rms2):
    r_mean, r_msq, r_t2, r_var, r_A, r_B = (P.row(i) for i in range(6))
    r_A2, r_B2 = r_mean, r_var
    r_tmp, r_ve = r_t2, r_msq
    if en == "vector":
        eng.tensor_single_scalar(r_mean, P.PB4[0:1, :], 1.0 / 1024.0, ALU.mult
                                 ).then_inc(s_rv, 1)
        eng.tensor_single_scalar(r_msq, P.PB5[0:1, :], 1.0 / 1024.0, ALU.mult)
        eng.wait_ge(s_rs, rsb + 1)
        eng.tensor_tensor(r_var, r_msq, r_tmp, ALU.subtract)
        eng.tensor_single_scalar(r_ve, r_var, EPS_LN, ALU.add).then_inc(s_rv, 1)
        eng.wait_ge(s_rs, rsb + 2)
        eng.wait_ge(s_rv, rvb + 1)
        eng.tensor_tensor(r_B, r_mean, r_A, ALU.mult)
        if rms2:
            eng.tensor_tensor(r_t2, r_A, r_A, ALU.mult)
            eng.tensor_tensor(r_t2, r_var, r_t2, ALU.mult)
            eng.tensor_single_scalar(r_t2, r_t2, EPS_RMS, ALU.add).then_inc(s_rv, 1)
            eng.wait_ge(s_rs, rsb + 3)
            eng.tensor_tensor(r_A2, r_A, r_t2, ALU.mult)
            eng.tensor_tensor(r_B2, r_B, r_t2, ALU.mult).then_inc(s_rv, 1)
        else:
            eng.tensor_copy(P.row(0), r_A).then_inc(s_rv, 1)
    if en == "scalar":
        eng.wait_ge(s_rv, rvb + 1)
        eng.square(r_tmp, r_mean).then_inc(s_rs, 1)
        eng.wait_ge(s_rv, rvb + 2)
        eng.activation(r_A, r_ve, AF.Ln)
        eng.activation(r_A, r_A, AF.Exp, scale=-0.5).then_inc(s_rs, 1)
        if rms2:
            eng.wait_ge(s_rv, rvb + 3)
            eng.activation(r_t2, r_t2, AF.Ln)
            eng.activation(r_t2, r_t2, AF.Exp, scale=-0.5).then_inc(s_rs, 1)
    return (4, 3) if rms2 else (3, 2)



def ph_l0a(P, xn32="XN32", xnbf="XNBF", att="ATT"):
    t = P.t
    s_wld = P.sem("la_wld")
    s_wcv = P.sem("la_wcv")
    s_xld = P.sem("la_xld")
    s_sq = P.sem("la_sq")
    s_stt = P.sem("la_stt")
    s_rv = P.sem("la_rv")
    s_rs = P.sem("la_rs")
    s_bc = P.sem("la_bc")
    s_xn = P.sem("la_xn")
    s_mm = P.sem("la_mm")
    s_me = P.sem("la_me")
    s_xst = P.sem("la_xst")
    s_nv = P.sem("la_nv")
    s_ns = P.sem("la_ns")
    s_hb = P.sem("la_hb")
    s_wp = P.sem("la_wp")
    s_wc2 = P.sem("la_wc2")
    s_op = P.sem("la_op")
    NRV, NRS = 4, 3
    r_att = StRing(P, "la_att", 4)
    P.recycle_pool = [(s_xld, 256), (s_sq, 16), (s_stt, 16), (s_rv, 64), (s_rs, 48),
                      (s_bc, 16), (s_xn, 16), (s_mm, 128), (s_me, 128),
                      (s_nv, 8), (s_ns, 8), (s_hb, 4), (s_op, 128), (s_wld, 32),
                      (s_wp, 64), (s_wcv, 2), (s_wc2, 4)]

    def emit(en, eng):
        SA = [P.a_f32(P.A1, i * 2048, 2048) for i in range(8)]
        SB = [P.a_f32(P.A2, i * 2048, 2048) for i in range(8)]
        SA3 = [x.rearrange("p (c u) -> p c u", u=128) for x in SA]
        SB3 = [x.rearrange("p (c u) -> p c u", u=128) for x in SB]
        WM = P.a_bf(P.A3, 0, 8192)
        HBF = [P.a_bf(P.A3, 8192 + i * 2048, 2048) for i in range(4)]
        HBF3 = [x.rearrange("p (c u) -> p c u", u=128) for x in HBF]
        WPJ = P.a_bf(P.A3, 16384, 4096)
        TMP = P.a_f32(P.GS, 0, 2048)
        TMP3 = TMP.rearrange("p (c u) -> p c u", u=128)
        XCH = P.a_f32(P.A2, 0, 4096)
        XSQ = P.a_f32(P.A2, 4096, 4096)
        XNF = P.a_f32(P.A2, 8192, 4096)
        XNB = P.a_bf(P.A2, 24576, 4096)
        XN2B = P.a_bf(P.A2, 28672, 4096)
        WSTG = [P.a_f32(P.A2, 0, 4096), P.a_f32(P.A2, 4096, 4096)]
        ATS = [P.a_f32(P.A2, i * 512, 512) for i in range(4)]
        XCH3 = XCH.rearrange("p (dc u) -> p dc u", u=512)
        XSQ3 = XSQ.rearrange("p (dc u) -> p dc u", u=512)
        XNF3 = XNF.rearrange("p (dc u) -> p dc u", u=512)
        XNB3 = XNB.rearrange("p (dc u) -> p dc u", u=512)
        XN2B3 = XN2B.rearrange("p (dc u) -> p dc u", u=512)
        WPJ3 = WPJ.rearrange("p (ic f) -> p ic f", f=1024)

        if en == "sync":
            for half in range(2):
                eng.dma_start(out=v3(WSTG[half], 1024),
                              in_=t["wm_l"][half * 512:(half + 1) * 512, :]
                              .rearrange("(dc p) f -> p dc f", p=128)).then_inc(s_wld, 16)
        if en == "vector":
            eng.wait_ge(s_wld, 32)
            for half in range(2):
                eng.tensor_copy(P.a_bf(P.A3, half * 4096, 4096), WSTG[half]).then_inc(s_wcv, 1)

        for b in range(BPC):
            for tc in range(4):
                ci = b * 4 + tc
                c0 = tc * 512
                rvb, rsb = NRV * ci, NRS * ci
                if en == "sync":
                    if ci == 0:
                        eng.wait_ge(s_wcv, 2)
                    else:
                        eng.wait_ge(s_xn, ci)
                    if tc == 0 and b > 0:
                        r_att.wait_all(eng, 32 * b)
                    eng.dma_start(out=XCH3, in_=fmaj(t["xin"], b, c0, 512)
                                  ).then_inc(s_xld, 16)
                if en == "scalar":
                    eng.wait_ge(s_xld, 16 * (ci + 1))
                    eng.activation(XSQ, XCH, AF.Square).then_inc(s_sq, 1)
                    emit_ln_rows(P, en, eng, s_rv, s_rs, rvb, rsb, rms2=True)
                if en == "tensor":
                    eng.wait_ge(s_xld, 16 * (ci + 1))
                    if ci > 0:
                        eng.wait_ge(s_xn, ci)
                    if b > 0 and tc == 0:
                        eng.wait_ge(r_att.rd, 32 * b)
                    for dc in range(8):
                        eng.matmul(P.PB4[0:1, :], P.onesc[:], XCH3[:, dc, :],
                                   start=(dc == 0), stop=(dc == 7))
                    eng.wait_ge(s_sq, ci + 1)
                    for dc in range(8):
                        ti = eng.matmul(P.PB5[0:1, :], P.onesc[:], XSQ3[:, dc, :],
                                        start=(dc == 0), stop=(dc == 7))
                    ti.then_inc(s_stt, 1)
                    eng.wait_ge(s_rv, rvb + NRV)
                    eng.wait_ge(s_rs, rsb + NRS)
                    eng.matmul(P.PB4[:], P.onesr[:], P.row(4), start=True, stop=True)
                    eng.matmul(P.PB5[:], P.onesr[:], P.row(5), start=True, stop=True)
                    eng.matmul(P.PB6[:], P.onesr[:], P.row(0), start=True, stop=True)
                    ti = eng.matmul(P.PB7[:], P.onesr[:], P.row(3), start=True, stop=True)
                    ti.then_inc(s_bc, 1)
                    eng.wait_ge(s_xn, ci + 1)
                    for fc in range(8):
                        g = ci * 8 + fc
                        ps = P.P4[:, (fc % 4) * 512:(fc % 4) * 512 + 512]
                        if g >= 4:
                            eng.wait_ge(s_me, g - 3)
                        for dc in range(8):
                            ti = eng.matmul(
                                ps, WM[:, dc * 1024 + fc * 128:dc * 1024 + fc * 128 + 128],
                                XN2B3[:, dc, :], start=(dc == 0), stop=(dc == 7))
                        ti.then_inc(s_mm, 1)
                if en == "vector":
                    eng.wait_ge(s_stt, ci + 1)
                    emit_ln_rows(P, en, eng, s_rv, s_rs, rvb, rsb, rms2=True)
                    eng.wait_ge(s_bc, ci + 1)
                    if ci > 0:
                        eng.wait_ge(s_xst, 32 * ci)
                    for dc in range(8):
                        eng.tensor_tensor(XNF3[:, dc, :], XCH3[:, dc, :], P.PB4[:], ALU.mult)
                        eng.tensor_tensor(XNF3[:, dc, :], XNF3[:, dc, :], P.PB5[:],
                                          ALU.subtract)
                        eng.tensor_copy(XNB3[:, dc, :], XNF3[:, dc, :])
                        eng.tensor_tensor(XSQ3[:, dc, :], XCH3[:, dc, :], P.PB6[:], ALU.mult)
                        ti = eng.tensor_tensor(XN2B3[:, dc, :], XSQ3[:, dc, :], P.PB7[:],
                                               ALU.subtract)
                    ti.then_inc(s_xn, 1)
                    for fc in range(8):
                        g = ci * 8 + fc
                        eng.wait_ge(s_mm, g + 1)
                        ps = P.P4[:, (fc % 4) * 512:(fc % 4) * 512 + 512]
                        eng.tensor_copy(SA[fc][:, c0:c0 + 512], ps).then_inc(s_me, 1)
                if en == "gpsimd":
                    eng.wait_ge(s_xn, ci + 1)
                    eng.dma_start(out=fmaj(t[xn32], b, c0, 512), in_=XNF3).then_inc(s_xst, 16)
                    eng.dma_start(out=fmaj(t[xnbf], b, c0, 512), in_=XNB3).then_inc(s_xst, 16)

            if en == "vector":
                eng.wait_ge(s_me, (b * 4 + 4) * 8)
                for kh in range(2):
                    U = SB[7]
                    eng.tensor_tensor(TMP, SA[kh], SA[kh], ALU.mult)
                    for c in range(1, 4):
                        eng.tensor_tensor(U, SA[2 * c + kh], SA[2 * c + kh], ALU.mult)
                        eng.tensor_tensor(TMP, TMP, U, ALU.add)
                    eng.tensor_single_scalar(TMP, TMP, 1e-12, ALU.max).then_inc(s_nv, 1)
                    eng.wait_ge(s_ns, b * 2 + kh + 1)
                    for c in range(4):
                        eng.tensor_tensor(SA[2 * c + kh], SA[2 * c + kh], TMP, ALU.mult)
            if en == "scalar":
                for kh in range(2):
                    eng.wait_ge(s_nv, b * 2 + kh + 1)
                    eng.activation(TMP, TMP, AF.Ln)
                    eng.activation(TMP, TMP, AF.Exp, scale=-0.5).then_inc(s_ns, 1)

            if en == "vector":
                cur3, nxt3 = SA3, SB3
                for step in range(7):
                    s = 1 << step
                    for kh in range(2):
                        for i in range(2):
                            for j in range(2):
                                ot = nxt3[2 * (i * 2 + j) + kh]
                                a0 = cur3[2 * (i * 2 + 0) + kh]
                                a1 = cur3[2 * (i * 2 + 1) + kh]
                                b0 = cur3[2 * (0 * 2 + j) + kh]
                                b1 = cur3[2 * (1 * 2 + j) + kh]
                                eng.tensor_tensor(ot[:, :, s:], a0[:, :, s:],
                                                  b0[:, :, :128 - s], ALU.mult)
                                eng.tensor_tensor(TMP3[:, :, s:], a1[:, :, s:],
                                                  b1[:, :, :128 - s], ALU.mult)
                                eng.tensor_tensor(ot[:, :, s:], ot[:, :, s:],
                                                  TMP3[:, :, s:], ALU.add)
                        for cc in range(4):
                            ti = 2 * cc + kh
                            eng.tensor_copy(nxt3[ti][:, :, :s], cur3[ti][:, :, :s])
                    cur3, nxt3 = nxt3, cur3
                SL = cur3
                eng.drain()
                for kh in range(2):
                    for j in range(2):
                        eng.tensor_copy(P.wcar[:, kh * 2 + j:kh * 2 + j + 1],
                                        P.vcols_c[:, kh * 2 + j:kh * 2 + j + 1])
                eng.drain()
                for c in range(16):
                    for kh in range(2):
                        w0 = P.wcar[:, kh * 2 + 0:kh * 2 + 1]
                        w1 = P.wcar[:, kh * 2 + 1:kh * 2 + 2]
                        sv0 = P.wcar[:, 4 + c * 4 + kh * 2:4 + c * 4 + kh * 2 + 1]
                        sv1 = P.wcar[:, 5 + c * 4 + kh * 2:5 + c * 4 + kh * 2 + 1]
                        eng.tensor_copy(sv0, w0)
                        eng.tensor_copy(sv1, w1)
                        if c < 15:
                            tt0 = P.wcar[:, 68:69]
                            tt1 = P.wcar[:, 69:70]
                            tt2 = P.wcar[:, 70:71]
                            tt3 = P.wcar[:, 71:72]
                            Pm = [SL[2 * cc + kh][:, c, 127:128] for cc in range(4)]
                            eng.drain()
                            eng.tensor_tensor(tt0, Pm[0], sv0, ALU.mult)
                            eng.tensor_tensor(tt1, Pm[1], sv1, ALU.mult)
                            eng.tensor_tensor(tt2, Pm[2], sv0, ALU.mult)
                            eng.tensor_tensor(tt3, Pm[3], sv1, ALU.mult)
                            eng.drain()
                            eng.tensor_tensor(w0, tt0, tt1, ALU.add)
                            eng.tensor_tensor(w1, tt2, tt3, ALU.add)
                            eng.drain()
                eng.drain()
                for c in range(16):
                    for kh in range(2):
                        sv0 = P.wcar[:, 4 + c * 4 + kh * 2:4 + c * 4 + kh * 2 + 1]
                        sv1 = P.wcar[:, 5 + c * 4 + kh * 2:5 + c * 4 + kh * 2 + 1]
                        for i in range(2):
                            tmpd = TMP3[:, c, :]
                            eng.tensor_single_scalar(
                                tmpd, SL[2 * (i * 2 + 0) + kh][:, c, :], sv0, ALU.mult)
                            eng.drain()
                            ti = eng.scalar_tensor_tensor(
                                HBF3[i * 2 + kh][:, c, :], SL[2 * (i * 2 + 1) + kh][:, c, :],
                                sv1, tmpd, ALU.mult, ALU.add)
                ti.then_inc(s_hb, 1)

            if en == "sync":
                eng.wait_ge(s_hb, b + 1)
                eng.dma_start(out=v3(WSTG[0], 1024),
                              in_=t["wproj_l"].rearrange("(ic p) f -> p ic f", p=128)
                              ).then_inc(s_wp, 16)
            if en == "vector":
                eng.wait_ge(s_wp, 16 * (b + 1))
                eng.tensor_copy(WPJ, WSTG[0]).then_inc(s_wc2, 1)
            if en == "tensor":
                eng.wait_ge(s_wc2, b + 1)
                for oj in range(32):
                    ec, tc2 = oj // 4, oj % 4
                    oi = b * 32 + oj
                    ps = P.P4[:, (oj % 4) * 512:(oj % 4) * 512 + 512]
                    if oi >= 4:
                        eng.wait_ge(r_att.rd, oi - 3)
                    for ic in range(4):
                        ti = eng.matmul(ps, WPJ3[:, ic, ec * 128:ec * 128 + 128],
                                        HBF[ic][:, tc2 * 512:tc2 * 512 + 512],
                                        start=(ic == 0), stop=(ic == 3))
                    ti.then_inc(s_op, 1)
            if en == "vector":
                for oj in range(32):
                    ec, tc2 = oj // 4, oj % 4
                    oi = b * 32 + oj
                    eng.wait_ge(s_op, oi + 1)
                    r_att.produce_guard(eng, oi)
                    ps = P.P4[:, (oj % 4) * 512:(oj % 4) * 512 + 512]
                    r_att.filled_on(
                        eng.tensor_single_scalar(ATS[oi % 4], ps, P.oscale_c[:, ec:ec + 1],
                                                 ALU.mult), oi)
            if en == "gpsimd":
                for oj in range(32):
                    ec, tc2 = oj // 4, oj % 4
                    oi = b * 32 + oj
                    r_att.store_wait(eng, oi)
                    eng.dma_start(out=t[att][b * D + ec * 128:b * D + ec * 128 + 128,
                                             tc2 * 512:tc2 * 512 + 512],
                                  in_=ATS[oi % 4]).then_inc(r_att.st_sem(oi), 16)
        if en == "gpsimd":
            r_att.wait_all(eng, 128)
            eng.wait_ge(s_xst, 32 * 16)
    return emit



def ph_router0(P, xn32="XN32", gates="GATES0"):
    t = P.t
    rx = Ring(P, "r0x", 2)
    rg = StRing(P, "r0g", 2)
    s_wl = P.sem("r0_wl")
    s_mm = P.sem("r0_mm")
    s_mv = P.sem("r0_mv")
    s_ex = P.sem("r0_ex")

    def emit(en, eng):
        RTW = P.a_f32(P.GS, 0, 64).rearrange("p (dc e) -> p dc e", e=8)
        XT = [P.a_f32(P.A3, i * 1024, 1024).rearrange("p (dc u) -> p dc u", u=128)
              for i in range(2)]
        P8 = P.a_f32(P.A3, 2048, 8)
        E8 = P.a_f32(P.A3, 2056, 8)
        GT = [P.a_f32(P.A3, 2064 + i * 8, 8) for i in range(2)]
        nmax = P.cols[:, 0:1]
        msum = P.cols[:, 1:2]
        rs = P.cols[:, 2:3]
        m1 = P.cols[:, 3:4]
        if en == "sync":
            eng.dma_start(out=RTW, in_=t["router"].rearrange("(dc p) e -> p dc e", p=128)
                          ).then_inc(s_wl, 16)
            for i in range(64):
                b, tt = i // 16, i % 16
                if i >= 2:
                    eng.wait_ge(s_ex, i - 1)
                eng.dma_start(out=XT[i % 2],
                              in_=t[xn32][b * D:(b + 1) * D, tt * 128:tt * 128 + 128]
                              .rearrange("(dc p) u -> p dc u", p=128)
                              ).then_inc(rx.ld_sem(i), 16)
        if en == "tensor":
            eng.wait_ge(s_wl, 16)
            for i in range(64):
                rx.consume_wait(eng, i)
                if i > 0:
                    eng.wait_ge(s_ex, i)
                for dc in range(8):
                    ti = eng.matmul(P.PB4[:, 0:8], XT[i % 2][:, dc, :], RTW[:, dc, :],
                                    start=(dc == 0), stop=(dc == 7))
                ti.then_inc(s_mm, 1)
        if en == "vector":
            for i in range(64):
                eng.wait_ge(s_mm, i + 1)
                eng.reduce_max(nmax, P.PB4[:, 0:8], axis=AX.X, negate=True
                               ).then_inc(s_mv, 1)
                eng.wait_ge(s_ex, i + 1)
                eng.drain()
                eng.reciprocal(rs, msum)
                eng.drain()
                eng.tensor_single_scalar(P8, P8, rs, ALU.mult)
                eng.drain()
                eng.reduce_max(m1, P8, axis=AX.X)
                eng.drain()
                eng.tensor_single_scalar(E8, P8, m1, ALU.is_ge)
                eng.drain()
                eng.tensor_single_scalar(E8, E8, -2.0, ALU.mult)
                eng.drain()
                eng.tensor_tensor(E8, P8, E8, ALU.add)
                eng.drain()
                eng.reduce_max(m1, E8, axis=AX.X)
                eng.drain()
                eng.tensor_single_scalar(E8, P8, m1, ALU.is_ge)
                eng.drain()
                rg.produce_guard(eng, i)
                rg.filled_on(eng.tensor_tensor(GT[i % 2], P8, E8, ALU.mult), i)
        if en == "scalar":
            for i in range(64):
                eng.wait_ge(s_mm, i + 1)
                eng.wait_ge(s_mv, i + 1)
                eng.activation(P8, P.PB4[:, 0:8], AF.Exp, bias=nmax,
                               accum_out=msum).then_inc(s_ex, 1)
        if en == "gpsimd":
            for i in range(64):
                b, tt = i // 16, i % 16
                rg.store_wait(eng, i)
                eng.dma_start(out=t[gates][b * T + tt * 128:b * T + tt * 128 + 128, :],
                              in_=GT[i % 2]).then_inc(rg.st_sem(i), 16)
            rg.wait_all(eng, 64)
    return emit


def bankv(P, j):
    if j < 4:
        return P.P4[:, j * 512:(j + 1) * 512]
    return [P.PB4, P.PB5, P.PB6, P.PB7][j - 4][:]



def ph_ffn0(P, xnbf="XNBF", gates="GATES0", acc="ACC"):
    t = P.t
    s_upl = P.sem("f0_upl")
    s_upe = P.sem("f0_upe")
    s_um = P.sem("f0_um")
    s_gel = P.sem("f0_gel")
    s_dk = P.sem("f0_dk")
    s_ev = P.sem("f0_ev")
    s_ev4 = P.sem("f0_ev4")
    rxn = Ring(P, "f0xn", 2)
    rga = Ring(P, "f0ga", 2)
    rac = Ring(P, "f0ac", 2)
    rdn = Ring(P, "f0dn", 2)
    rst = StRing(P, "f0st", 2)

    def emit(en, eng):
        UPW = P.A1[:].rearrange("p (dc f) -> p dc f", f=4096)
        XN = [P.a_bf(P.A2, i * 4096, 4096).rearrange("p (dc u) -> p dc u", u=512)
              for i in range(2)]
        GC = [P.a_f32(P.A2, 4096 + i * 32, 32) for i in range(2)]
        GC3 = [x.rearrange("p (q e) -> p q e", e=8) for x in GC]
        AC = [P.a_f32(P.A2, 4160 + i * 4096, 4096) for i in range(2)]
        AC3 = [x.rearrange("p (q u) -> p q u", u=1024) for x in AC]
        DN = [P.a_bf(P.A2, 24704 + i * 1024, 1024) for i in range(2)]
        GH = P.A3[:, 0:16384].rearrange("p (fc u) -> p fc u", u=512)

        for e in range(9):
            if en == "sync":
                if e > 0:
                    eng.wait_ge(s_upe, e)
                eng.dma_start(out=UPW, in_=t["UPbf"][e].rearrange("(dc p) f -> p dc f",
                                                                  p=128)
                              ).then_inc(s_upl, 16)
            for ci in range(16):
                gci = e * 16 + ci
                b, tc = ci // 4, ci % 4
                trow = b * T + tc * 512
                if en == "sync":
                    if gci >= 2:
                        eng.wait_ge(s_um, 32 * (gci - 1))
                    eng.dma_start(out=XN[gci % 2], in_=fmaj(t[xnbf], b, tc * 512, 512)
                                  ).then_inc(rxn.ld_sem(gci), 16)
                    if e > 0:
                        gj = gci - 16
                        if gj >= 2:
                            eng.wait_ge(s_ev, gci - 1)
                        eng.dma_start(out=GC3[gj % 2],
                                      in_=t[gates][trow:trow + 512, :]
                                      .rearrange("(q p) e -> p q e", p=128)
                                      ).then_inc(rga.ld_sem(gj), 16)
                        eng.wait_ge(rst.st[gci % 2], 16 * 4 * (gci // 2))
                        eng.dma_start(out=AC3[gj % 2],
                                      in_=t[acc][trow:trow + 512, :]
                                      .rearrange("(q p) u -> p q u", p=128)
                                      ).then_inc(rac.ld_sem(gj), 16)
                    for kc in range(32):
                        gdi = gci * 32 + kc
                        if gdi >= 2:
                            eng.wait_ge(s_dk, gdi - 1)
                        eng.dma_start(out=DN[gdi % 2],
                                      in_=t["DOWNbf"][e][kc * 128:kc * 128 + 128, :]
                                      ).then_inc(rdn.ld_sem(gdi), 16)
                if en == "tensor":
                    if ci == 0:
                        eng.wait_ge(s_upl, 16 * (e + 1))
                    rxn.consume_wait(eng, gci)
                    if gci > 0:
                        eng.wait_ge(s_ev4, gci)
                    for fc in range(32):
                        g = gci * 32 + fc
                        ps = P.P4[:, (fc % 4) * 512:(fc % 4) * 512 + 512]
                        if fc >= 4:
                            eng.wait_ge(s_gel, g - 3)
                        for dc in range(8):
                            ti = eng.matmul(ps, UPW[:, dc, fc * 128:fc * 128 + 128],
                                            XN[gci % 2][:, dc, :],
                                            start=(dc == 0), stop=(dc == 7))
                        ti.then_inc(s_um, 1)
                    if ci == 15:
                        eng.wait_ge(s_um, 32 * (gci + 1))
                        eng.sem_inc(s_upe, 1)
                    if gci > 0:
                        eng.wait_ge(s_ev, gci)
                    eng.wait_ge(s_gel, 32 * (gci + 1))
                    for kc in range(32):
                        gdi = gci * 32 + kc
                        rdn.consume_wait(eng, gdi)
                        for tt in range(4):
                            for fh in range(2):
                                ti = eng.matmul(bankv(P, tt * 2 + fh),
                                                GH[:, kc, tt * 128:tt * 128 + 128],
                                                DN[gdi % 2][:, fh * 512:fh * 512 + 512],
                                                start=(kc == 0), stop=(kc == 31))
                        ti.then_inc(s_dk, 1)
                if en == "scalar":
                    for fc in range(32):
                        g = gci * 32 + fc
                        eng.wait_ge(s_um, g + 1)
                        ps = P.P4[:, (fc % 4) * 512:(fc % 4) * 512 + 512]
                        eng.activation(GH[:, fc, :], ps, AF.Gelu).then_inc(s_gel, 1)
                if en == "vector":
                    eng.wait_ge(s_dk, 32 * (gci + 1))
                    if e > 0:
                        rga.consume_wait(eng, gci - 16)
                        rac.consume_wait(eng, gci - 16)
                    if gci >= 2:
                        eng.wait_ge(rst.st[gci % 2], 16 * 4 * (gci // 2))
                    for tt in range(4):
                        for fh in range(2):
                            dst = AC3[gci % 2][:, tt, fh * 512:fh * 512 + 512]
                            bk = bankv(P, tt * 2 + fh)
                            if e == 0:
                                ti = eng.tensor_copy(dst, bk)
                            else:
                                gcol = GC[gci % 2][:, tt * 8 + e - 1:tt * 8 + e]
                                ti = eng.scalar_tensor_tensor(dst, bk, gcol, dst,
                                                              ALU.mult, ALU.add)
                        if tt == 1:
                            ti.then_inc(s_ev4, 1)
                    ti.then_inc(s_ev, 1)
                if en == "gpsimd":
                    eng.wait_ge(s_ev, gci + 1)
                    for tt in range(4):
                        ti = eng.dma_start(out=t[acc][trow + tt * 128:trow + tt * 128 + 128, :],
                                           in_=AC3[gci % 2][:, tt, :])
                        ti.then_inc(rst.st[gci % 2], 16)
        if en == "gpsimd":
            for j in range(2):
                eng.wait_ge(rst.st[j], 16 * 4 * 72)
    return emit



def ph_cmb0(P, xin="xin", xn="XN32", att="ATT", acc="ACC",
            x1="X1", xnout="XN32", xnbfout="XNBF"):
    t = P.t
    s_ld = P.sem("c0_ld")
    s_tr = P.sem("c0_tr")
    s_cv = P.sem("c0_cv")
    s_sq = P.sem("c0_sq")
    s_stt = P.sem("c0_stt")
    s_rv = P.sem("c0_rv")
    s_rs = P.sem("c0_rs")
    s_bc = P.sem("c0_bc")
    s_xn = P.sem("c0_xn")
    s_st = P.sem("c0_st")

    def emit(en, eng):
        C_X = P.a_f32(P.A2, 0, 4096)
        C_XN = P.a_f32(P.A2, 4096, 4096)
        C_AT = P.a_f32(P.A2, 8192, 4096)
        C_AC = P.a_f32(P.A1, 0, 4096)
        XSQ = P.a_f32(P.A1, 4096, 4096)
        XNF = P.a_f32(P.A1, 8192, 4096)
        XNB = P.a_bf(P.A1, 24576, 4096)
        C_X3 = v3(C_X, 512)
        C_XN3 = v3(C_XN, 512)
        C_AT3 = v3(C_AT, 512)
        C_AC3 = C_AC.rearrange("p (q u) -> p q u", u=1024)
        XSQ3 = v3(XSQ, 512)
        XNF3 = v3(XNF, 512)
        XNB3 = v3(XNB, 512)
        NRV, NRS = 3, 2
        for ci in range(16):
            b, tc = ci // 4, ci % 4
            c0 = tc * 512
            trow = b * T + c0
            rvb, rsb = NRV * ci, NRS * ci
            if en == "sync":
                if ci > 0:
                    eng.wait_ge(s_xn, ci)
                    eng.wait_ge(s_st, 48 * ci)
                eng.dma_start(out=C_X3, in_=fmaj(t[xin], b, c0, 512)).then_inc(s_ld, 16)
                eng.dma_start(out=C_XN3, in_=fmaj(t[xn], b, c0, 512)).then_inc(s_ld, 16)
                eng.dma_start(out=C_AT3, in_=fmaj(t[att], b, c0, 512)).then_inc(s_ld, 16)
                eng.dma_start(out=C_AC3, in_=t[acc][trow:trow + 512, :]
                              .rearrange("(q p) u -> p q u", p=128)).then_inc(s_ld, 16)
            if en == "tensor":
                eng.wait_ge(s_ld, 64 * (ci + 1))
                if ci > 0:
                    eng.wait_ge(s_xn, ci)
                for dc in range(8):
                    if ci * 8 + dc >= 4:
                        eng.wait_ge(s_cv, ci * 8 + dc - 3)
                    for tt in range(4):
                        ti = eng.transpose(
                            P.P4[:, (dc % 4) * 512 + tt * 128:(dc % 4) * 512 + tt * 128 + 128],
                            C_AC3[:, tt, dc * 128:dc * 128 + 128], P.ident_f[:])
                    ti.then_inc(s_tr, 1)
                eng.wait_ge(s_cv, ci * 8 + 8)
                for dc in range(8):
                    eng.matmul(P.PB4[0:1, :], P.onesc[:], C_X3[:, dc, :],
                               start=(dc == 0), stop=(dc == 7))
                eng.wait_ge(s_sq, ci + 1)
                for dc in range(8):
                    ti = eng.matmul(P.PB5[0:1, :], P.onesc[:], XSQ3[:, dc, :],
                                    start=(dc == 0), stop=(dc == 7))
                ti.then_inc(s_stt, 1)
                eng.wait_ge(s_rv, rvb + NRV)
                eng.wait_ge(s_rs, rsb + NRS)
                eng.matmul(P.PB6[:], P.onesr[:], P.row(0), start=True, stop=True)
                ti = eng.matmul(P.PB7[:], P.onesr[:], P.row(5), start=True, stop=True)
                ti.then_inc(s_bc, 1)
            if en == "vector":
                eng.wait_ge(s_ld, 64 * (ci + 1))
                for dc in range(8):
                    eng.wait_ge(s_tr, ci * 8 + dc + 1)
                    eng.scalar_tensor_tensor(C_X3[:, dc, :], C_XN3[:, dc, :], 2.0,
                                             C_X3[:, dc, :], ALU.mult, ALU.add)
                    eng.tensor_tensor(C_X3[:, dc, :], C_X3[:, dc, :], C_AT3[:, dc, :],
                                      ALU.add)
                    ti = eng.tensor_tensor(C_X3[:, dc, :], C_X3[:, dc, :],
                                           P.P4[:, (dc % 4) * 512:(dc % 4) * 512 + 512],
                                           ALU.add)
                    ti.then_inc(s_cv, 1)
                eng.wait_ge(s_stt, ci + 1)
                emit_ln_rows(P, en, eng, s_rv, s_rs, rvb, rsb, rms2=False)
                eng.wait_ge(s_bc, ci + 1)
                if ci > 0:
                    eng.wait_ge(s_st, 16 * 3 * ci)
                for dc in range(8):
                    eng.tensor_tensor(XNF3[:, dc, :], C_X3[:, dc, :], P.PB6[:], ALU.mult)
                    eng.tensor_tensor(XNF3[:, dc, :], XNF3[:, dc, :], P.PB7[:],
                                      ALU.subtract)
                    ti = eng.tensor_copy(XNB3[:, dc, :], XNF3[:, dc, :])
                ti.then_inc(s_xn, 1)
            if en == "scalar":
                eng.wait_ge(s_cv, ci * 8 + 8)
                eng.activation(XSQ, C_X, AF.Square).then_inc(s_sq, 1)
                emit_ln_rows(P, en, eng, s_rv, s_rs, rvb, rsb, rms2=False)
            if en == "gpsimd":
                eng.wait_ge(s_xn, ci + 1)
                eng.dma_start(out=fmaj(t[x1], b, c0, 512), in_=C_X3).then_inc(s_st, 16)
                eng.dma_start(out=fmaj(t[xnout], b, c0, 512), in_=XNF3).then_inc(s_st, 16)
                eng.dma_start(out=fmaj(t[xnbfout], b, c0, 512), in_=XNB3).then_inc(s_st, 16)
        if en == "gpsimd":
            eng.wait_ge(s_st, 16 * 3 * 16)
    return emit



def ph_rt1(P, xn32="XN32", gates="GATES1"):
    t = P.t
    rx = Ring(P, "r1x", 2)
    rg = StRing(P, "r1g", 2)
    s_wl = P.sem("r1_wl")
    s_mm = P.sem("r1_mm")
    s_sg = P.sem("r1_sg")

    def emit(en, eng):
        ESL = P.a_f32(P.A3, 0, 512).rearrange("p (dc e) -> p dc e", e=64)
        XT = [P.a_f32(P.A3, 512 + i * 1024, 1024).rearrange("p (dc u) -> p dc u", u=128)
              for i in range(2)]
        SEL = P.a_f32(P.A3, 2560, 64)
        SELW = P.a_f32(P.A3, 2624, 64)
        E64 = P.a_f32(P.A3, 2688, 64)
        GT = [P.a_f32(P.A3, 2752 + i * 64, 64) for i in range(2)]
        mj = P.cols[:, 0:1]
        if en == "sync":
            eng.dma_start(out=ESL, in_=t["esel_l"].rearrange("(dc p) e -> p dc e", p=128)
                          ).then_inc(s_wl, 16)
            for i in range(64):
                b, tt = i // 16, i % 16
                if i >= 2:
                    eng.wait_ge(s_sg, i - 1)
                eng.dma_start(out=XT[i % 2],
                              in_=t[xn32][b * D:(b + 1) * D, tt * 128:tt * 128 + 128]
                              .rearrange("(dc p) u -> p dc u", p=128)
                              ).then_inc(rx.ld_sem(i), 16)
        if en == "tensor":
            eng.wait_ge(s_wl, 16)
            for i in range(64):
                rx.consume_wait(eng, i)
                if i > 0:
                    eng.wait_ge(s_sg, i)
                for dc in range(8):
                    ti = eng.matmul(P.PB4[:, 0:64], XT[i % 2][:, dc, :], ESL[:, dc, :],
                                    start=(dc == 0), stop=(dc == 7))
                ti.then_inc(s_mm, 1)
        if en == "scalar":
            for i in range(64):
                eng.wait_ge(s_mm, i + 1)
                if i >= 1:
                    eng.wait_ge(rg.rd, i)
                eng.activation(SEL, P.PB4[:, 0:64], AF.Sigmoid).then_inc(s_sg, 1)
        if en == "vector":
            for i in range(64):
                eng.wait_ge(s_sg, i + 1)
                eng.tensor_copy(SELW, SEL)
                eng.drain()
                for j in range(3):
                    eng.reduce_max(mj, SELW, axis=AX.X)
                    eng.drain()
                    eng.tensor_single_scalar(E64, SELW, mj, ALU.is_ge)
                    eng.drain()
                    eng.tensor_single_scalar(E64, E64, -3.0, ALU.mult)
                    eng.drain()
                    eng.tensor_tensor(SELW, SELW, E64, ALU.add)
                    eng.drain()
                eng.reduce_max(mj, SELW, axis=AX.X)
                eng.drain()
                eng.tensor_single_scalar(E64, SEL, mj, ALU.is_ge)
                eng.drain()
                rg.produce_guard(eng, i)
                rg.filled_on(eng.tensor_tensor(GT[i % 2], SEL, E64, ALU.mult), i)
        if en == "gpsimd":
            for i in range(64):
                b, tt = i // 16, i % 16
                rg.store_wait(eng, i)
                eng.dma_start(out=t[gates][b * T + tt * 128:b * T + tt * 128 + 128, :],
                              in_=GT[i % 2]).then_inc(rg.st_sem(i), 16)
            rg.wait_all(eng, 64)
    return emit



def ph_attn1(P, xnbf="XNBF", att="ATT1"):
    t = P.t
    s_wql = P.sem("a1_wql")
    s_qm = P.sem("a1_qm")
    s_qe = P.sem("a1_qe")
    s_wol = P.sem("a1_wol")
    s_om = P.sem("a1_om")
    s_kt = P.sem("a1_kt")
    s_ke = P.sem("a1_ke")
    s_tm = P.sem("a1_tm")
    s_se = P.sem("a1_se")
    s_ve = P.sem("a1_ve")
    r_att = StRing(P, "a1_st", 2)
    rxq = Ring(P, "a1_xq", 2)
    s_rp = P.sem("a1_rp")

    def emit(en, eng):
        QT4 = P.A1[:, 0:16384].rearrange("p (tt h d) -> p tt h d", h=16, d=64)
        KT4 = P.A1[:, 16384:32768].rearrange("p (tt h d) -> p tt h d", h=16, d=64)
        VT3 = P.A2[:, 0:16384].rearrange("p (tt f) -> p tt f", f=1024)
        CF3 = P.A2[:, 16384:32768].rearrange("p (dc u) -> p dc u", u=2048)
        WQG = P.a_bf(P.A3, 0, 4096).rearrange("p (dc f) -> p dc f", f=512)
        WOU = P.a_bf(P.A3, 0, 8192).rearrange("p (dc f) -> p dc f", f=1024)
        XT1 = [P.a_bf(P.A3, 8192 + i * 1024, 1024).rearrange("p (dc u) -> p dc u", u=128)
               for i in range(2)]
        KH = P.a_bf(P.A3, 10240, 2048)
        QH = P.a_bf(P.A3, 12288, 128)
        PSBT = [P.a_bf(P.A3, 12416, 2048), P.a_bf(P.GS, 1024, 2048)]
        BCS = P.a_f32(P.A3, 7232, 128)
        ROPEC = P.a_f32(P.A3, 8256, 512)
        ROPES = P.a_f32(P.A3, 8768, 512)
        TMQ = P.a_f32(P.GS, 0, 256).rearrange("p (h d) -> p h d", d=32)
        TMQ2 = P.a_f32(P.GS, 256, 256).rearrange("p (h d) -> p h d", d=32)
        ATS = [P.a_f32(P.A1, i * 512, 512) for i in range(2)]
        REG = [P.P4[:], P.P4b[:]]
        REGb = [r.bitcast(BF16) for r in REG]
        rrow = P.rows[0:1, 2560:2688]
        ctr = {"tm": 0, "se": 0, "ve": 0}
        exp_done, se_done, ve_done = {}, {}, {}

        if en == "sync":
            eng.dma_start(out=ROPEC, in_=t["ropec"]).then_inc(s_rp, 16)
            eng.dma_start(out=ROPES, in_=t["ropes"]).then_inc(s_rp, 16)
        if en == "vector":
            eng.wait_ge(s_rp, 32)
        for b in range(BPC):
            for fg in range(6):
                qg = b * 6 + fg
                if en == "sync":
                    if qg > 0:
                        eng.wait_ge(s_qm, 16 * qg)
                    if b > 0:
                        eng.wait_ge(s_om, 32 * b)
                    eng.dma_start(out=WQG,
                                  in_=t["WQKVbf"][:, fg * 512:fg * 512 + 512]
                                  .rearrange("(dc p) f -> p dc f", p=128)
                                  ).then_inc(s_wql, 16)
                    for tt in range(16):
                        qi = qg * 16 + tt
                        if qi >= 2:
                            eng.wait_ge(s_qm, qi - 1)
                        eng.dma_start(out=XT1[qi % 2], in_=t[xnbf]
                                      [b * D:(b + 1) * D, tt * 128:tt * 128 + 128]
                                      .rearrange("(dc p) u -> p dc u", p=128)
                                      ).then_inc(rxq.ld_sem(qi), 16)
                if en == "tensor":
                    eng.wait_ge(s_wql, 16 * (qg + 1))
                    if b > 0 and fg == 0:
                        eng.wait_ge(r_att.rd, 32 * b)
                    for tt in range(16):
                        qi = qg * 16 + tt
                        rxq.consume_wait(eng, qi)
                        if qi >= 4:
                            eng.wait_ge(s_qe, qi - 3)
                        ps = P.P4[:, (qi % 4) * 512:(qi % 4) * 512 + 512]
                        for dc in range(8):
                            ti = eng.matmul(ps, XT1[qi % 2][:, dc, :], WQG[:, dc, :],
                                            start=(dc == 0), stop=(dc == 7))
                        ti.then_inc(s_qm, 1)
                if en == "vector":
                    if fg == 0 and b > 0:
                        r_att.wait_all(eng, 32 * b)
                    for tt in range(16):
                        qi = qg * 16 + tt
                        eng.wait_ge(s_qm, qi + 1)
                        ps = P.P4[:, (qi % 4) * 512:(qi % 4) * 512 + 512]
                        if fg < 4:
                            ps3 = ps.rearrange("p (h d) -> p h d", d=64)
                            x1 = ps3[:, :, 0:32]
                            x2 = ps3[:, :, 32:64]
                            Cb = ROPEC[:, tt * 32:tt * 32 + 32].unsqueeze(1) \
                                .broadcast_to([128, 8, 32])
                            Sb = ROPES[:, tt * 32:tt * 32 + 32].unsqueeze(1) \
                                .broadcast_to([128, 8, 32])
                            O4 = QT4 if fg < 2 else KT4
                            h0 = (fg % 2) * 8
                            eng.tensor_tensor(TMQ[:], x1, Cb, ALU.mult)
                            eng.tensor_tensor(TMQ2[:], x2, Sb, ALU.mult)
                            eng.drain()
                            eng.tensor_tensor(O4[:, tt, h0:h0 + 8, 0:32], TMQ[:], TMQ2[:],
                                              ALU.subtract)
                            eng.tensor_tensor(TMQ[:], x1, Sb, ALU.mult)
                            eng.tensor_tensor(TMQ2[:], x2, Cb, ALU.mult)
                            eng.drain()
                            ti = eng.tensor_tensor(O4[:, tt, h0:h0 + 8, 32:64], TMQ[:],
                                                   TMQ2[:], ALU.add)
                        else:
                            f0 = (fg - 4) * 512
                            ti = eng.tensor_copy(VT3[:, tt, f0:f0 + 512], ps)
                        ti.then_inc(s_qe, 1)

            def emit_backlog(j):
                pj = j % 2
                hj = (j // 16) % 16
                qtj = j % 16
                t3 = ctr["tm"] + 1
                t4 = ctr["tm"] + 2
                t5 = ctr["tm"] + 3
                ctr["tm"] += 3
                s3 = ctr["se"] + 1
                ctr["se"] += 1
                v1 = ctr["ve"] + 1
                v2 = ctr["ve"] + 2
                ctr["ve"] += 2
                if en == "tensor":
                    eng.wait_ge(s_se, exp_done[j])
                    for kc in range(16):
                        ti = eng.matmul(REG[pj][0:1, 384:512], P.onesb[:],
                                        PSBT[pj][:, kc * 128:kc * 128 + 128],
                                        start=(kc == 0), stop=(kc == 15))
                    ti.then_inc(s_tm, 1)
                    eng.wait_ge(s_ve, v1)
                    ti = eng.matmul(REG[pj][0:64, 128:256], P.onesrb[:], rrow,
                                    start=True, stop=True)
                    ti.then_inc(s_tm, 1)
                    for kc in range(16):
                        ti = eng.matmul(REG[pj][0:64, 0:128],
                                        VT3[:, kc, hj * 64:hj * 64 + 64],
                                        PSBT[pj][:, kc * 128:kc * 128 + 128],
                                        start=(kc == 0), stop=(kc == 15))
                    ti.then_inc(s_tm, 1)
                if en == "scalar":
                    eng.wait_ge(s_tm, t5)
                    eng.copy(BCS[0:64, :], REG[pj][0:64, 128:256]).then_inc(s_se, 1)
                if en == "vector":
                    eng.wait_ge(s_tm, t3)
                    eng.reciprocal(rrow, REG[pj][0:1, 384:512]).then_inc(s_ve, 1)
                    eng.wait_ge(s_tm, t5)
                    eng.wait_ge(s_se, s3)
                    eng.tensor_tensor(CF3[(hj % 2) * 64:(hj % 2) * 64 + 64, hj // 2,
                                          qtj * 128:qtj * 128 + 128],
                                      REG[pj][0:64, 0:128], BCS[0:64, :],
                                      ALU.mult).then_inc(s_ve, 1)
                se_done[j] = s3
                ve_done[j] = v2

            for h in range(16):
                hb = b * 16 + h
                qq0 = hb * 16
                if en == "tensor":
                    if h == 0:
                        eng.wait_ge(s_qe, 96 * (b + 1))
                    if qq0 >= 2:
                        eng.wait_ge(s_se, se_done[qq0 - 2])
                        eng.wait_ge(s_ve, ve_done[qq0 - 2])
                    for r in range(2):
                        kr = hb * 2 + r
                        if kr >= 1:
                            eng.wait_ge(s_ke, kr)
                        for j in range(8):
                            ti = eng.transpose(REGb[0][0:64, j * 128:j * 128 + 128],
                                               KT4[:, r * 8 + j, h, :], P.ident_b[:])
                        ti.then_inc(s_kt, 1)
                if en == "scalar":
                    for r in range(2):
                        kr = hb * 2 + r
                        eng.wait_ge(s_kt, kr + 1)
                        eng.copy(KH[0:64, r * 1024:r * 1024 + 1024],
                                 REGb[0][0:64, 0:1024]).then_inc(s_ke, 1)
                for qt in range(16):
                    qq = hb * 16 + qt
                    p = qt % 2
                    t1 = ctr["tm"] + 1
                    t2 = ctr["tm"] + 2
                    ctr["tm"] += 2
                    s1 = ctr["se"] + 1
                    s2 = ctr["se"] + 2
                    ctr["se"] += 2
                    exp_done[qq] = s2
                    if en == "tensor":
                        if qq >= 2:
                            eng.wait_ge(s_se, se_done[qq - 2])
                            eng.wait_ge(s_ve, ve_done[qq - 2])
                        eng.wait_ge(s_ke, 2 * (hb + 1))
                        ti = eng.transpose(REGb[p][0:64, 512:640], QT4[:, qt, h, :],
                                           P.ident_b[:])
                        ti.then_inc(s_tm, 1)
                        eng.wait_ge(s_se, s1)
                        for kc in range(16):
                            ti = eng.matmul(REG[p][:, kc * 128:kc * 128 + 128],
                                            KH[0:64, kc * 128:kc * 128 + 128],
                                            QH[0:64, :], start=True, stop=True)
                        ti.then_inc(s_tm, 1)
                    if en == "scalar":
                        eng.wait_ge(s_tm, t1)
                        eng.copy(QH[0:64, :], REGb[p][0:64, 512:640]).then_inc(s_se, 1)
                        eng.wait_ge(s_tm, t2)
                        eng.activation(PSBT[p][:], REG[p], AF.Exp, scale=0.125
                                       ).then_inc(s_se, 1)
                    if qq % 256 != 0:
                        emit_backlog(qq - 1)
            emit_backlog(b * 256 + 255)
            if en == "sync":
                eng.wait_ge(s_qm, 96 * (b + 1))
                eng.dma_start(out=WOU, in_=t["WOUTbf"].rearrange("(dc p) f -> p dc f",
                                                                 p=128)
                              ).then_inc(s_wol, 16)
            if en == "tensor":
                eng.wait_ge(s_wol, 16 * (b + 1))
                eng.wait_ge(s_ve, ve_done[(b + 1) * 256 - 1])
                eng.wait_ge(s_se, se_done[(b + 1) * 256 - 1])
                for oj in range(32):
                    ec, tc2 = oj // 4, oj % 4
                    oi = b * 32 + oj
                    ps = P.P4[:, (oj % 4) * 512:(oj % 4) * 512 + 512]
                    if oi >= 4:
                        eng.wait_ge(r_att.rd, oi - 3)
                    for dc in range(8):
                        ti = eng.matmul(ps, WOU[:, dc, ec * 128:ec * 128 + 128],
                                        CF3[:, dc, tc2 * 512:tc2 * 512 + 512],
                                        start=(dc == 0), stop=(dc == 7))
                    ti.then_inc(s_om, 1)
            if en == "vector":
                for oj in range(32):
                    ec, tc2 = oj // 4, oj % 4
                    oi = b * 32 + oj
                    eng.wait_ge(s_om, oi + 1)
                    r_att.produce_guard(eng, oi)
                    ps = P.P4[:, (oj % 4) * 512:(oj % 4) * 512 + 512]
                    r_att.filled_on(
                        eng.tensor_single_scalar(ATS[oi % 2], ps, P.bout_c[:, ec:ec + 1],
                                                 ALU.add), oi)
            if en == "gpsimd":
                for oj in range(32):
                    ec, tc2 = oj // 4, oj % 4
                    oi = b * 32 + oj
                    r_att.store_wait(eng, oi)
                    eng.dma_start(out=t[att][b * D + ec * 128:b * D + ec * 128 + 128,
                                             tc2 * 512:tc2 * 512 + 512],
                                  in_=ATS[oi % 2]).then_inc(r_att.st_sem(oi), 16)
        if en == "gpsimd":
            r_att.wait_all(eng, 128)
    return emit



def ph_ffn1(P, xnbf="XNBF", gates="GATES1", x1="X1", att="ATT1", out="out"):
    t = P.t
    if getattr(P, "recycle_pool", None):
        pool = list(P.recycle_pool)
        def S(_name):
            return pool.pop(0)
    else:
        def S(_name):
            return (P.sem(_name), 0)
    h_gt, b_gt = S("f1_gt")
    h_ge, b_ge = S("f1_ge")
    h_hm, b_hm = S("f1_hm")
    h_gm, b_gm = S("f1_gm")
    h_hr, b_hr = S("f1_hr")
    h_hg, b_hg = S("f1_hg")
    h_bm, b_bm = S("f1_bm")
    h_ev, b_ev = S("f1_ev")
    h_sl, b_sl = S("f1_slc")
    h_xn, b_xn = P.sem("f1_xn"), 0
    h_xg, b_xg = P.sem("f1_xg"), 0
    h_xc, b_xc = P.sem("f1_xc"), 0
    h_ost, b_ost = P.sem("f1_ost"), 0
    kld = [(P.sem("f1_k0"), 0), (P.sem("f1_k1"), 0)]
    vld = [(P.sem("f1_v0"), 0), (P.sem("f1_v1"), 0)]

    def emit(en, eng):
        HGT = P.A1[:]
        XNC = P.a_bf(P.A3, 0, 4096).rearrange("p (dc u) -> p dc u", u=512)
        KS = [P.a_bf(P.A3, 4096 + i * 1024, 1024).rearrange("p (dc e) -> p dc e", e=128)
              for i in range(2)]
        VS = [P.a_bf(P.A3, 6144 + i * 1024, 1024) for i in range(2)]
        SEL = P.a_bf(P.A3, 8192, 8192)
        GTT = P.a_bf(P.A3, 16384, 512)
        HR = P.a_bf(P.A3, 16896, 512)
        SELS = P.a_f32(P.A2, 12544, 2048)
        GC = P.a_f32(P.A2, 12288, 256)
        X1C = P.a_f32(P.A2, 0, 4096)
        ATC = P.a_f32(P.A2, 4096, 4096)
        OCH = P.a_f32(P.A2, 8192, 4096)
        X1C3 = v3(X1C, 512)
        ATC3 = v3(ATC, 512)
        OCH3 = v3(OCH, 512)
        GC3 = GC.rearrange("p (q e) -> p q e", e=64)

        for qtr in range(4):
            if en == "sync":
                if qtr > 0:
                    eng.wait_ge(h_sl, b_sl + qtr)
                eng.dma_start(out=SELS[0:64, :],
                              in_=t["sel64"][:, qtr * 2048:qtr * 2048 + 2048]
                              ).then_inc(kld[0][0], 16)
            if en == "vector":
                eng.wait_ge(kld[0][0], kld[0][1] + 16 * (qtr + 1))
                eng.tensor_copy(SEL[0:64, qtr * 2048:qtr * 2048 + 2048], SELS[0:64, :]
                                ).then_inc(h_sl, 1)
        KB0 = kld[0][1] + 64

        for ci in range(16):
            b, tc = ci // 4, ci % 4
            trow = b * T + tc * 512
            if en == "sync":
                if ci > 0:
                    eng.wait_ge(h_ev, b_ev + ci)
                eng.dma_start(out=XNC, in_=fmaj(t[xnbf], b, tc * 512, 512)
                              ).then_inc(h_xn, 16)
                eng.dma_start(out=GC3, in_=t[gates][trow:trow + 512, :]
                              .rearrange("(q p) e -> p q e", p=128)).then_inc(h_xg, 16)
                eng.dma_start(out=X1C3, in_=fmaj(t[x1], b, tc * 512, 512)).then_inc(h_xc, 16)
                eng.dma_start(out=ATC3, in_=fmaj(t[att], b, tc * 512, 512)).then_inc(h_xc, 16)
                for k in range(64):
                    kk = ci * 64 + k
                    if kk >= 2:
                        eng.wait_ge(h_hm, b_hm + kk - 1)
                    hs, bs = kld[kk % 2]
                    eng.dma_start(out=KS[kk % 2], in_=t["KEYSbf"][k]
                                  .rearrange("(dc p) e -> p dc e", p=128)
                                  ).then_inc(hs, 16)
                for k in range(64):
                    kk = ci * 64 + k
                    if kk >= 2:
                        eng.wait_ge(h_bm, b_bm + kk - 1)
                    hs, bs = vld[kk % 2]
                    eng.dma_start(out=VS[kk % 2], in_=t["VALbf"][k]).then_inc(hs, 16)
            if en == "tensor":
                if ci > 0:
                    eng.wait_ge(h_ev, b_ev + ci)
                eng.wait_ge(h_xg, b_xg + 16 * (ci + 1))
                for q4 in range(4):
                    ti = eng.transpose(P.P4b[:, 1536 + q4 * 128:1536 + q4 * 128 + 128]
                                       [0:64, 0:128].bitcast(F32) if False else
                                       P.P4b[0:64, 1536 + q4 * 128:1536 + q4 * 128 + 128],
                                       GC3[:, q4, :], P.ident_f[:])
                ti.then_inc(h_gt, 1)
                eng.wait_ge(h_xn, b_xn + 16 * (ci + 1))
                eng.wait_ge(h_ge, b_ge + ci + 1)
                for k in range(64):
                    kk = ci * 64 + k
                    hs, bs = kld[kk % 2]
                    eng.wait_ge(hs, (KB0 if kk % 2 == 0 else bs) + 16 * (kk // 2 + 1))
                    hb = P.P4[:, (k % 2) * 512:(k % 2) * 512 + 512]
                    if kk >= 2:
                        eng.wait_ge(h_hr, b_hr + kk - 1)
                    for dc in range(8):
                        ti = eng.matmul(hb, KS[kk % 2][:, dc, :], XNC[:, dc, :],
                                        start=(dc == 0), stop=(dc == 7))
                    ti.then_inc(h_hm, 1)
                    gb = P.P4[:, 1024 + (k % 2) * 512:1024 + (k % 2) * 512 + 512]
                    if kk >= 2:
                        eng.wait_ge(h_hg, b_hg + kk - 1)
                    ti = eng.matmul(gb, SEL[0:64, k * 128:k * 128 + 128],
                                    GTT[0:64, :], start=True, stop=True)
                    ti.then_inc(h_gm, 1)
                eng.wait_ge(h_hg, b_hg + 64 * (ci + 1))
                for k in range(64):
                    kk = ci * 64 + k
                    hs, bs = vld[kk % 2]
                    eng.wait_ge(hs, bs + 16 * (kk // 2 + 1))
                    for vc in range(8):
                        ti = eng.matmul(bankv(P, vc), VS[kk % 2][:, vc * 128:vc * 128 + 128],
                                        HGT[:, k * 512:k * 512 + 512],
                                        start=(k == 0), stop=(k == 63))
                    ti.then_inc(h_bm, 1)
            if en == "scalar":
                eng.wait_ge(h_gt, b_gt + ci + 1)
                eng.copy(GTT[0:64, :], P.P4b[0:64, 1536:2048]).then_inc(h_ge, 1)
                for k in range(64):
                    kk = ci * 64 + k
                    eng.wait_ge(h_hm, b_hm + kk + 1)
                    hb = P.P4[:, (k % 2) * 512:(k % 2) * 512 + 512]
                    if kk >= 1:
                        eng.wait_ge(h_hg, b_hg + kk)
                    eng.activation(HR[:], hb, AF.Relu).then_inc(h_hr, 1)
            if en == "vector":
                for k in range(64):
                    kk = ci * 64 + k
                    eng.wait_ge(h_hr, b_hr + kk + 1)
                    eng.wait_ge(h_gm, b_gm + kk + 1)
                    gb = P.P4[:, 1024 + (k % 2) * 512:1024 + (k % 2) * 512 + 512]
                    eng.tensor_tensor(HGT[:, k * 512:k * 512 + 512], HR[:], gb,
                                      ALU.mult).then_inc(h_hg, 1)
                eng.wait_ge(h_bm, b_bm + 64 * (ci + 1))
                eng.wait_ge(h_xc, b_xc + 32 * (ci + 1))
                if ci > 0:
                    eng.wait_ge(h_ost, b_ost + 16 * ci)
                for dc in range(8):
                    eng.tensor_tensor(OCH3[:, dc, :], X1C3[:, dc, :], ATC3[:, dc, :],
                                      ALU.add)
                    ti = eng.tensor_tensor(OCH3[:, dc, :], OCH3[:, dc, :],
                                           bankv(P, dc), ALU.add)
                ti.then_inc(h_ev, 1)
            if en == "gpsimd":
                eng.wait_ge(h_ev, b_ev + ci + 1)
                eng.dma_start(out=fmaj(t[out], b, tc * 512, 512), in_=OCH3
                              ).then_inc(h_ost, 16)
        if en == "gpsimd":
            eng.wait_ge(h_ost, b_ost + 16 * 16)
    return emit



PARTS = {
    "l0a": {
        "ins": ["xin", "wm_l", "vcols", "wproj_l", "oscale"],
        "outs": ["XN32", "XNBF", "ATT"],
        "scratch": [],
        "phases": lambda P: [ph_init(P), ph_l0a(P)],
    },
    "ffn0": {
        "ins": ["XN32", "XNBF", "router", "shup", "shdn", "rup", "rdn"],
        "outs": ["ACC", "GATES0"],
        "scratch": ["UPbf", "DOWNbf"],
        "phases": lambda P: [ph_init(P), ph_precast(P, PRECAST_FFN0, "f0"),
                             ph_router0(P), ph_ffn0(P)],
    },
    "cmb0": {
        "ins": ["xin", "XN32", "ATT", "ACC"],
        "outs": ["X1", "XN132", "XN1BF"],
        "scratch": [],
        "phases": lambda P: [ph_init(P), ph_cmb0(P, xnout="XN132", xnbfout="XN1BF")],
    },
    "attn1": {
        "ins": ["XN1BF", "wqkv_l", "wout_l", "bout", "ropec", "ropes"],
        "outs": ["ATT1"],
        "scratch": ["WQKVbf", "WOUTbf"],
        "phases": lambda P: [ph_init(P), ph_precast(P, PRECAST_ATTN1, "a1"),
                             ph_attn1(P, xnbf="XN1BF")],
    },
    "ffn1": {
        "ins": ["XN132", "XN1BF", "X1", "ATT1", "esel_l", "keysw", "valsw", "sel64"],
        "outs": ["out", "GATES1"],
        "scratch": ["KEYSbf", "VALbf"],
        "phases": lambda P: [ph_init(P), ph_precast(P, PRECAST_FFN1, "f1"),
                             ph_rt1(P, xn32="XN132", gates="GATES1"),
                             ph_ffn1(P, xnbf="XN1BF", gates="GATES1")],
    },
    "all": {
        "ins": ["xin", "wm_l", "vcols", "wproj_l", "oscale", "shup", "shdn", "rup",
                "rdn", "router", "wqkv_l", "wout_l", "bout", "esel_l", "keysw",
                "valsw", "ropec", "ropes", "sel64"],
        "outs": ["out"],
        "scratch": ["XN32", "XNBF", "ATT", "ACC", "X1", "GATES0", "GATES1",
                    "UPbf", "DOWNbf", "WQKVbf", "WOUTbf", "KEYSbf", "VALbf"],
        "phases": lambda P: [
            ph_init(P),
            ph_precast(P, PRECAST_FFN0 + PRECAST_ATTN1 + PRECAST_FFN1, "all"),
            ph_l0a(P),
            ph_router0(P),
            ph_ffn0(P),
            ph_cmb0(P, xnout="XN32", xnbfout="XNBF"),
            ph_rt1(P, xn32="XN32", gates="GATES1"),
            ph_attn1(P, xnbf="XNBF", att="ATT"),
            ph_ffn1(P, xnbf="XNBF", gates="GATES1", att="ATT"),
        ],
    },
}


def build_part(part):
    spec = PARTS[part]
    P = Prog(spec["ins"], spec["outs"], spec["scratch"])
    for ph in spec["phases"](P):
        P.add(ph)
    return P.finalize(), spec


def run_part(part, in_map_per_core, core_ids=None):
    nc, spec = build_part(part)
    core_ids = core_ids or list(range(len(in_map_per_core)))
    res = run_bass_kernel_spmd(nc, in_map_per_core, core_ids=core_ids)
    return res


def kernel(**inputs):
    w = prep_shared(inputs)
    x = inputs["L0_attn_x"]
    nc, spec = build_part("all")
    in_maps = []
    for c in range(NCORES):
        m = {k: w[k] for k in spec["ins"] if k != "xin"}
        m["xin"] = prep_core_x(x, c)
        in_maps.append(m)
    res = run_bass_kernel_spmd(nc, in_maps, core_ids=list(range(NCORES)))
    out = np.empty((32, T, D), np.float32)
    for c in range(NCORES):
        out[c * BPC:(c + 1) * BPC] = np.asarray(res.results[c]["out"], np.float32) \
            .reshape(BPC, D, T).transpose(0, 2, 1)
    return out
